# version 22
# baseline (speedup 1.0000x reference)
import contextlib
import numpy as np
import ml_dtypes

import concourse.bass as bass
import concourse.mybir as mybir
from concourse.bass_utils import run_bass_kernel_spmd
from concourse.bass_core_rust import IndirectOffsetOnAxis

F32 = mybir.dt.float32
BF16 = mybir.dt.bfloat16
AF = mybir.ActivationFunctionType
ALU = mybir.AluOpType
AX = mybir.AxisListType

N_CORES = 8
LN_EPS = 1e-5
MODE = "fused"
SPARSE_MOE = True
DBG_MOE_NEXP = None


def cfg_full():
    return dict(D=1024, T=2048, H=16, HD=64, R=32, FF=4096, NE=8)


def tiles(total, w):
    o = 0
    while o < total:
        yield o, min(w, total - o)
        o += w


def moe_cap(T):
    return max(128, (3 * T // 8) // 128 * 128)



class Bld:
    NLANE = 8

    def __init__(self, nc, stack):
        self.nc = nc
        self.ops = {e: [] for e in ("T", "V", "S", "G", "Q")}
        self.sem_names = ["sT", "sV", "sS", "sG"]
        for q in ("Q", "G"):
            self.sem_names += [f"q{q}{i}" for i in range(self.NLANE)]
        self.sems = {n: stack.enter_context(nc.semaphore(n)) for n in self.sem_names}
        self.cnt = {n: 0 for n in self.sem_names}
        self.rr = {"Q": 0, "G": 0}
        self.lastw = {}
        self.eng_sem = {"T": "sT", "V": "sV", "S": "sS", "G": "sG"}
        self.psum = stack.enter_context(nc.psum_tensor("psA", [128, 4096], F32))
        self.bank_w = [None] * 8
        self.bank_r = [[] for _ in range(8)]
        self.t_dirty = False

    def bank(self, bk, lo=0, hi=512, np_=128, p0=0):
        return self.psum[p0:p0 + np_, bk * 512 + lo: bk * 512 + hi]

    def bankb(self, bk, lo=0, hi=128, np_=128, p0=0):
        return self.psum.bitcast(BF16)[p0:p0 + np_, bk * 1024 + lo: bk * 1024 + hi]

    def _wait_list(self, ek, waits):
        out = []
        for t in waits:
            if t is None:
                continue
            s, v = t
            if v <= 0:
                continue
            key = (ek, s)
            if self.lastw.get(key, 0) < v:
                self.lastw[key] = v
                out.append((self.sems[s], v))
        return out

    def op(self, ek, fn, waits=(), inc=False):
        waits = list(waits)
        sem_name = self.eng_sem.get(ek)
        if ek in ("V", "S", "G"):
            waits.append((sem_name, self.cnt[sem_name]))
            inc = True
        wl = self._wait_list(ek, waits)
        tok = None
        if inc:
            self.cnt[sem_name] += 1
            tok = (sem_name, self.cnt[sem_name])
            if ek == "T":
                self.t_dirty = False
        elif ek == "T":
            self.t_dirty = True
        sems = self.sems

        def cl(eng, fn=fn, wl=wl, inc=inc, sem_name=sem_name):
            for (h, v) in wl:
                eng.wait_ge(h, v)
            ins = fn(eng)
            if inc:
                ins.then_inc(sems[sem_name], 1)
        self.ops[ek].append(cl)
        return tok

    def dma(self, q, out, in_, waits=()):
        lane = self.rr[q] % self.NLANE
        self.rr[q] += 1
        sem_name = f"q{q}{lane}"
        wl = self._wait_list(q, list(waits) + [(sem_name, self.cnt[sem_name])])
        self.cnt[sem_name] += 16
        tok = (sem_name, self.cnt[sem_name])
        sems = self.sems

        def cl(eng, out=out, in_=in_, wl=wl, sem_name=sem_name):
            for (h, v) in wl:
                eng.wait_ge(h, v)
            eng.dma_start(out=out, in_=in_).then_inc(sems[sem_name], 16)
        self.ops[q].append(cl)
        return tok

    def idma(self, fn, waits=()):
        lane = self.rr["G"] % self.NLANE
        self.rr["G"] += 1
        sem_name = f"qG{lane}"
        wl = self._wait_list("G", list(waits) + [(sem_name, self.cnt[sem_name])])
        self.cnt[sem_name] += 16
        tok = (sem_name, self.cnt[sem_name])
        sems = self.sems

        def cl(eng, fn=fn, wl=wl, sem_name=sem_name):
            for (h, v) in wl:
                eng.wait_ge(h, v)
            fn(eng).then_inc(sems[sem_name], 16)
        self.ops["G"].append(cl)
        return tok

    def barrier(self, engs="TVSGQ"):
        assert not self.t_dirty, "barrier with un-inc'd T ops"
        snap = [(s, self.cnt[s]) for s in self.sem_names]
        for ek in engs:
            own = self.eng_sem.get(ek)
            wl = self._wait_list(ek, [(s, v) for (s, v) in snap if s != own])
            if not wl:
                continue

            def cl(eng, wl=wl):
                for (h, v) in wl:
                    eng.wait_ge(h, v)
            self.ops[ek].append(cl)

    def mm_group(self, bk, out_ap, pairs, waits=()):
        w = list(waits) + self.bank_r[bk] + [self.bank_w[bk]]
        n = len(pairs)
        tok = None
        for i, (l, r) in enumerate(pairs):
            tok = self.op(
                "T",
                (lambda e, l=l, r=r, st=(i == 0), sp=(i == n - 1), o=out_ap:
                 e.matmul(o, l, r, start=st, stop=sp)),
                waits=w if i == 0 else (), inc=(i == n - 1))
        self.bank_w[bk] = tok
        self.bank_r[bk] = []
        return tok

    def transpose_to(self, bk, out_ap, in_ap, ident, waits=()):
        w = list(waits) + self.bank_r[bk] + [self.bank_w[bk]]
        tok = self.op("T", (lambda e, o=out_ap, i=in_ap, id_=ident:
                            e.transpose(o, i, id_)), waits=w, inc=True)
        self.bank_w[bk] = tok
        self.bank_r[bk] = []
        return tok

    def evac(self, ek, bks, fn, waits=()):
        if isinstance(bks, int):
            bks = [bks]
        w = list(waits) + [self.bank_w[bk] for bk in bks]
        tok = self.op(ek, fn, waits=w, inc=True)
        for bk in bks:
            self.bank_r[bk].append(tok)
        return tok

    def finish(self, out_tokens=()):
        snap = [(s, self.cnt[s]) for s in self.sem_names]
        wl = self._wait_list("Q", list(out_tokens) + snap)

        def cl(eng, wl=wl):
            for (h, v) in wl:
                eng.wait_ge(h, v)
        self.ops["Q"].append(cl)
        with self.nc.Block() as block:
            for ek, reg in (("T", block.tensor), ("V", block.vector),
                            ("S", block.scalar), ("G", block.gpsimd),
                            ("Q", block.sync)):
                ops = self.ops[ek]

                def runner(eng, ops=ops):
                    for fn in ops:
                        fn(eng)
                reg(runner)



class Env:
    def __init__(self, b: Bld, stack, cfg):
        nc = b.nc
        self.cfg = cfg
        D, T, H, FF, NE = cfg["D"], cfg["T"], cfg["H"], cfg["FF"], cfg["NE"]
        DC = D // 128
        DCT = DC * T
        TH = T // 2
        KC = T // 128
        VON = KC * H * 65
        WELS = DC * D
        FHW = min(2048, FF)
        FC = FF // 128
        r = max(DCT, VON, 5 * T + 2048, 2 * WELS + 256, min(16, FC) * TH,
                8 * FHW, 4 * TH + 2 * FHW + FC * 128, 8 * 2 * D, 16384)
        self.RSZ = r
        self.R0, self.R1, self.R2, self.R3 = 0, r, 2 * r, 2 * r + DCT
        self.R4 = 3 * r + DCT
        self.R4_els = max(4 * T, 4 * D, KC * 512, 2 * D + 3 * 1024 + 128, 2048)
        self.a1_els = self.R4 + self.R4_els
        self.a2_els = max(DCT, 2 * WELS,
                          8 * TH + 2 * DC * max(FF // 8, 128) + 4096,
                          8 * TH + 4 * D, 8192)
        self.A1 = stack.enter_context(nc.sbuf_tensor("A1", [128, self.a1_els], BF16))
        self.A2 = stack.enter_context(nc.sbuf_tensor("A2", [128, self.a2_els], BF16))
        sb = lambda name, shape, dt: stack.enter_context(nc.sbuf_tensor(name, shape, dt))
        self.ident = sb("c_ident", [128, 128], F32)
        self.identb = sb("c_identb", [128, 128], BF16)
        self.ones_row = sb("c_ones_row", [1, 1024], BF16)
        self.ones_row_f = sb("c_ones_row_f", [1, 128], F32)
        self.ones_col_bf = sb("c_ones_col_bf", [128, 1], BF16)
        self.ones_col_f = sb("c_ones_col_f", [128, 1], F32)
        self.stair = sb("c_stair", [128, 896], BF16)
        self.bias = sb("c_bias", [1, 2048], BF16)
        self.rows = sb("c_rows", [1, 2112], F32)
        self.rows_bf = sb("c_rows_bf", [1, 1024], BF16)
        self.invbc = sb("c_invbc", [64, 512], BF16)
        self.smalls = sb("c_smalls", [128, 64], F32)
        self.colbf = sb("c_colbf", [128, 16], BF16)
        self.rwt = sb("c_rwt", [128, DC * NE], BF16)
        self.ustrict = sb("c_ustrict", [128, 128], F32)
        self.iota = sb("c_iota", [128, T // 128], mybir.dt.int32)
        self.ifill = sb("c_ifill", [128, 8], mybir.dt.int32)
        self.rwtf = sb("c_rwtf", [128, DC * NE], F32)
        self.cw = sb("c_cw", [128, KC * NE], F32)
        self.locs = sb("c_locs", [128, 64], mybir.dt.int32)
        b.op("G", lambda e: e.memset(self.ident[:, :], 0.0))
        b.op("G", lambda e: e.affine_select(
            out=self.ident[:, :], in_=self.ident[:, :],
            compare_op=ALU.not_equal, fill=1.0,
            base=0, pattern=[[-1, 128]], channel_multiplier=1))
        b.op("G", lambda e: e.memset(self.identb[:, :], 0.0))
        b.op("G", lambda e: e.affine_select(
            out=self.identb[:, :], in_=self.identb[:, :],
            compare_op=ALU.not_equal, fill=1.0,
            base=0, pattern=[[-1, 128]], channel_multiplier=1))
        b.op("G", lambda e: e.memset(self.ones_row[:, :], 1.0))
        b.op("G", lambda e: e.memset(self.ones_row_f[:, :], 1.0))
        b.op("G", lambda e: e.memset(self.ones_col_bf[:, :], 1.0))
        b.op("G", lambda e: e.memset(self.ones_col_f[:, :], 1.0))
        b.op("G", lambda e: e.memset(self.stair[:, :], 1.0))
        b.op("G", lambda e: e.affine_select(
            out=self.stair[:, :], in_=self.stair[:, :],
            compare_op=ALU.is_ge, fill=0.0,
            base=-384, pattern=[[1, 896]], channel_multiplier=-1))
        b.op("G", lambda e: e.memset(self.ustrict[:, :], 1.0))
        b.op("G", lambda e: e.affine_select(
            out=self.ustrict[:, :], in_=self.ustrict[:, :],
            compare_op=ALU.is_ge, fill=0.0,
            base=-1, pattern=[[1, 128]], channel_multiplier=-1))
        b.op("G", lambda e: e.memset(self.ifill[:, :], T))
        b.op("G", lambda e: e.iota(self.iota[:, :], pattern=[[128, T // 128]],
                                   base=0, channel_multiplier=1))

    def a1(self, lo, n):
        assert 0 <= lo and lo + n <= self.a1_els, (lo, n, self.a1_els)
        return self.A1[:, lo:lo + n]

    def a1f(self, lo, n_f32):
        return self.a1(lo, 2 * n_f32).bitcast(F32)

    def a2(self, lo, n):
        assert 0 <= lo and lo + n <= self.a2_els, (lo, n, self.a2_els)
        return self.A2[:, lo:lo + n]

    def a2f(self, lo, n_f32):
        return self.a2(lo, 2 * n_f32).bitcast(F32)



def layernorm(b: Bld, ev: Env, x_dram, xlnf_dram=None, xtm_dram=None):
    cfg = ev.cfg
    D, T = cfg["D"], cfg["T"]
    DC = D // 128
    DCT = DC * T
    xbf = ev.a1(ev.R0, DCT)
    xf = ev.a1f(ev.R1, 2 * T)
    sqbf = ev.a1(ev.R1 + 4 * T, T)
    bc = ev.a1(ev.R1 + 5 * T, 1024)
    xln = ev.a1(ev.R2, DCT)
    tile_list = list(tiles(T, 512))
    nt = len(tile_list)
    assert nt <= 4
    inv_d = 1.0 / D

    pre = []
    for bk in range(8):
        pre += b.bank_r[bk] + [b.bank_w[bk]]
    buf_free = [None, None]
    last_mm = None
    for dc in range(DC):
        buf = dc % 2
        ld = b.dma("Q", xf[:, buf * T:(buf + 1) * T],
                   x_dram[dc * 128:(dc + 1) * 128, :], waits=[buf_free[buf]])
        cast = b.op("V", (lambda e, dc=dc, buf=buf: e.tensor_copy(
            xbf[:, dc * T:(dc + 1) * T], xf[:, buf * T:(buf + 1) * T])), waits=[ld])
        sq = b.op("S", (lambda e, dc=dc: e.activation(
            sqbf[:, :], xbf[:, dc * T:(dc + 1) * T], AF.Square)),
            waits=[cast, last_mm])
        buf_free[buf] = cast
        mm_w = [cast, sq] + (pre if dc == 0 else [])
        for ti, (o, w) in enumerate(tile_list):
            pa = b.bank(ti, 0, w, np_=1)
            xa_ = xbf[:, dc * T + o: dc * T + o + w]
            b.op("T", (lambda e, pa=pa, xa_=xa_, st=(dc == 0), sp=(dc == DC - 1):
                       e.matmul(pa, ev.ones_col_bf[:, :], xa_, start=st, stop=sp)),
                 waits=mm_w if ti == 0 else ())
            pb = b.bank(4 + ti, 0, w, np_=1)
            sa = sqbf[:, o:o + w]
            last_mm = b.op("T", (lambda e, pb=pb, sa=sa, st=(dc == 0),
                                 sp=(dc == DC - 1):
                                 e.matmul(pb, ev.ones_col_bf[:, :], sa,
                                          start=st, stop=sp)),
                           inc=(ti == nt - 1))
    for bk in range(8):
        b.bank_w[bk] = last_mm
        b.bank_r[bk] = []

    rows, rows_bf = ev.rows, ev.rows_bf
    prev_asm_st = [[] for _ in range(4)]
    asmb_st = [[], []]
    slot_rd = [list(buf_free), list(buf_free)]
    slot_st = [list(buf_free), list(buf_free)]
    for ti, (o, w) in enumerate(tile_list):
        s0 = rows[0:1, 0:w]
        s1 = rows[0:1, 512:512 + w]
        s2 = rows[0:1, 1024:1024 + w]
        rbf = rows_bf[0:1, 0:w]
        mbf = rows_bf[0:1, 512:512 + w]
        b.evac("V", ti, (lambda e, ti=ti, w=w, s0=s0: e.tensor_scalar_mul(
            s0, b.bank(ti, 0, w, np_=1), inv_d)))
        b.evac("V", 4 + ti, (lambda e, ti=ti, w=w, s1=s1: e.tensor_scalar_mul(
            s1, b.bank(4 + ti, 0, w, np_=1), inv_d)))
        b.op("V", lambda e, s0=s0, s2=s2: e.tensor_mul(s2, s0, s0))
        b.op("V", lambda e, s1=s1, s2=s2: e.tensor_sub(s1, s1, s2))
        tve = b.op("V", lambda e, s1=s1: e.tensor_scalar_add(s1, s1, LN_EPS))
        tsq = b.op("S", (lambda e, s1=s1, s2=s2: e.activation(s2, s1, AF.Sqrt)),
                   waits=[tve])
        trec = b.op("V", (lambda e, s1=s1, s2=s2: e.reciprocal(s1, s2)),
                    waits=[tsq])
        bk_r, bk_m = ti, 4 + ti
        if xlnf_dram is None:
            b.op("V", lambda e, s0=s0, s1=s1, s2=s2: e.tensor_mul(s2, s0, s1))
            b.op("V", lambda e, s1=s1, rbf=rbf: e.tensor_copy(rbf, s1))
            cv = b.op("V", lambda e, s2=s2, mbf=mbf: e.tensor_copy(mbf, s2))
            b.mm_group(bk_r, b.bank(bk_r, 0, w),
                       [(ev.ones_row[0:1, 0:128], rbf)], waits=[cv])
            b.mm_group(bk_m, b.bank(bk_m, 0, w),
                       [(ev.ones_row[0:1, 0:128], mbf)], waits=[cv])
            b.evac("V", bk_r, (lambda e, w=w, bk_r=bk_r: e.tensor_copy(
                bc[:, 0:w], b.bank(bk_r, 0, w))))
            b.evac("V", bk_m, (lambda e, w=w, bk_m=bk_m: e.tensor_copy(
                bc[:, 512:512 + w], b.bank(bk_m, 0, w))))
            for dc in range(DC):
                xo = xln[:, dc * T + o: dc * T + o + w]
                xi = xbf[:, dc * T + o: dc * T + o + w]
                b.op("V", (lambda e, xo=xo, xi=xi, w=w: e.tensor_mul(
                    xo, xi, bc[:, 0:w])))
                b.op("V", (lambda e, xo=xo, w=w: e.tensor_sub(
                    xo, xo, bc[:, 512:512 + w])))
        else:
            b.mm_group(bk_r, b.bank(bk_r, 0, w),
                       [(ev.ones_row_f[0:1, 0:128], s1)], waits=[trec])
            b.mm_group(bk_m, b.bank(bk_m, 0, w),
                       [(ev.ones_row_f[0:1, 0:128], s0)], waits=[trec])
            xft = [ev.a1f(ev.R1, 512), ev.a1f(ev.R1 + 1024, 512)]
            xlt = [ev.a1f(ev.R1 + 2048, 512), ev.a1f(ev.R1 + 3072, 512)]
            asm = ev.a1f(ev.R1 + 4096, 4096)
            bcf = ev.a1f(ev.R1 + 12288, 1024)
            b.evac("V", bk_r, (lambda e, w=w, bk_r=bk_r: e.tensor_copy(
                bcf[:, 0:w], b.bank(bk_r, 0, w))))
            cvb = b.evac("V", bk_m, (lambda e, w=w, bk_m=bk_m: e.tensor_copy(
                bcf[:, 512:512 + w], b.bank(bk_m, 0, w))))
            slot_rd[0] = slot_rd[0] + [cvb]
            slot_rd[1] = slot_rd[1] + [cvb]
            asm_toks = [[] for _ in range(4)]
            for dc in range(DC):
                par = dc % 2
                ldx = b.dma("Q", xft[par][:, 0:w],
                            x_dram[dc * 128:(dc + 1) * 128, o:o + w],
                            waits=slot_rd[par])
                b.op("V", (lambda e, par=par, w=w:
                           e.tensor_sub(xlt[par][:, 0:w], xft[par][:, 0:w],
                                        bcf[:, 512:512 + w])),
                     waits=[ldx] + slot_st[par])
                t2 = b.op("V", (lambda e, par=par, w=w:
                                e.tensor_mul(xlt[par][:, 0:w], xlt[par][:, 0:w],
                                             bcf[:, 0:w])))
                slot_rd[par] = [t2]
                xo = xln[:, dc * T + o: dc * T + o + w]
                b.op("V", (lambda e, xo=xo, par=par, w=w: e.tensor_copy(
                    xo, xlt[par][:, 0:w])))
                slot_st[par] = [b.dma(
                    "Q", xlnf_dram[dc * 128:(dc + 1) * 128, o:o + w],
                    xlt[par][:, 0:w], waits=[t2])]
                if xtm_dram is not None:
                    for j in range(w // 128):
                        bt = bk_r if j % 2 == 0 else bk_m
                        src = xlt[par][:, j * 128: j * 128 + 128]
                        b.transpose_to(bt, b.bank(bt, 0, 128), src,
                                       ev.ident[:, :],
                                       waits=[t2] + prev_asm_st[j])
                        dsta = asm[:, j * 1024 + dc * 128:
                                   j * 1024 + dc * 128 + 128]
                        asm_toks[j].append(b.evac("V", bt, (
                            lambda e, dsta=dsta, bt=bt: e.tensor_copy(
                                dsta, b.bank(bt, 0, 128)))))
                        slot_rd[par] = slot_rd[par] + [asm_toks[j][-1]]
            if xtm_dram is not None:
                asmb = ev.a1(ev.R1 + 14336, 2048)
                for j in range(w // 128):
                    jp = j % 2
                    cvt = b.op("V", (lambda e, j=j, jp=jp: e.tensor_copy(
                        asmb[:, jp * 1024: jp * 1024 + D],
                        asm[:, j * 1024: j * 1024 + D])),
                        waits=asm_toks[j] + asmb_st[jp])
                    prev_asm_st[j] = [cvt]
                    asmb_st[jp] = [b.dma(
                        "Q", xtm_dram[o + j * 128: o + j * 128 + 128, :],
                        asmb[:, jp * 1024: jp * 1024 + D], waits=[cvt])]



def pre_transpose(b: Bld, ev: Env, x_tm_dram, x_fm_dram):
    cfg = ev.cfg
    D, T = cfg["D"], cfg["T"]
    DC = D // 128
    xin = ev.a1f(ev.R0, 2 * D)
    xtr = ev.a1f(ev.R1, 2 * DC * 128)
    buf_free = [None, None]
    out_free = [None, None]
    for tc in range(T // 128):
        buf = tc % 2
        ld = b.dma("Q", xin[:, buf * D:(buf + 1) * D],
                   x_tm_dram[tc * 128:(tc + 1) * 128, :], waits=[buf_free[buf]])
        cp_toks = []
        for dc in range(DC):
            bk = 4 + (dc % 2)
            src = xin[:, buf * D + dc * 128: buf * D + dc * 128 + 128]
            dst = xtr[:, buf * DC * 128 + dc * 128: buf * DC * 128 + dc * 128 + 128]
            b.transpose_to(bk, b.bank(bk, 0, 128), src, ev.ident[:, :], waits=[ld])
            cp = b.evac("V", bk, (lambda e, dst=dst, bk=bk: e.tensor_copy(
                dst, b.bank(bk, 0, 128))), waits=[out_free[buf]])
            cp_toks.append(cp)
        st = b.dma("Q",
                   x_fm_dram[:, tc * 128:(tc + 1) * 128].rearrange(
                       "(c p) n -> p c n", p=128),
                   xtr[:, buf * DC * 128:(buf + 1) * DC * 128].rearrange(
                       "p (c n) -> p c n", n=128),
                   waits=cp_toks)
        buf_free[buf] = cp_toks[-1]
        out_free[buf] = st



def proj_fm(b: Bld, ev: Env, xln, wsl, out, scale=1.0, bias_row=None, wait0=()):
    cfg = ev.cfg
    D, T = cfg["D"], cfg["T"]
    DC = D // 128
    rot = [0]
    last = []
    for ic in range(DC):
        for (o, w) in tiles(T, 512):
            bk = rot[0] % 4
            rot[0] += 1
            pairs = [(wsl(dc, ic), xln[:, dc * T + o: dc * T + o + w])
                     for dc in range(DC)]
            if bias_row is not None:
                pairs.append((bias_row[0:1, ic * 128: ic * 128 + 128],
                              ev.ones_row[0:1, 0:w]))
            b.mm_group(bk, b.bank(bk, 0, w), pairs, waits=wait0)
            dst = out[:, ic * T + o: ic * T + o + w]
            if scale != 1.0:
                last.append(b.evac("V", bk, (lambda e, dst=dst, bk=bk, w=w:
                                             e.tensor_scalar_mul(
                                                 dst, b.bank(bk, 0, w), scale))))
            else:
                last.append(b.evac("S", bk, (lambda e, dst=dst, bk=bk, w=w:
                                             e.activation(dst, b.bank(bk, 0, w),
                                                          AF.Copy))))
    return last


def proj_tm_vones(b: Bld, ev: Env, xln, wsl_r, v_ones, bias_row=None, wait0=()):
    cfg = ev.cfg
    D, T, H, HD = cfg["D"], cfg["T"], cfg["H"], cfg["HD"]
    assert HD == 64
    DC = D // 128
    rot = [0]
    last = []
    for tc in range(T // 128):
        for (io, iw) in tiles(D, 512):
            bk = rot[0] % 4
            rot[0] += 1
            pairs = [(xln[:, dc * T + tc * 128: dc * T + tc * 128 + 128],
                      wsl_r(dc, io, iw)) for dc in range(DC)]
            if bias_row is not None:
                pairs.append((ev.ones_row[0:1, 0:128],
                              bias_row[0:1, io: io + iw]))
            b.mm_group(bk, b.bank(bk, 0, iw), pairs, waits=wait0)
            nh = iw // 64
            base = tc * H * 65 + (io // 64) * 65
            dst = v_ones[:, base: base + nh * 65].rearrange(
                "p (h c) -> p h c", h=nh)[:, :, 0:64]
            last.append(b.evac("V", bk, (lambda e, dst=dst, bk=bk, iw=iw, nh=nh:
                                         e.tensor_copy(dst, b.bank(bk, 0, iw).rearrange(
                                             "p (h d) -> p h d", h=nh)))))
    return last



def attention_core(b: Bld, ev: Env, sc_lh, sc_rh, v_ones, y, causal, ptil2):
    cfg = ev.cfg
    T, H, HD = cfg["T"], cfg["H"], cfg["HD"]
    KC = T // 128
    ptils = [ev.a1(ev.R4, KC * 512), ptil2]
    inv_f = ev.rows[0:1, 1536:2048]
    inv_bf = ev.rows_bf[0:1, 0:512]
    items = [(h, go, gw) for h in range(H) for (go, gw) in tiles(T, 512)]
    exp_q = {}
    av_toks = {}

    def stage_a(i):
        h, go, gw = items[i]
        ptil = ptils[i % 2]
        nkc = (go + gw) // 128 if causal else KC
        war = [av_toks.get(i - 2)]
        toks = []
        for kc in range(nkc):
            bk = kc % 6
            b.mm_group(bk, b.bank(bk, 0, gw),
                       [(sc_lh(h, kc), sc_rh(h, go, gw))])
            pdst = ptil[:, kc * 512: kc * 512 + gw]
            tok = b.evac("S", bk, (lambda e, pdst=pdst, bk=bk, gw=gw:
                                   e.activation(pdst, b.bank(bk, 0, gw),
                                                AF.Exp)),
                         waits=war if kc == 0 else ())
            if causal and kc * 128 + 127 >= go:
                off = 384 + go - kc * 128
                assert 0 <= off and off + gw <= 896, (off, gw)
                msl = ev.stair[:, off:off + gw]
                tok = b.op("V", (lambda e, pdst=pdst, msl=msl:
                                 e.tensor_mul(pdst, pdst, msl)),
                           waits=[tok])
            toks.append(tok)
        exp_q[i] = toks

    def stage_b(i):
        h, go, gw = items[i]
        ptil = ptils[i % 2]
        nkc = (go + gw) // 128 if causal else KC
        pairs = []
        for kc in range(nkc):
            vsl = v_ones[:, (kc * H + h) * 65: (kc * H + h) * 65 + 65]
            pairs.append((vsl, ptil[:, kc * 512: kc * 512 + gw]))
        av_toks[i] = b.mm_group(6, b.bank(6, 0, gw, np_=65), pairs,
                                waits=exp_q.pop(i))
        b.evac("V", 6, (lambda e, gw=gw: e.reciprocal(
            inv_f[0:1, 0:gw], b.bank(6, 0, gw, np_=1, p0=64))))
        cv = b.op("V", lambda e, gw=gw: e.tensor_copy(inv_bf[0:1, 0:gw],
                                                      inv_f[0:1, 0:gw]))
        b.mm_group(7, b.bank(7, 0, gw, np_=64),
                   [(ev.ones_row[0:1, 0:64], inv_bf[0:1, 0:gw])], waits=[cv])
        b.evac("V", 7, (lambda e, gw=gw: e.tensor_copy(
            ev.invbc[0:64, 0:gw], b.bank(7, 0, gw, np_=64))))
        dc = (h * HD) // 128
        p0 = (h * HD) % 128
        ydst = y[p0:p0 + 64, dc * T + go: dc * T + go + gw]
        b.evac("V", 6, (lambda e, ydst=ydst, gw=gw: e.tensor_mul(
            ydst, b.bank(6, 0, gw, np_=64), ev.invbc[0:64, 0:gw])))

    stage_a(0)
    for i in range(1, len(items)):
        stage_a(i)
        stage_b(i - 1)
    stage_b(len(items) - 1)



def out_proj_residual(b: Bld, ev: Env, y, wsl, x_in, x_out, bias_row=None,
                      wait0=()):
    cfg = ev.cfg
    D, T = cfg["D"], cfg["T"]
    DC = D // 128
    xi = ev.a1f(ev.R4, T)
    xo = ev.a1f(ev.R4 + 2 * T, T)
    st_tok = [None]
    tt_last = [None]
    rot = [0]
    for dco in range(DC):
        ld = b.dma("Q", xi[:, :], x_in[dco * 128:(dco + 1) * 128, :],
                   waits=[tt_last[0]])
        tts = []
        for (o, w) in tiles(T, 512):
            bk = rot[0] % 4
            rot[0] += 1
            pairs = [(wsl(dc, dco), y[:, dc * T + o: dc * T + o + w])
                     for dc in range(DC)]
            if bias_row is not None:
                pairs.append((bias_row[0:1, dco * 128: dco * 128 + 128],
                              ev.ones_row[0:1, 0:w]))
            b.mm_group(bk, b.bank(bk, 0, w), pairs, waits=wait0)
            dst = xo[:, o:o + w]
            src1 = xi[:, o:o + w]
            tts.append(b.evac("V", bk, (lambda e, dst=dst, src1=src1, bk=bk, w=w:
                                        e.tensor_add(dst, b.bank(bk, 0, w), src1)),
                              waits=[ld, st_tok[0]]))
        tt_last[0] = tts[-1]
        st_tok[0] = b.dma("Q", x_out[dco * 128:(dco + 1) * 128, :], xo[:, :],
                          waits=tts)
    return st_tok[0]



def block_l0attn(b: Bld, ev: Env, x_in, x_out, W):
    cfg = ev.cfg
    D, T, H, HD = cfg["D"], cfg["T"], cfg["H"], cfg["HD"]
    DC = D // 128
    DCT = DC * T
    KC = T // 128
    WELS = DC * D
    wA = ev.a2(0, WELS)
    wB = ev.a2(WELS, WELS)
    xln = ev.a1(ev.R2, DCT)
    q = ev.a1(ev.R0, DCT)
    k = ev.a1(ev.R3, DCT)
    v_ones = ev.a1(ev.R1, KC * H * 65)
    y = ev.a1(ev.R2, DCT)

    def wload(dst, nm, waits=()):
        return b.dma("G", dst.rearrange("p (c n) -> p c n", c=DC),
                     W[nm].rearrange("(c p) n -> p c n", p=128), waits=waits)

    ldq = wload(wA, "Wq")
    ldk = wload(wB, "Wk")
    layernorm(b, ev, x_in)
    b.barrier()
    wsl = lambda Wt: (lambda dc, ic: Wt[:, dc * D + ic * 128: dc * D + ic * 128 + 128])
    q_toks = proj_fm(b, ev, xln, wsl(wA), q, scale=1.0 / float(np.sqrt(HD)),
                     wait0=[ldq])
    ldv = wload(wA, "Wv", waits=[q_toks[-1]])
    k_toks = proj_fm(b, ev, xln, wsl(wB), k, wait0=[ldk])
    ldo = wload(wB, "Wo", waits=[k_toks[-1]])
    von_ms = b.op("G", lambda e: e.memset(v_ones[:, :], 1.0))
    wslA_r = lambda dc, io, iw: wA[:, dc * D + io: dc * D + io + iw]
    proj_tm_vones(b, ev, xln, wslA_r, v_ones, wait0=[ldv, von_ms])
    b.barrier()

    hpc = 128 // HD
    sc_lh = lambda h, kc: k[(h % hpc) * HD:(h % hpc) * HD + HD,
                            (h // hpc) * T + kc * 128:(h // hpc) * T + kc * 128 + 128]
    sc_rh = lambda h, go, gw: q[(h % hpc) * HD:(h % hpc) * HD + HD,
                                (h // hpc) * T + go:(h // hpc) * T + go + gw]
    if 2 * WELS + KC * 512 <= ev.a2_els:
        ptil2 = ev.a2(2 * WELS, KC * 512)
    else:
        assert KC * 512 <= WELS
        ptil2 = ev.a2(0, KC * 512)
    attention_core(b, ev, sc_lh, sc_rh, v_ones, y, True, ptil2)
    b.barrier()
    wslo = lambda dc, dco: wB[:, dc * D + dco * 128: dc * D + dco * 128 + 128]
    st = out_proj_residual(b, ev, y, wslo, x_in, x_out, wait0=[ldo])
    if getattr(ev, "l0f_W", None) is not None:
        sc_stage = ev.a2f(ev.a2_els - 4096, 2048)
        t1 = tern_scale_pass(b, ev, ev.l0f_W["W1"], cfg["FF"], D, None,
                             thrcol=0, bank=6, eng="V", stage=sc_stage,
                             cw_=1024)
        tern_scale_pass(b, ev, ev.l0f_W["W2"], D, cfg["FF"], None,
                        thrcol=1, bank=7, war=t1, eng="V", stage=sc_stage,
                        cw_=1024)
    b.barrier()
    return st



def tern_scale_pass(b: Bld, ev: Env, Wdram, nrows, ncols, stage_off,
                    thrcol, bank=6, war=None, eng="V", stage=None, cw_=1024):
    cw_ = min(cw_, ncols)
    if stage is None:
        stage = ev.a1f(stage_off, 2 * cw_)
    asum = ev.smalls[:, 60:61]
    tmp = ev.smalls[:, 61:62]
    thr = ev.smalls[:, thrcol:thrcol + 1]
    zz = b.op(eng, lambda e: e.memset(asum, 0.0))
    buf_free = [war, war]
    ad = zz
    i = 0
    for g in range(nrows // 128):
        for (co, cn) in tiles(ncols, cw_):
            buf = i % 2
            i += 1
            ld = b.dma("Q", stage[:, buf * cw_: buf * cw_ + cn],
                       Wdram[g * 128:(g + 1) * 128, co:co + cn],
                       waits=[buf_free[buf]])
            rd = b.op(eng, (lambda e, buf=buf, cn=cn: e.reduce_sum(
                tmp, stage[:, buf * cw_: buf * cw_ + cn], axis=AX.X,
                apply_absolute_value=True)), waits=[ld, zz])
            ad = b.op(eng, lambda e: e.tensor_add(asum, asum, tmp))
            buf_free[buf] = rd
    b.mm_group(bank, b.bank(bank, 0, 1, np_=1),
               [(asum, ev.ones_col_f[:, :])], waits=[ad])
    srow = ev.rows[0:1, 2048:2049]
    b.evac(eng, bank, (lambda e: e.tensor_scalar(
        srow, b.bank(bank, 0, 1, np_=1), 1.0 / float(nrows * ncols), 1e-8,
        ALU.mult, ALU.max)))
    hrow = ev.rows[0:1, 2049:2050]
    cv = b.op(eng, lambda e: e.tensor_scalar_mul(hrow, srow, 0.5))
    b.mm_group(bank, b.bank(bank, 0, 1),
               [(ev.ones_row_f[0:1, 0:128], hrow)], waits=[cv])
    return b.evac(eng, bank, (lambda e: e.tensor_copy(thr, b.bank(bank, 0, 1))))


def ternarize(b: Bld, ev: Env, src_f32, thrcol, sgn, ab, tern_f, waits=()):
    t1 = b.op("S", (lambda e: e.activation(sgn, src_f32, AF.Sign)), waits=waits)
    t2 = b.op("S", (lambda e: e.activation(ab, src_f32, AF.Abs)))
    thr = ev.smalls[:, thrcol:thrcol + 1]
    b.op("V", (lambda e: e.tensor_scalar(ab, ab, thr, None, ALU.is_gt)),
         waits=[t2])
    return b.op("V", (lambda e: e.tensor_mul(tern_f, ab, sgn)), waits=[t1])



def block_l0ffn(b: Bld, ev: Env, x_in, x_out, W):
    cfg = ev.cfg
    D, T, FF = cfg["D"], cfg["T"], cfg["FF"]
    DC = D // 128
    DCT = DC * T
    TH = T // 2
    FC = FF // 128
    FHW = min(1024, FF)
    GR = DC * 128
    r = ev.RSZ
    xln = ev.a1(ev.R2, DCT)
    w1c = ev.w1c_dram
    w2c = ev.w2c_dram

    def h_sl(fc, o, w):
        if fc < 16:
            return ev.a1(ev.R1 + fc * TH + o, w)
        if fc < 24:
            return ev.a1(ev.R4 + (fc - 16) * TH + o, w)
        return ev.a2((fc - 24) * TH + o, w)

    if getattr(ev, "l0f_W", None) is None:
        t1 = tern_scale_pass(b, ev, W["W1"], FF, D, ev.R3, thrcol=0)
        tern_scale_pass(b, ev, W["W2"], D, FF, ev.R3, thrcol=1, bank=7,
                        war=t1)
    layernorm(b, ev, x_in)
    b.barrier()

    b1row = W["b1"].rearrange("(a d) -> a d", a=1)
    b2row = W["b2"].rearrange("(a d) -> a d", a=1)

    stage = ev.a1f(ev.R3, 2 * D)
    sgn = ev.a1f(ev.R3 + 4 * D, D)
    ab = ev.a1f(ev.R3 + 6 * D, D)
    ternf = ev.a1f(ev.R3 + 8 * D, D)
    gbuf = [ev.a1(ev.R0, GR), ev.a1(ev.R0 + GR, GR)]
    buf_free = [None, None]
    gb_free = [None, None]
    w1_cache = []
    prev_tk = [None]
    for fc in range(FC):
        buf = fc % 2
        ld = b.dma("Q", stage[:, buf * D:(buf + 1) * D],
                   W["W1"][fc * 128:(fc + 1) * 128, :], waits=[buf_free[buf]])
        tk = ternarize(b, ev, stage[:, buf * D:(buf + 1) * D], 0,
                       sgn, ab, ternf, waits=[ld, prev_tk[0]])
        prev_tk[0] = tk
        buf_free[buf] = tk
        cps = []
        for dc in range(DC):
            bk = 4 + (dc % 2)
            b.transpose_to(bk, b.bank(bk, 0, 128),
                           ternf[:, dc * 128: dc * 128 + 128],
                           ev.ident[:, :], waits=[tk])
            dst = gbuf[buf][:, dc * 128: dc * 128 + 128]
            cps.append(b.evac("V", bk, (lambda e, dst=dst, bk=bk:
                                        e.tensor_copy(dst, b.bank(bk, 0, 128))),
                              waits=[gb_free[buf]]))
        w1_cache.append(b.dma("Q", w1c[fc], gbuf[buf], waits=cps))
        gb_free[buf] = w1_cache[-1]

    ring = [ev.a1(ev.R3 + 10 * D + i * GR, GR) for i in range(3)]

    def h_steps(th, war):
        rot = [0]
        b1_half = [0]
        b1_tok = [b.dma("G", ev.bias[0:1, 0:min(FF, 2048)],
                        b1row[0:1, 0:min(FF, 2048)], waits=war)]
        slot_mm = [list(war), list(war), list(war)]
        h_toks = [None]
        for fc in range(FC):
            sl = fc % 3
            ld = b.dma("Q", ring[sl], w1c[fc],
                       waits=slot_mm[sl] + [w1_cache[fc]])
            if fc * 128 >= (b1_half[0] + 1) * 2048:
                b1_half[0] += 1
                b1_tok[0] = b.dma(
                    "G", ev.bias[0:1, 0:2048],
                    b1row[0:1, b1_half[0] * 2048:(b1_half[0] + 1) * 2048],
                    waits=[h_toks[-1]])
            gts = []
            for (o, w) in tiles(TH, 512):
                bk = rot[0] % 4
                rot[0] += 1
                pairs = [(ring[sl][:, dc * 128: dc * 128 + 128],
                          xln[:, dc * T + th * TH + o: dc * T + th * TH + o + w])
                         for dc in range(DC)]
                boff = fc * 128 - b1_half[0] * 2048
                pairs.append((ev.bias[0:1, boff:boff + 128],
                              ev.ones_row[0:1, 0:w]))
                gts.append(b.mm_group(bk, b.bank(bk, 0, w), pairs,
                                      waits=[ld, b1_tok[0]]))
                dst = h_sl(fc, o, w)
                h_toks.append(b.evac("S", bk, (lambda e, dst=dst, bk=bk, w=w:
                                               e.activation(dst, b.bank(bk, 0, w),
                                                            AF.Gelu))))
            slot_mm[sl] = gts
            yield h_toks[-1]

    w2stage = ev.a1f(ev.R0, 2 * FHW)
    ternf2 = ev.a1f(ev.R0 + 4 * FHW, FHW)
    sgn2 = ev.a1f(ev.R0 + 6 * FHW, FHW)
    ab2 = ev.a1f(ev.R3, FHW)
    strip = ev.a2(8 * TH, FC * 128)

    def w2_steps(war):
        buf2 = [list(war), list(war)]
        prev2 = [prev_tk[0]] + war
        strip_free = [None]
        cache = []
        for dcg in range(DC):
            w2_done = []
            for fh, fhw in tiles(FF, FHW):
                pbuf = (fh // FHW) % 2
                ld = b.dma("Q", w2stage[:, pbuf * FHW: pbuf * FHW + fhw],
                           W["W2"][dcg * 128:(dcg + 1) * 128, fh:fh + fhw],
                           waits=buf2[pbuf] + prev2)
                tk = ternarize(b, ev, w2stage[:, pbuf * FHW:pbuf * FHW + fhw],
                               1, sgn2[:, 0:fhw], ab2[:, 0:fhw],
                               ternf2[:, 0:fhw], waits=[ld] + prev2)
                prev2 = [tk]
                buf2[pbuf] = [tk]
                for j in range(fhw // 128):
                    fc = (fh + j * 128) // 128
                    bk = 4 + (j % 2)
                    b.transpose_to(bk, b.bank(bk, 0, 128),
                                   ternf2[:, j * 128: j * 128 + 128],
                                   ev.ident[:, :], waits=[tk])
                    dst = strip[:, fc * 128: fc * 128 + 128]
                    w2_done.append(b.evac("V", bk, (lambda e, dst=dst, bk=bk:
                                                    e.tensor_copy(dst, b.bank(bk, 0, 128))),
                                          waits=[strip_free[0]]))
            strip_free[0] = b.dma(
                "Q", w2c[:, dcg * 128:(dcg + 1) * 128].rearrange(
                    "(c p) n -> p c n", p=128),
                strip.rearrange("p (c n) -> p c n", n=128), waits=w2_done)
            cache.append(strip_free[0])
            yield cache

    g_w2 = w2_steps(w1_cache[-2:])
    g_h0 = h_steps(0, [])
    w2_cache = []
    h_last = None
    hs_per = max(1, FC // DC)
    try:
        for dcg in range(DC):
            w2_cache = next(g_w2)
            for _ in range(hs_per):
                h_last = next(g_h0, h_last)
    except StopIteration:
        pass
    for t in g_h0:
        h_last = t

    st_tok = [None]
    dh_list = list(tiles(D, 512))
    for th in range(2):
        if th == 1:
            gen = h_steps(1, [bar_tok[0]] if False else [])
            for t in gen:
                h_last = t
        ldb2 = b.dma("G", ev.bias[0:1, 0:D], b2row, waits=[h_last])
        w2th = ev.a1(ev.R0, FC * 512)
        xi = ev.a1f(ev.R3, TH)
        xo = ev.a1f(ev.R3 + 2 * TH, TH)
        last_dh_mm = [h_last]
        tt_last = [h_last]
        for (dho, dhw) in dh_list:
            ldh = []
            for fg in range(0, FC, 8):
                fgn = min(8, FC - fg)
                ldh.append(b.dma(
                    "Q",
                    w2th[:, fg * dhw: (fg + fgn) * dhw].rearrange(
                        "p (c n) -> p c n", c=fgn),
                    w2c[fg * 128:(fg + fgn) * 128, dho:dho + dhw].rearrange(
                        "(c p) n -> p c n", p=128),
                    waits=[last_dh_mm[0]] + w2_cache))
            for dcg in range(dho // 128, (dho + dhw) // 128):
                jj = dcg - dho // 128
                ld_x = b.dma("Q", xi[:, 0:TH],
                             x_in[dcg * 128:(dcg + 1) * 128,
                                  th * TH:(th + 1) * TH],
                             waits=[tt_last[0]])
                tts = []
                for (o, w) in tiles(TH, 512):
                    bk = (o // 512) % 2
                    pairs = [(w2th[:, fc * dhw + jj * 128:
                                   fc * dhw + jj * 128 + 128], h_sl(fc, o, w))
                             for fc in range(FC)]
                    pairs.append((ev.bias[0:1, dcg * 128: dcg * 128 + 128],
                                  ev.ones_row[0:1, 0:w]))
                    gt = b.mm_group(bk, b.bank(bk, 0, w), pairs,
                                    waits=ldh + [ldb2])
                    last_dh_mm[0] = gt
                    dst = xo[:, o:o + w]
                    src1 = xi[:, o:o + w]
                    tts.append(b.evac("V", bk, (lambda e, dst=dst, src1=src1,
                                                bk=bk, w=w: e.tensor_add(
                        dst, b.bank(bk, 0, w), src1)), waits=[ld_x, st_tok[0]]))
                tt_last[0] = tts[-1]
                st_tok[0] = b.dma("Q", x_out[dcg * 128:(dcg + 1) * 128,
                                             th * TH:(th + 1) * TH],
                                  xo[:, 0:TH], waits=tts)
        b.barrier()
    return st_tok[0]



def block_l1attn(b: Bld, ev: Env, x_in, x_out, W):
    cfg = ev.cfg
    D, T, H, HD, R = cfg["D"], cfg["T"], cfg["H"], cfg["HD"], cfg["R"]
    assert HD == 64
    DC = D // 128
    DCT = DC * T
    KC = T // 128
    WELS = DC * D
    HH = H // 2
    xln = ev.a1(ev.R2, DCT)
    qk = ev.a1(ev.R0, DCT)
    v_ones = ev.a1(ev.R0, KC * H * 65)
    zq = ev.a1(ev.R3, HH * T)
    zk = ev.a2(0, HH * T)
    y = ev.a1(ev.R2, DCT)
    wA = ev.a1(ev.R1, WELS)
    wB = ev.a1(ev.R1 + WELS, WELS)
    tiny = ev.R1 + 2 * WELS
    vt_bf = ev.a1(tiny, 64)
    ut_bf = ev.a1(tiny + 64, 64)
    vut2 = ev.a1(tiny + 128, 64)
    v2 = ev.a1(tiny + 192, 32)
    v2q = ev.a1(tiny + 224, 32)
    stage = ev.a1f(ev.R4, 2 * D)
    gran_bf = ev.a1(ev.R4, 2 * D)

    layernorm(b, ev, x_in)
    b.barrier()

    pads = ev.a1f(ev.R4, 128)
    prev_ld, prev_tr = None, None
    for (nm, dst) in (("U", ut_bf), ("V", vt_bf)):
        ms = b.op("G", (lambda e: e.memset(pads[:, :], 0.0)),
                  waits=[prev_ld, prev_tr])
        ld = b.dma("Q", pads[0:HD, 0:R], W[nm], waits=[ms])
        tr = b.transpose_to(6, b.bank(6, 0, 128), pads[:, :], ev.ident[:, :],
                            waits=[ld, ms])
        cp = b.evac("V", 6, (lambda e, dst=dst: e.tensor_copy(
            dst[0:R, 0:HD], b.bank(6, 0, HD, np_=R))))
        prev_ld, prev_tr = ld, tr
    b.mm_group(7, b.bank(7, 0, HD, np_=HD),
               [(vt_bf[0:R, 0:HD], ut_bf[0:R, 0:HD])], waits=[cp])
    b.evac("V", 7, (lambda e: e.tensor_copy(vut2[0:64, :],
                                            b.bank(7, 0, HD, np_=HD))))
    b.evac("V", 7, (lambda e: e.tensor_copy(vut2[64:128, :],
                                            b.bank(7, 0, HD, np_=HD))))
    ldv2a = b.dma("G", v2[0:64, 0:R], W["V"])
    ldv2b = b.dma("G", v2[64:128, 0:R], W["V"])
    b.op("V", (lambda e: e.tensor_scalar_mul(v2q[:, :], v2[:, :],
                                             float(R) ** -0.5)),
         waits=[ldv2a, ldv2b])
    b.barrier()

    def wt_prep(nm, slot, war=None):
        toks = []
        buf_free = [war, war]
        for g in range(DC):
            buf = g % 2
            ld = b.dma("Q", stage[:, buf * D:(buf + 1) * D],
                       W[nm][g * 128:(g + 1) * 128, :], waits=[buf_free[buf]])
            for dc in range(DC):
                bk = 4 + (dc % 2)
                b.transpose_to(bk, b.bank(bk, 0, 128),
                               stage[:, buf * D + dc * 128: buf * D + dc * 128 + 128],
                               ev.ident[:, :], waits=[ld])
                dst = slot[:, dc * D + g * 128: dc * D + g * 128 + 128]
                toks.append(b.evac("V", bk, (lambda e, dst=dst, bk=bk:
                                             e.tensor_copy(dst, b.bank(bk, 0, 128)))))
            buf_free[buf] = toks[-1]
        return toks

    def z_proj(zdst, vsrc, src_toks):
        toks = []
        rot = [0]
        for h in range(H):
            for (o, w) in tiles(T, 512):
                bk = rot[0] % 4
                rot[0] += 1
                p0 = (h % 2) * 64
                lh = vsrc[p0:p0 + 64, 0:R]
                rh = qk[p0:p0 + 64, (h // 2) * T + o:(h // 2) * T + o + w]
                b.mm_group(bk, b.bank(bk, 0, w, np_=R), [(lh, rh)],
                           waits=[src_toks[-1]])
                dst = zdst[(h % 2) * 32:(h % 2) * 32 + 32,
                           (h // 2) * T + o:(h // 2) * T + o + w]
                toks.append(b.evac("V", bk, (lambda e, dst=dst, bk=bk, w=w:
                                             e.tensor_copy(dst, b.bank(bk, 0, w, np_=R)))))
        return toks

    wsl = lambda Wt: (lambda dc, ic: Wt[:, dc * D + ic * 128: dc * D + ic * 128 + 128])
    wtq = wt_prep("Wq", wA)
    ldbq = b.dma("G", ev.bias[0:1, 0:D], W["bq"].rearrange("(a d) -> a d", a=1))
    q_toks = proj_fm(b, ev, xln, wsl(wA), qk, bias_row=ev.bias,
                     wait0=[wtq[-1], ldbq])
    zq_toks = z_proj(zq, v2q, q_toks)
    wtk = wt_prep("Wk", wB, war=wtq[-1])
    ldbk = b.dma("G", ev.bias[0:1, 0:D], W["bk"].rearrange("(a d) -> a d", a=1),
                 waits=[q_toks[-1]])
    k_toks = proj_fm(b, ev, xln, wsl(wB), qk, bias_row=ev.bias,
                     wait0=[wtk[-1], ldbk, zq_toks[-1]])
    zk_toks = z_proj(zk, v2, k_toks)

    wveff_toks = [None]
    bufv = [None, None]
    for g in range(DC):
        buf = g % 2
        ld = b.dma("G", gran_bf[:, buf * D:(buf + 1) * D],
                   W["Wv"][g * 128:(g + 1) * 128, :],
                   waits=[bufv[buf], q_toks[-1], wtk[-1], zk_toks[-1]])
        for half in range(2):
            h_ = 2 * g + half
            for dc in range(DC):
                bk = dc % 4
                lh = gran_bf[half * 64:(half + 1) * 64,
                             buf * D + dc * 128: buf * D + dc * 128 + 128]
                rh = vut2[half * 64:(half + 1) * 64, :]
                b.mm_group(bk, b.bank(bk, 0, 64), [(lh, rh)], waits=[ld])
                dst = wA[:, dc * D + h_ * 64: dc * D + h_ * 64 + 64]
                wveff_toks.append(b.evac("V", bk, (lambda e, dst=dst, bk=bk:
                                                   e.tensor_copy(dst, b.bank(bk, 0, 64)))))
        bufv[buf] = wveff_toks[-1]

    ldbv = b.dma("G", ev.bias[0:1, 0:D], W["bv"].rearrange("(a d) -> a d", a=1),
                 waits=[k_toks[-1]])
    cb_toks = []
    for dcb in range(DC):
        b.mm_group(6, b.bank(6, 0, 1),
                   [(ev.bias[0:1, dcb * 128: dcb * 128 + 128],
                     ev.ones_row[0:1, 0:1])], waits=[ldbv])
        dst = ev.colbf[:, dcb:dcb + 1]
        cb_toks.append(b.evac("V", 6, (lambda e, dst=dst: e.tensor_copy(
            dst, b.bank(6, 0, 1)))))
    bve_toks = []
    for h_ in range(H):
        p0 = (h_ % 2) * 64
        lh = ev.colbf[p0:p0 + 64, h_ // 2: h_ // 2 + 1]
        rh = vut2[p0:p0 + 64, :]
        b.mm_group(7, b.bank(7, 0, 64, np_=1), [(lh, rh)], waits=[cb_toks[-1]])
        dst = ev.bias[0:1, h_ * 64: h_ * 64 + 64]
        bve_toks.append(b.evac("V", 7, (lambda e, dst=dst: e.tensor_copy(
            dst, b.bank(7, 0, 64, np_=1)))))

    von_ms = b.op("G", lambda e: e.memset(v_ones[:, :], 1.0),
                  waits=[zk_toks[-1]])
    wslA_r = lambda dc, io, iw: wA[:, dc * D + io: dc * D + io + iw]
    v_toks = proj_tm_vones(b, ev, xln, wslA_r, v_ones, bias_row=ev.bias,
                           wait0=[wveff_toks[-1], bve_toks[-1], von_ms])
    b.barrier()

    wto = wt_prep("Wo", wB)
    ldbo = b.dma("G", ev.bias[0:1, 0:D], W["bo"].rearrange("(a d) -> a d", a=1),
                 waits=[v_toks[-1]])
    b.barrier()

    sc_lh = lambda h, kc: zk[(h % 2) * 32:(h % 2) * 32 + 32,
                             (h // 2) * T + kc * 128:(h // 2) * T + kc * 128 + 128]
    sc_rh = lambda h, go, gw: zq[(h % 2) * 32:(h % 2) * 32 + 32,
                                 (h // 2) * T + go:(h // 2) * T + go + gw]
    if KC * 512 <= WELS:
        ptil2 = ev.a1(ev.R1, KC * 512)
    else:
        ptil2 = ev.a1(ev.R1 + 2 * WELS + 256, KC * 512)
    attention_core(b, ev, sc_lh, sc_rh, v_ones, y, False, ptil2)
    b.barrier()
    wslo = lambda dc, dco: wB[:, dc * D + dco * 128: dc * D + dco * 128 + 128]
    st = out_proj_residual(b, ev, y, wslo, x_in, x_out, bias_row=ev.bias,
                           wait0=[wto[-1], ldbo])
    b.barrier()
    return st



def block_moe(b: Bld, ev: Env, x_in, out_tm, W):
    cfg = ev.cfg
    D, T, FF, NE = cfg["D"], cfg["T"], cfg["FF"], cfg["NE"]
    DC = D // 128
    DCT = DC * T
    TH = T // 2
    FC = FF // 128
    FQ = FF // 4
    KC = T // 128
    TCH = TH // 128
    xln = ev.a1(ev.R2, DCT)
    wq_els = DC * FQ

    def acc_sl(tc):
        if tc < 8:
            return ev.a1f(ev.R0 + tc * 2 * D, D)
        return ev.a1f(ev.R1 + (tc - 8) * 2 * D, D)

    def h_sl(fc, o, w):
        if fc < 16:
            return ev.a1(ev.R3 + fc * TH + o, w)
        if fc < 24:
            return ev.a1(ev.R4 + (fc - 16) * TH + o, w)
        return ev.a2((fc - 24) * TH + o, w)

    wdring = ev.a2(8 * TH + 2 * DC * max(FF // 8, 128), 4096)
    xlnf = ev.xlnf_dram

    layernorm(b, ev, x_in, xlnf_dram=xlnf)
    b.barrier()

    pads = ev.a1f(ev.R4, D)
    asb = ev.a1f(ev.R4 + 2 * D, 512)
    xlf = [ev.a1f(ev.R4 + 2 * D + 1024, 512),
           ev.a1f(ev.R4 + 2 * D + 2048, 512)]
    ms = b.op("G", lambda e: e.memset(pads[:, :], 0.0))
    msa = b.op("G", lambda e: e.memset(asb[:, :], 0.0))
    ldr = b.dma("Q", pads[0:NE, 0:D], W["router_w"], waits=[ms])
    rw_toks = []
    for dc in range(DC):
        bk = 4 + (dc % 2)
        b.transpose_to(bk, b.bank(bk, 0, 128),
                       pads[:, dc * 128: dc * 128 + 128], ev.ident[:, :],
                       waits=[ldr, ms])
        dstf = ev.rwtf[:, dc * NE: dc * NE + NE]
        rw_toks.append(b.evac("V", bk, (lambda e, dstf=dstf, bk=bk:
                                        e.tensor_copy(dstf, b.bank(bk, 0, NE)))))
    sm = ev.smalls
    slot_rd = [None, None]
    for ti, (o, wti) in enumerate(tiles(T, 512)):
        Abk = 2 + (ti % 2)
        pre_w = b.bank_r[Abk] + [b.bank_w[Abk], rw_toks[-1]]
        for dc in range(DC):
            par = dc % 2
            ldx = b.dma("Q", xlf[par][:, 0:wti],
                        xlnf[dc * 128:(dc + 1) * 128, o:o + wti],
                        waits=[slot_rd[par]])
            lh = ev.rwtf[:, dc * NE: dc * NE + NE]
            slot_rd[par] = b.op(
                "T", (lambda e, lh=lh, par=par, wti=wti, Abk=Abk,
                      st=(dc == 0), sp=(dc == DC - 1):
                      e.matmul(b.bank(Abk, 0, wti, np_=NE), lh,
                               xlf[par][:, 0:wti], start=st, stop=sp)),
                waits=[ldx] + (pre_w if dc == 0 else []), inc=True)
        b.bank_w[Abk] = slot_rd[(DC - 1) % 2]
        b.bank_r[Abk] = []
        cpA = b.evac("V", Abk, (lambda e, wti=wti, Abk=Abk: e.tensor_copy(
            asb[0:NE, 0:wti], b.bank(Abk, 0, wti, np_=NE))), waits=[msa])
        for j in range(wti // 128):
            tc = (o + j * 128) // 128
            bk = 6 + (j % 2)
            b.transpose_to(bk, b.bank(bk, 0, 128),
                           asb[:, j * 128: j * 128 + 128], ev.ident[:, :],
                           waits=[cpA])
            _moe_top2(b, ev, bk, tc)
    b.barrier()

    z0 = b.op("G", lambda e: e.memset(ev.a1f(ev.R0, 8 * D)[:, :], 0.0))
    z1 = b.op("G", lambda e: e.memset(ev.a1f(ev.R1, 8 * D)[:, :], 0.0))
    b.barrier()

    FF8 = max(FF // 8, 128)
    NPH = FF // FF8
    w8els = DC * FF8
    wq8 = [ev.a2(8 * TH, w8els), ev.a2(8 * TH + w8els, w8els)]
    slot_free = [None, None]
    wq_tok = [None]
    for e_ in range(NE):
        for th in range(2):
            for gate in (True, False):
                nm = "w_gate" if gate else "w_up"
                for ph in range(NPH):
                    slot = ph % 2
                    ld = b.dma("G", wq8[slot].rearrange("p (c n) -> p c n", c=DC),
                               W[nm][e_].rearrange("(c p) n -> p c n", p=128)[
                                   :, :, ph * FF8:(ph + 1) * FF8],
                               waits=[slot_free[slot]])
                    rot = [0]
                    tok = None
                    for fcl in range(FF8 // 128):
                        fc = ph * (FF8 // 128) + fcl
                        for (o, w) in tiles(TH, 512):
                            bk = rot[0] % 2
                            rot[0] += 1
                            pairs = [(wq8[slot][:, dc * FF8 + fcl * 128:
                                                dc * FF8 + fcl * 128 + 128],
                                      xln[:, dc * T + th * TH + o:
                                          dc * T + th * TH + o + w])
                                     for dc in range(DC)]
                            b.mm_group(bk, b.bank(bk, 0, w), pairs, waits=[ld])
                            dst = h_sl(fc, o, w)
                            if gate:
                                tok = b.evac("S", bk, (lambda e, dst=dst, bk=bk,
                                                       w=w: e.activation(
                                    dst, b.bank(bk, 0, w), AF.Silu)))
                            else:
                                tok = b.evac("V", bk, (lambda e, dst=dst, bk=bk,
                                                       w=w: e.tensor_mul(
                                    dst, b.bank(bk, 0, w), dst)))
                    slot_free[slot] = tok
                    wq_tok[0] = tok
            ring_tok = [None, None]
            for dh in range(2):
                dhw = min(512, D - dh * 512)
                if dhw <= 0:
                    break
                first_done = [False] * TCH
                last_mm = [None]
                for fj in range(FC // 4):
                    slot = fj % 2
                    ld = b.dma("G",
                               wdring[:, slot * 2048: slot * 2048 + 4 * dhw].rearrange(
                                   "p (c n) -> p c n", c=4),
                               W["w_down"][e_].rearrange("(c p) n -> p c n", p=128)[
                                   :, 4 * fj: 4 * fj + 4, dh * 512: dh * 512 + dhw],
                               waits=[ring_tok[slot]])
                    for fc4 in range(4):
                        fc = fj * 4 + fc4
                        for tc in range(TCH):
                            lh = h_sl(fc, tc * 128, 128)
                            rh = wdring[:, slot * 2048 + fc4 * dhw:
                                        slot * 2048 + fc4 * dhw + dhw]
                            st_ = (fc == 0)
                            sp_ = (fc == FC - 1)
                            w_ = [ld, wq_tok[0]]
                            if st_:
                                w_ += b.bank_r[tc] + [b.bank_w[tc]]
                            tok = b.op("T", (lambda e, lh=lh, rh=rh, st_=st_,
                                             sp_=sp_, tc=tc, dhw=dhw:
                                             e.matmul(b.bank(tc, 0, dhw), lh, rh,
                                                      start=st_, stop=sp_)),
                                       waits=w_,
                                       inc=(fc4 == 3 and tc == TCH - 1))
                    ring_tok[slot] = tok
                    last_mm[0] = tok
                for tc in range(TCH):
                    b.bank_w[tc] = last_mm[0]
                    b.bank_r[tc] = []
                    gtc = th * TCH + tc
                    a_sl = acc_sl(gtc)[:, dh * 512: dh * 512 + dhw]
                    c_sl = ev.cw[:, gtc * NE + e_: gtc * NE + e_ + 1]
                    b.evac("V", tc, (lambda e, a_sl=a_sl, c_sl=c_sl, tc=tc,
                                     dhw=dhw: e.scalar_tensor_tensor(
                        a_sl, b.bank(tc, 0, dhw), c_sl, a_sl,
                        ALU.mult, ALU.add)), waits=[z0, z1])
    b.barrier()

    def xfull(dc):
        if dc < 4:
            return ev.a1f(ev.R3 + dc * 2 * T, T)
        if dc < 6:
            return ev.a1f(ev.R4 + (dc - 4) * 2 * T, T)
        return ev.a2f((dc - 6) * 2 * T, T)

    ld_toks = []
    for dc in range(DC):
        ld_toks.append(b.dma("Q", xfull(dc)[:, :],
                             x_in[dc * 128:(dc + 1) * 128, :]))
    obuf = ev.a2f(8 * TH, 2 * D)
    st_tok = [None, None]
    for tc in range(KC):
        par = tc % 2
        tts = []
        for dc in range(DC):
            bk = 4 + (dc % 2)
            b.transpose_to(bk, b.bank(bk, 0, 128),
                           xfull(dc)[:, tc * 128: tc * 128 + 128],
                           ev.ident[:, :], waits=[ld_toks[dc]])
            dst = obuf[:, par * D + dc * 128: par * D + dc * 128 + 128]
            a_sl = acc_sl(tc)[:, dc * 128: dc * 128 + 128]
            tts.append(b.evac("V", bk, (lambda e, dst=dst, a_sl=a_sl, bk=bk:
                                        e.tensor_add(dst, b.bank(bk, 0, 128), a_sl)),
                              waits=[st_tok[par]]))
        st_tok[par] = b.dma("Q", out_tm[tc * 128:(tc + 1) * 128, :],
                            obuf[:, par * D:(par + 1) * D], waits=tts)
    b.barrier()
    return [t for t in st_tok if t]


def _moe_top2(b, ev, bk, tc):
    NE = ev.cfg["NE"]
    sm = ev.smalls
    if True:
        b.evac("V", bk, (lambda e, bk=bk: e.reduce_max(
            sm[:, 0:1], b.bank(bk, 0, NE), axis=AX.X)))
        tv = b.evac("V", bk, (lambda e, bk=bk: e.tensor_scalar(
            sm[:, 8:8 + NE], b.bank(bk, 0, NE), sm[:, 0:1], None, ALU.subtract)))
        ts = b.op("S", (lambda e: e.activation(sm[:, 16:16 + NE],
                                               sm[:, 8:8 + NE], AF.Exp)),
                  waits=[tv])
        b.op("V", (lambda e: e.reduce_sum(sm[:, 1:2], sm[:, 16:16 + NE],
                                          axis=AX.X)), waits=[ts])
        b.op("V", lambda e: e.reciprocal(sm[:, 2:3], sm[:, 1:2]))
        b.op("V", lambda e: e.tensor_scalar(
            sm[:, 24:24 + NE], sm[:, 16:16 + NE], sm[:, 2:3], None, ALU.mult))
        b.op("V", lambda e: e.reduce_max(sm[:, 3:4], sm[:, 24:24 + NE], axis=AX.X))
        b.op("V", lambda e: e.tensor_scalar(
            sm[:, 32:32 + NE], sm[:, 24:24 + NE], sm[:, 3:4], 1e9,
            ALU.is_equal, ALU.mult))
        b.op("V", lambda e: e.tensor_tensor(
            sm[:, 40:40 + NE], sm[:, 24:24 + NE], sm[:, 32:32 + NE], ALU.subtract))
        b.op("V", lambda e: e.reduce_max(sm[:, 4:5], sm[:, 40:40 + NE], axis=AX.X))
        b.op("V", lambda e: e.tensor_scalar(
            sm[:, 48:48 + NE], sm[:, 24:24 + NE], sm[:, 4:5], None, ALU.is_ge))
        b.op("V", lambda e: e.tensor_tensor(
            sm[:, 48:48 + NE], sm[:, 48:48 + NE], sm[:, 24:24 + NE], ALU.mult))
        b.op("V", lambda e: e.tensor_add(sm[:, 5:6], sm[:, 3:4], sm[:, 4:5]))
        b.op("V", lambda e: e.reciprocal(sm[:, 6:7], sm[:, 5:6]))
        dst = ev.cw[:, tc * NE:(tc + 1) * NE]
        b.op("V", (lambda e, dst=dst: e.tensor_scalar(
            dst, sm[:, 48:48 + NE], sm[:, 6:7], None, ALU.mult)))



def block_moe_sparse(b: Bld, ev: Env, x_in, out_tm, W):
    cfg = ev.cfg
    D, T, FF, NE = cfg["D"], cfg["T"], cfg["FF"], cfg["NE"]
    DC = D // 128
    DCT = DC * T
    C = moe_cap(T)
    JCN = C // 128
    FC = FF // 128
    KC = T // 128
    xln = ev.a1(ev.R2, DCT)
    xlnf = ev.xlnf_dram
    xtm = ev.xtm_dram
    yed = ev.yed_dram
    t1d = ev.t1_dram
    t2d = ev.t2_dram
    idxd = ev.idx_dram
    cwd = ev.cw_dram
    xga = ev.xga_dram
    cwsl = ev.cwsl_dram

    def h_sl(fc, o, w):
        if fc < 16:
            return ev.a1(ev.R3 + fc * C + o, w)
        if fc < 24:
            return ev.a1(ev.R4 + (fc - 16) * C + o, w)
        return ev.a2((fc - 24) * C + o, w)

    wdring = ev.a2(8 * C + 2 * DC * max(FF // 8, 128), 4096)

    bregs = {}

    def _mk_regs(e):
        bregs["C"] = e.to_reg(C - 1)
        bregs["T"] = e.to_reg(T - 1)
        bregs["Y"] = e.to_reg(NE * C + 127)
        bregs["S"] = e.to_reg(NE * C - 1)
        return e.engine_nop()
    b.op("G", _mk_regs)

    zsrc = ev.a1(ev.R3, 2048)
    zmb = b.op("V", lambda e: e.memset(zsrc[:, :], 0.0))
    zga_toks = []
    for r0 in range(0, NE * C, 128):
        zga_toks.append(b.dma("Q", xga[r0:r0 + 128, :], zsrc[:, 0:D],
                              waits=[zmb]))
    zga_toks.append(b.dma(
        "Q", cwsl.rearrange("(p c) a -> p (c a)", p=128),
        zsrc.bitcast(F32)[:, 0:NE * C // 128], waits=[zmb]))

    layernorm(b, ev, x_in, xlnf_dram=xlnf, xtm_dram=xtm)
    b.barrier()

    pads = ev.a1f(ev.R4, D)
    asb = ev.a1f(ev.R4 + 2 * D, 512)
    xlf = [ev.a1f(ev.R4 + 2 * D + 1024, 512),
           ev.a1f(ev.R4 + 2 * D + 2048, 512)]
    ms = b.op("G", lambda e: e.memset(pads[:, :], 0.0))
    msa = b.op("G", lambda e: e.memset(asb[:, :], 0.0))
    ldr = b.dma("Q", pads[0:NE, 0:D], W["router_w"], waits=[ms])
    rw_toks = []
    for dc in range(DC):
        bk = 4 + (dc % 2)
        b.transpose_to(bk, b.bank(bk, 0, 128),
                       pads[:, dc * 128: dc * 128 + 128], ev.ident[:, :],
                       waits=[ldr, ms])
        dstf = ev.rwtf[:, dc * NE: dc * NE + NE]
        rw_toks.append(b.evac("V", bk, (lambda e, dstf=dstf, bk=bk:
                                        e.tensor_copy(dstf, b.bank(bk, 0, NE)))))
    sm = ev.smalls
    prefill_toks = []
    for e_ in range(NE):
        prefill_toks.append(b.dma(
            "Q", idxd[e_].rearrange("(p a) -> p a", p=128),
            ev.ifill[:, 0:C // 128]))
    carry = ev.rows[0:1, 2050:2050 + NE]
    bx = ev.R4 + 2 * D + 3072
    mk = ev.a1f(bx, NE)
    rank = ev.a1f(bx + 16, NE)
    r2f = ev.a1f(bx + 32, NE)
    ecadd = ev.a1f(bx + 48, NE)
    ri2b = [ev.locs[:, 8 * i: 8 * i + NE] for i in range(4)]
    rib = [ev.locs[:, 32 + 8 * i: 32 + 8 * i + NE] for i in range(4)]
    ec_t = None
    for e_ in range(NE):
        ec_t = b.op("V", (lambda e, e_=e_: e.memset(
            ecadd[:, e_:e_ + 1], float(e_ * C))))
    cz = b.op("V", lambda e: e.memset(carry, 0.0))
    st = {"carry_t": cz, "idx": [], "up": [],
          "ri_rd": [[] for _ in range(4)], "ri2_rd": [[] for _ in range(4)],
          "xrow_war": [[] for _ in range(4)]}
    idx_sc_toks = st["idx"]
    up_toks = st["up"]

    def idx_stage(tc):
        xpar = tc % 4
        ldx = b.dma("Q", ev.a1(ev.R0 + xpar * 1024, 1024)[:, 0:D],
                    xtm[tc * 128:(tc + 1) * 128, :], waits=st["xrow_war"][xpar])
        cwt = ev.cw[:, tc * NE:(tc + 1) * NE]
        tmk = b.op("V", (lambda e, cwt=cwt: e.tensor_single_scalar(
            mk, cwt, 0.0, ALU.is_gt)))
        bkr = tc % 2
        b.mm_group(bkr, b.bank(bkr, 0, NE),
                   [(ev.ustrict[:, :], mk),
                    (ev.ones_row_f[0:1, 0:128], carry)],
                   waits=[tmk, st["carry_t"]])
        b.evac("V", bkr, (lambda e, bkr=bkr: e.tensor_copy(
            rank, b.bank(bkr, 0, NE))))
        bkc = 4
        b.mm_group(bkc, b.bank(bkc, 0, NE, np_=1),
                   [(ev.ones_col_f[:, :], mk)], waits=[tmk])
        st["carry_t"] = b.evac("V", bkc, (lambda e: e.tensor_tensor(
            carry, carry, b.bank(bkc, 0, NE, np_=1), ALU.add)))
        b.op("V", lambda e: e.tensor_single_scalar(rank, rank, 65536.0,
                                                   ALU.subtract))
        b.op("V", lambda e: e.tensor_tensor(rank, rank, mk, ALU.mult))
        b.op("V", lambda e: e.tensor_single_scalar(rank, rank, 65536.0, ALU.add))
        ri = rib[xpar]
        cvi = b.op("V", (lambda e, ri=ri: e.tensor_copy(ri, rank)),
                   waits=st["ri_rd"][xpar])
        b.op("V", lambda e: e.tensor_tensor(r2f, rank, ecadd, ALU.add),
             waits=[ec_t])
        ri2 = ri2b[xpar]
        cvi2 = b.op("V", (lambda e, ri2=ri2: e.tensor_copy(ri2, r2f)),
                    waits=st["ri2_rd"][xpar])
        st["ri_rd"][xpar] = []
        st["ri2_rd"][xpar] = []
        for e_ in range(NE):
            tk = b.idma(
                (lambda e, e_=e_, tc=tc, ri=ri: e.indirect_dma_start(
                    idxd.rearrange("e (c a) -> (e c) a", a=1),
                    IndirectOffsetOnAxis(ri[:, e_:e_ + 1], 0),
                    ev.iota[:, tc:tc + 1], None,
                    element_offset=e_ * C,
                    bounds_check=bregs["C"], oob_is_err=False)),
                waits=[cvi] + prefill_toks)
            st["idx"].append(tk)
            st["ri_rd"][xpar].append(tk)
        for e_ in range(NE):
            sx = b.idma(
                (lambda e, e_=e_, xpar=xpar, ri2=ri2: e.indirect_dma_start(
                    xga[:, :], IndirectOffsetOnAxis(ri2[:, e_:e_ + 1], 0),
                    ev.a1(ev.R0 + xpar * 1024, 1024)[:, 0:D], None,
                    bounds_check=bregs["S"], oob_is_err=False)),
                waits=[cvi2, ldx])
            sc = b.idma(
                (lambda e, e_=e_, tc=tc, ri2=ri2: e.indirect_dma_start(
                    cwsl[:, :], IndirectOffsetOnAxis(ri2[:, e_:e_ + 1], 0),
                    ev.cw[:, tc * NE + e_: tc * NE + e_ + 1], None,
                    bounds_check=bregs["S"], oob_is_err=False)),
                waits=[cvi2])
            st["up"] += [sx, sc]
            st["ri2_rd"][xpar] += [sx, sc]
        st["xrow_war"][xpar] = st["up"][-2 * NE:]

    slot_rd = [None, None]
    for ti, (o, wti) in enumerate(tiles(T, 512)):
        Abk = 2 + (ti % 2)
        pre_w = b.bank_r[Abk] + [b.bank_w[Abk], rw_toks[-1]]
        for dc in range(DC):
            par = dc % 2
            ldx = b.dma("Q", xlf[par][:, 0:wti],
                        xlnf[dc * 128:(dc + 1) * 128, o:o + wti],
                        waits=[slot_rd[par]])
            lh = ev.rwtf[:, dc * NE: dc * NE + NE]
            slot_rd[par] = b.op(
                "T", (lambda e, lh=lh, par=par, wti=wti, Abk=Abk,
                      st=(dc == 0), sp=(dc == DC - 1):
                      e.matmul(b.bank(Abk, 0, wti, np_=NE), lh,
                               xlf[par][:, 0:wti], start=st, stop=sp)),
                waits=[ldx] + (pre_w if dc == 0 else []), inc=True)
        b.bank_w[Abk] = slot_rd[(DC - 1) % 2]
        b.bank_r[Abk] = []
        cpA = b.evac("V", Abk, (lambda e, wti=wti, Abk=Abk: e.tensor_copy(
            asb[0:NE, 0:wti], b.bank(Abk, 0, wti, np_=NE))), waits=[msa])
        for j in range(wti // 128):
            tc = (o + j * 128) // 128
            bk = 6 + (j % 2)
            b.transpose_to(bk, b.bank(bk, 0, 128),
                           asb[:, j * 128: j * 128 + 128], ev.ident[:, :],
                           waits=[cpA])
            _moe_top2(b, ev, bk, tc)
            idx_stage(tc)

    st_cw = b.dma("Q", cwd.rearrange("(c p) n -> p c n", p=128),
                  ev.cw.rearrange("p (c n) -> p c n", n=NE),
                  waits=up_toks[-1:])
    locs = ev.locs
    NJ = NE * JCN
    lfil = b.op("G", lambda e: e.memset(locs[:, 0:KC], NE * C),
                waits=up_toks[-NE:] + idx_sc_toks[-NE:])
    zbuf = ev.a1f(ev.R1 + 8192, 1024)
    zz = b.op("V", lambda e: e.memset(zbuf[:, :], 0.0))
    zpad = b.dma("Q", yed[NE * C:NE * C + 128, :], zbuf[:, 0:D], waits=[zz])
    pf1 = b.dma("Q", t1d.rearrange("(p c) a -> p (c a)", p=128),
                locs[:, 0:KC], waits=[lfil])
    pf2 = b.dma("Q", t2d.rearrange("(p c) a -> p (c a)", p=128),
                locs[:, 0:KC], waits=[lfil])
    lio = b.op("G", lambda e: e.iota(locs[:, 0:NJ], pattern=[[128, NJ]],
                                     base=0, channel_multiplier=1),
               waits=[pf1, pf2] + up_toks[-NE:])
    i2sb = ev.smalls[:, 0:16].bitcast(mybir.dt.int32)
    tb_buf = {1: i2sb[:, 8:16], 2: i2sb[:, 0:8]}
    tb_prev = {1: [pf1, lio], 2: [pf2, lio]}

    def tb_round(k):
        for cid, tbl, e_ in ((2, t2d, k), (1, t1d, NE - 1 - k)):
            buf = tb_buf[cid]
            toks = []
            for jc in range(JCN):
                lde = b.dma("Q", buf[:, jc:jc + 1],
                            idxd[e_, jc * 128:(jc + 1) * 128].rearrange(
                                "(p a) -> p a", p=128),
                            waits=tb_prev[cid] + idx_sc_toks[-NE:])
                toks.append(b.idma(
                    (lambda e, tbl=tbl, buf=buf, jc=jc, e_=e_:
                     e.indirect_dma_start(
                         tbl[:, :],
                         IndirectOffsetOnAxis(buf[:, jc:jc + 1], 0),
                         locs[:, e_ * JCN + jc: e_ * JCN + jc + 1], None,
                         bounds_check=bregs["T"], oob_is_err=False)),
                    waits=[lde] + tb_prev[cid]))
            tb_prev[cid] = toks
    b.barrier()

    FF8 = max(FF // 8, 128)
    NPH = FF // FF8
    w8els = DC * FF8
    wq8 = [ev.a2(8 * C, w8els), ev.a2(8 * C + w8els, w8els)]
    xgt = [ev.a1(ev.R0, JCN * D), ev.a1(ev.R0 + 8192, JCN * D)]
    xgf = [ev.a1(ev.R1, DC * C), ev.a1(ev.R2, DC * C)]
    y_sb = ev.a1f(ev.R1 + DC * C, 4 * 1024)
    cwcp = [ev.smalls[:, 24 + 8 * i: 24 + 8 * i + JCN] for i in range(4)]
    slot_free = [None, None]
    wq_tok = [None]
    y_war = [[] for _ in range(JCN)]
    up_w = up_toks[-NE:]
    xgt_war = [list(up_w), list(up_w)]
    fm_war = [[], []]
    cw_war = [list(up_w) for _ in range(4)]
    ld_x = {}
    ld_cw = {}
    trd = {}
    idx_sb = ev.colbf[:, 0:16].bitcast(mybir.dt.int32)
    STG = globals().get("DBG_MOE_STG", 99)
    NEXP = NE if DBG_MOE_NEXP is None else DBG_MOE_NEXP

    def load(e_):
        par, p4 = e_ % 2, e_ % 4
        xts, cts = [], []
        for jc in range(JCN):
            xts.append(b.dma(
                "Q", xgt[par][:, jc * D: jc * D + D],
                xga[e_ * C + jc * 128: e_ * C + jc * 128 + 128, :],
                waits=(xgt_war[par] + idx_sc_toks[-NE:]) if jc == 0 else ()))
            cts.append(b.dma(
                "Q", cwcp[p4][:, jc:jc + 1],
                cwsl[e_ * C + jc * 128: e_ * C + jc * 128 + 128, :],
                waits=cw_war[p4] if jc == 0 else ()))
        ld_x[e_], ld_cw[e_] = xts, cts

    def xpose(e_):
        par = e_ % 2
        jlast = None
        for jc in range(JCN):
            for dc in range(DC):
                bt = 6 + (dc % 2)
                src = xgt[par][:, jc * D + dc * 128: jc * D + dc * 128 + 128]
                b.transpose_to(bt, b.bankb(bt, 0, 128), src, ev.identb[:, :],
                               waits=[ld_x[e_][jc]])
                dst = xgf[par][:, dc * C + jc * 128: dc * C + jc * 128 + 128]
                jlast = b.evac("V", bt, (lambda e, dst=dst, bt=bt:
                                         e.tensor_copy(dst, b.bankb(bt, 0, 128))),
                               waits=fm_war[par])
        trd[e_] = jlast
        xgt_war[par] = [jlast]

    def compute(e_):
        par, p4 = e_ % 2, e_ % 4
        xg_fm = xgf[par]
        cwc = cwcp[p4]
        for gate in (True, False):
            nm = "w_gate" if gate else "w_up"
            for ph in range(NPH):
                slot = ph % 2
                ld = b.dma("G", wq8[slot].rearrange("p (c n) -> p c n", c=DC),
                           W[nm][e_].rearrange("(c p) n -> p c n", p=128)[
                               :, :, ph * FF8:(ph + 1) * FF8],
                           waits=[slot_free[slot]])
                rot = [0]
                tok = None
                for fcl in range(FF8 // 128):
                    fc = ph * (FF8 // 128) + fcl
                    for (o, w) in tiles(C, 512):
                        bk = rot[0] % 4
                        rot[0] += 1
                        pairs = [(wq8[slot][:, dc * FF8 + fcl * 128:
                                            dc * FF8 + fcl * 128 + 128],
                                  xg_fm[:, dc * C + o: dc * C + o + w])
                                 for dc in range(DC)]
                        b.mm_group(bk, b.bank(bk, 0, w), pairs,
                                   waits=[ld, trd[e_]])
                        dst = h_sl(fc, o, w)
                        if gate:
                            tok = b.evac("S", bk, (lambda e, dst=dst, bk=bk,
                                                   w=w: e.activation(
                                dst, b.bank(bk, 0, w), AF.Silu)))
                        else:
                            tok = b.evac("V", bk, (lambda e, dst=dst, bk=bk,
                                                   w=w: e.tensor_mul(
                                dst, b.bank(bk, 0, w), dst)))
                slot_free[slot] = tok
                wq_tok[0] = tok
        fm_war[par] = [wq_tok[0]]
        if STG < 5:
            return
        ring_tok = [None, None]
        ev_last = None
        for dh in range(2):
            dhw = min(512, D - dh * 512)
            if dhw <= 0:
                break
            last_mm = [None]
            for fj in range(FC // 4):
                slot = fj % 2
                ld = b.dma("G",
                           wdring[:, slot * 2048: slot * 2048 + 4 * dhw].rearrange(
                               "p (c n) -> p c n", c=4),
                           W["w_down"][e_].rearrange("(c p) n -> p c n", p=128)[
                               :, 4 * fj: 4 * fj + 4, dh * 512: dh * 512 + dhw],
                           waits=[ring_tok[slot]])
                for fc4 in range(4):
                    fc = fj * 4 + fc4
                    for jc in range(JCN):
                        lh = h_sl(fc, jc * 128, 128)
                        rh = wdring[:, slot * 2048 + fc4 * dhw:
                                    slot * 2048 + fc4 * dhw + dhw]
                        st_ = (fc == 0)
                        sp_ = (fc == FC - 1)
                        w_ = [ld, wq_tok[0]]
                        if st_:
                            w_ += b.bank_r[jc] + [b.bank_w[jc]]
                        tok = b.op("T", (lambda e, lh=lh, rh=rh, st_=st_,
                                         sp_=sp_, jc=jc, dhw=dhw:
                                         e.matmul(b.bank(jc, 0, dhw), lh, rh,
                                                  start=st_, stop=sp_)),
                                   waits=w_,
                                   inc=(fc4 == 3 and jc == JCN - 1))
                ring_tok[slot] = tok
                last_mm[0] = tok
            for jc in range(JCN):
                b.bank_w[jc] = last_mm[0]
                b.bank_r[jc] = []
                ydst = y_sb[:, jc * 512: jc * 512 + dhw]
                ev_t = b.evac("V", jc, (lambda e, ydst=ydst, jc=jc, dhw=dhw,
                                        cwc=cwc:
                                        e.tensor_scalar(ydst, b.bank(jc, 0, dhw),
                                                        cwc[:, jc:jc + 1], None,
                                                        ALU.mult)),
                              waits=y_war[jc] + [ld_cw[e_][jc]])
                ev_last = ev_t
                if STG < 6:
                    continue
                y_war[jc] = [b.dma(
                    "Q", yed[e_ * C + jc * 128: e_ * C + jc * 128 + 128,
                             dh * 512: dh * 512 + dhw],
                    ydst, waits=[ev_t])]
        cw_war[p4] = [ev_last]
        tb_round(e_)

    for k in range(min(2, NEXP)):
        load(k)
    if NEXP > 0:
        xpose(0)
    for e_ in range(NEXP):
        if e_ + 2 < NEXP:
            load(e_ + 2)
        if e_ + 1 < NEXP:
            xpose(e_ + 1)
        if STG < 4:
            continue
        compute(e_)
    b.barrier()

    def xfull(dc):
        if dc < 4:
            return ev.a1f(ev.R3 + dc * 2 * T, T)
        if dc < 6:
            return ev.a1f(ev.R4 + (dc - 4) * 2 * T, T)
        return ev.a2f((dc - 6) * 2 * T, T)

    ld_toks = []
    for dc in range(DC):
        ld_toks.append(b.dma("Q", xfull(dc)[:, :],
                             x_in[dc * 128:(dc + 1) * 128, :]))
    obuf = ev.a2f(max(8 * C, max(DC - 6, 0) * 2 * T), 2 * D)
    abuf = [ev.a1f(ev.R0, D), ev.a1f(ev.R0 + 2 * D, D)]
    bbuf = [ev.a1f(ev.R0 + 4 * D, D), ev.a1f(ev.R0 + 6 * D, D)]
    st_tok = [None, None]
    tc_war = [[], []]
    for tc in range(KC):
        par = tc % 2
        ldt = b.dma("Q", idx_sb[:, 2 * par: 2 * par + 1],
                    t1d[tc * 128:(tc + 1) * 128, :], waits=tc_war[par])
        ldt2 = b.dma("Q", idx_sb[:, 2 * par + 1: 2 * par + 2],
                     t2d[tc * 128:(tc + 1) * 128, :], waits=tc_war[par])
        g1 = b.idma((lambda e, par=par: e.indirect_dma_start(
            abuf[par][:, :], None, yed[:, :],
            IndirectOffsetOnAxis(idx_sb[:, 2 * par: 2 * par + 1], 0),
            bounds_check=bregs["Y"], oob_is_err=False)),
            waits=[ldt, st_tok[par]])
        g2 = b.idma((lambda e, par=par: e.indirect_dma_start(
            bbuf[par][:, :], None, yed[:, :],
            IndirectOffsetOnAxis(idx_sb[:, 2 * par + 1: 2 * par + 2], 0),
            bounds_check=bregs["Y"], oob_is_err=False)),
            waits=[ldt2])
        g12 = b.op("V", (lambda e, par=par: e.tensor_add(
            abuf[par][:, :], abuf[par][:, :], bbuf[par][:, :])),
            waits=[g1, g2])
        tc_war[par] = [g12]
        tts = []
        for dc in range(DC):
            bk = 4 + (dc % 2)
            b.transpose_to(bk, b.bank(bk, 0, 128),
                           xfull(dc)[:, tc * 128: tc * 128 + 128],
                           ev.ident[:, :], waits=[ld_toks[dc]])
            dst = obuf[:, par * D + dc * 128: par * D + dc * 128 + 128]
            a_sl = abuf[par][:, dc * 128: dc * 128 + 128]
            tts.append(b.evac("V", bk, (lambda e, dst=dst, a_sl=a_sl, bk=bk:
                                        e.tensor_add(dst, b.bank(bk, 0, 128), a_sl)),
                              waits=[g12, st_tok[par]]))
        st_tok[par] = b.dma("Q", out_tm[tc * 128:(tc + 1) * 128, :],
                            obuf[:, par * D:(par + 1) * D], waits=tts)
    b.barrier()
    return [t for t in st_tok if t]



BLOCK_WEIGHTS = {
    "l0a": [("Wq", "L0_attn_Wq"), ("Wk", "L0_attn_Wk"), ("Wv", "L0_attn_Wv"),
            ("Wo", "L0_attn_Wo")],
    "l0f": [("W1", "L0_ffn_W1"), ("b1", "L0_ffn_b1"), ("W2", "L0_ffn_W2"),
            ("b2", "L0_ffn_b2")],
    "l1a": [("Wq", "L1_attn_Wq"), ("bq", "L1_attn_bq"), ("Wk", "L1_attn_Wk"),
            ("bk", "L1_attn_bk"), ("Wv", "L1_attn_Wv"), ("bv", "L1_attn_bv"),
            ("Wo", "L1_attn_Wo"), ("bo", "L1_attn_bo"), ("U", "L1_attn_U"),
            ("V", "L1_attn_V")],
    "moe": [("router_w", "L1_ffn_router_w"), ("w_gate", "L1_ffn_w_gate"),
            ("w_up", "L1_ffn_w_up"), ("w_down", "L1_ffn_w_down")],
}

WSHAPES = {
    "L0_attn_Wq": ("D", "D"), "L0_attn_Wk": ("D", "D"),
    "L0_attn_Wv": ("D", "D"), "L0_attn_Wo": ("D", "D"),
    "L0_ffn_W1": ("FF", "D"), "L0_ffn_b1": ("FF",),
    "L0_ffn_W2": ("D", "FF"), "L0_ffn_b2": ("D",),
    "L1_attn_Wq": ("D", "D"), "L1_attn_bq": ("D",),
    "L1_attn_Wk": ("D", "D"), "L1_attn_bk": ("D",),
    "L1_attn_Wv": ("D", "D"), "L1_attn_bv": ("D",),
    "L1_attn_Wo": ("D", "D"), "L1_attn_bo": ("D",),
    "L1_attn_U": ("HD", "R"), "L1_attn_V": ("HD", "R"),
    "L1_ffn_router_w": ("NE", "D"),
    "L1_ffn_w_gate": ("NE", "D", "FF"), "L1_ffn_w_up": ("NE", "D", "FF"),
    "L1_ffn_w_down": ("NE", "FF", "D"),
}

BLOCK_FN = {}


def build_program(cfg, parts):
    nc = bass.Bass("TRN2", target_bir_lowering=False)
    stack = contextlib.ExitStack()
    b = Bld(nc, stack)
    ev = Env(b, stack, cfg)
    b.barrier()
    D, T = cfg["D"], cfg["T"]

    ev.xlnf_dram = None
    ev.xtm_dram = ev.yed_dram = ev.idx_dram = ev.cw_dram = None
    ev.t1_dram = ev.t2_dram = None
    if "moe" in parts:
        ev.xlnf_dram = nc.dram_tensor("xlnf", [D, T], F32, kind="Internal").ap()
        if SPARSE_MOE:
            NE = cfg["NE"]
            _k = "ExternalOutput" if globals().get("DBG_SPARSE") else "Internal"
            CP = moe_cap(T)
            ev.xtm_dram = nc.dram_tensor("xtm", [T, D], BF16, kind=_k).ap()
            ev.yed_dram = nc.dram_tensor("yed", [NE * CP + 128, D], F32,
                                         kind=_k).ap()
            ev.t1_dram = nc.dram_tensor("t1d", [T, 1], mybir.dt.int32,
                                        kind=_k).ap()
            ev.t2_dram = nc.dram_tensor("t2d", [T, 1], mybir.dt.int32,
                                        kind=_k).ap()
            ev.idx_dram = nc.dram_tensor("idxd", [NE, CP], mybir.dt.int32,
                                         kind=_k).ap()
            ev.cw_dram = nc.dram_tensor("cwd", [T, NE], F32, kind=_k).ap()
            ev.xga_dram = nc.dram_tensor("xga", [NE * CP, D], BF16,
                                         kind=_k).ap()
            ev.cwsl_dram = nc.dram_tensor("cwsl", [NE * CP, 1], F32,
                                          kind=_k).ap()

    needed = sorted({nm for p in parts for (_, nm) in BLOCK_WEIGHTS[p]})
    wdict = {nm: nc.dram_tensor(nm, [cfg[s] for s in WSHAPES[nm]], F32,
                                kind="ExternalInput").ap() for nm in needed}

    first, last = parts[0], parts[-1]
    if first == "l0a":
        x0 = nc.dram_tensor("x0", [T, D], F32, kind="ExternalInput").ap()
        xa = nc.dram_tensor("xa", [D, T], F32, kind="Internal").ap()
    else:
        xa = nc.dram_tensor("xin", [D, T], F32, kind="ExternalInput").ap()
    xb = nc.dram_tensor("xb", [D, T], F32, kind="Internal").ap()
    if last == "moe":
        out = nc.dram_tensor("out", [T, D], F32, kind="ExternalOutput").ap()
    else:
        out = nc.dram_tensor("xout", [D, T], F32, kind="ExternalOutput").ap()
    ev.w1c_dram = ev.w2c_dram = None
    if "l0f" in parts:
        FF = cfg["FF"]
        ev.w1c_dram = nc.dram_tensor(
            "w1c", [cfg["FF"] // 128, 128, (D // 128) * 128], BF16,
            kind="Internal").ap()
        ev.w2c_dram = nc.dram_tensor("w2c", [FF, D], BF16, kind="Internal").ap()

    if first == "l0a":
        pre_transpose(b, ev, x0, xa)
        b.barrier()

    ev.l0f_W = None
    fns = {"l0a": block_l0attn, "l0f": block_l0ffn, "l1a": block_l1attn,
           "moe": block_moe_sparse if SPARSE_MOE else block_moe}
    cur = xa
    final_toks = []
    for i, p in enumerate(parts):
        Wp = {k: wdict[nm] for (k, nm) in BLOCK_WEIGHTS[p]}
        is_last = i == len(parts) - 1
        dst = out if is_last else (xb if cur is xa else xa)
        tok = fns[p](b, ev, cur, dst, Wp)
        final_toks = tok if isinstance(tok, list) else [tok]
        cur = dst
    b.finish(final_toks)
    return nc, stack


_PART_SETS = {
    "fused": [["l0a", "l0f", "l1a", "moe"]],
    "blocks": [["l0a"], ["l0f"], ["l1a"], ["moe"]],
}


def kernel(**inputs):
    cfg = cfg_full()
    part_sets = _PART_SETS[MODE]
    x_full = np.ascontiguousarray(np.asarray(inputs["L0_attn_x"], np.float32))
    B = x_full.shape[0]
    assert B == N_CORES
    cur_per_core = [np.ascontiguousarray(x_full[i]) for i in range(B)]
    for parts in part_sets:
        nc, stack = build_program(cfg, parts)
        needed = sorted({nm for p in parts for (_, nm) in BLOCK_WEIGHTS[p]})
        wvals = {nm: np.ascontiguousarray(np.asarray(inputs[nm], np.float32))
                 for nm in needed}
        xkey = "x0" if parts[0] == "l0a" else "xin"
        in_maps = [dict(wvals, **{xkey: cur_per_core[c]}) for c in range(B)]
        res = run_bass_kernel_spmd(nc, in_maps, core_ids=list(range(B)))
        stack.close()
        okey = "out" if parts[-1] == "moe" else "xout"
        cur_per_core = [np.ascontiguousarray(np.asarray(res.results[c][okey],
                                                        np.float32))
                        for c in range(B)]
    return np.stack(cur_per_core, axis=0).astype(np.float32)


# version 23
# speedup vs baseline: 1.1345x; 1.1345x over previous
import contextlib
import numpy as np
import ml_dtypes

import concourse.bass as bass
import concourse.mybir as mybir
from concourse.bass_utils import run_bass_kernel_spmd
from concourse.bass_core_rust import IndirectOffsetOnAxis

F32 = mybir.dt.float32
BF16 = mybir.dt.bfloat16
AF = mybir.ActivationFunctionType
ALU = mybir.AluOpType
AX = mybir.AxisListType

N_CORES = 8
LN_EPS = 1e-5
MODE = "fused"
SPARSE_MOE = True
DBG_MOE_NEXP = None


def cfg_full():
    return dict(D=1024, T=2048, H=16, HD=64, R=32, FF=4096, NE=8)


def tiles(total, w):
    o = 0
    while o < total:
        yield o, min(w, total - o)
        o += w


def moe_cap(T):
    return max(128, (3 * T // 8) // 128 * 128)



class Bld:
    NLANE = 8

    def __init__(self, nc, stack):
        self.nc = nc
        self.ops = {e: [] for e in ("T", "V", "S", "G", "Q")}
        self.sem_names = ["sT", "sV", "sS", "sG"]
        for q in ("Q", "G"):
            self.sem_names += [f"q{q}{i}" for i in range(self.NLANE)]
        self.sems = {n: stack.enter_context(nc.semaphore(n)) for n in self.sem_names}
        self.cnt = {n: 0 for n in self.sem_names}
        self.rr = {"Q": 0, "G": 0}
        self.lastw = {}
        self.eng_sem = {"T": "sT", "V": "sV", "S": "sS", "G": "sG"}
        self.psum = stack.enter_context(nc.psum_tensor("psA", [128, 4096], F32))
        self.bank_w = [None] * 8
        self.bank_r = [[] for _ in range(8)]
        self.t_dirty = False

    def bank(self, bk, lo=0, hi=512, np_=128, p0=0):
        return self.psum[p0:p0 + np_, bk * 512 + lo: bk * 512 + hi]

    def bankb(self, bk, lo=0, hi=128, np_=128, p0=0):
        return self.psum.bitcast(BF16)[p0:p0 + np_, bk * 1024 + lo: bk * 1024 + hi]

    def _wait_list(self, ek, waits):
        out = []
        for t in waits:
            if t is None:
                continue
            s, v = t
            if v <= 0:
                continue
            key = (ek, s)
            if self.lastw.get(key, 0) < v:
                self.lastw[key] = v
                out.append((self.sems[s], v))
        return out

    def op(self, ek, fn, waits=(), inc=False):
        waits = list(waits)
        sem_name = self.eng_sem.get(ek)
        if ek in ("V", "S", "G"):
            waits.append((sem_name, self.cnt[sem_name]))
            inc = True
        wl = self._wait_list(ek, waits)
        tok = None
        if inc:
            self.cnt[sem_name] += 1
            tok = (sem_name, self.cnt[sem_name])
            if ek == "T":
                self.t_dirty = False
        elif ek == "T":
            self.t_dirty = True
        sems = self.sems

        def cl(eng, fn=fn, wl=wl, inc=inc, sem_name=sem_name):
            for (h, v) in wl:
                eng.wait_ge(h, v)
            ins = fn(eng)
            if inc:
                ins.then_inc(sems[sem_name], 1)
        self.ops[ek].append(cl)
        return tok

    def dma(self, q, out, in_, waits=()):
        lane = self.rr[q] % self.NLANE
        self.rr[q] += 1
        sem_name = f"q{q}{lane}"
        wl = self._wait_list(q, list(waits) + [(sem_name, self.cnt[sem_name])])
        self.cnt[sem_name] += 16
        tok = (sem_name, self.cnt[sem_name])
        sems = self.sems

        def cl(eng, out=out, in_=in_, wl=wl, sem_name=sem_name):
            for (h, v) in wl:
                eng.wait_ge(h, v)
            eng.dma_start(out=out, in_=in_).then_inc(sems[sem_name], 16)
        self.ops[q].append(cl)
        return tok

    def idma(self, fn, waits=()):
        lane = self.rr["G"] % self.NLANE
        self.rr["G"] += 1
        sem_name = f"qG{lane}"
        wl = self._wait_list("G", list(waits) + [(sem_name, self.cnt[sem_name])])
        self.cnt[sem_name] += 16
        tok = (sem_name, self.cnt[sem_name])
        sems = self.sems

        def cl(eng, fn=fn, wl=wl, sem_name=sem_name):
            for (h, v) in wl:
                eng.wait_ge(h, v)
            fn(eng).then_inc(sems[sem_name], 16)
        self.ops["G"].append(cl)
        return tok

    def barrier(self, engs="TVSGQ"):
        assert not self.t_dirty, "barrier with un-inc'd T ops"
        snap = [(s, self.cnt[s]) for s in self.sem_names]
        for ek in engs:
            own = self.eng_sem.get(ek)
            wl = self._wait_list(ek, [(s, v) for (s, v) in snap if s != own])
            if not wl:
                continue

            def cl(eng, wl=wl):
                for (h, v) in wl:
                    eng.wait_ge(h, v)
            self.ops[ek].append(cl)

    def mm_group(self, bk, out_ap, pairs, waits=()):
        w = list(waits) + self.bank_r[bk] + [self.bank_w[bk]]
        n = len(pairs)
        tok = None
        for i, (l, r) in enumerate(pairs):
            tok = self.op(
                "T",
                (lambda e, l=l, r=r, st=(i == 0), sp=(i == n - 1), o=out_ap:
                 e.matmul(o, l, r, start=st, stop=sp)),
                waits=w if i == 0 else (), inc=(i == n - 1))
        self.bank_w[bk] = tok
        self.bank_r[bk] = []
        return tok

    def transpose_to(self, bk, out_ap, in_ap, ident, waits=()):
        w = list(waits) + self.bank_r[bk] + [self.bank_w[bk]]
        tok = self.op("T", (lambda e, o=out_ap, i=in_ap, id_=ident:
                            e.transpose(o, i, id_)), waits=w, inc=True)
        self.bank_w[bk] = tok
        self.bank_r[bk] = []
        return tok

    def evac(self, ek, bks, fn, waits=()):
        if isinstance(bks, int):
            bks = [bks]
        w = list(waits) + [self.bank_w[bk] for bk in bks]
        tok = self.op(ek, fn, waits=w, inc=True)
        for bk in bks:
            self.bank_r[bk].append(tok)
        return tok

    def finish(self, out_tokens=()):
        snap = [(s, self.cnt[s]) for s in self.sem_names]
        wl = self._wait_list("Q", list(out_tokens) + snap)

        def cl(eng, wl=wl):
            for (h, v) in wl:
                eng.wait_ge(h, v)
        self.ops["Q"].append(cl)
        with self.nc.Block() as block:
            for ek, reg in (("T", block.tensor), ("V", block.vector),
                            ("S", block.scalar), ("G", block.gpsimd),
                            ("Q", block.sync)):
                ops = self.ops[ek]

                def runner(eng, ops=ops):
                    for fn in ops:
                        fn(eng)
                reg(runner)



class Env:
    def __init__(self, b: Bld, stack, cfg):
        nc = b.nc
        self.cfg = cfg
        D, T, H, FF, NE = cfg["D"], cfg["T"], cfg["H"], cfg["FF"], cfg["NE"]
        DC = D // 128
        DCT = DC * T
        TH = T // 2
        KC = T // 128
        VON = KC * H * 65
        WELS = DC * D
        FHW = min(2048, FF)
        FC = FF // 128
        r = max(DCT, VON, 5 * T + 2048, 2 * WELS + 256, min(16, FC) * TH,
                8 * FHW, 4 * TH + 2 * FHW + FC * 128, 8 * 2 * D, 16384)
        self.RSZ = r
        self.R0, self.R1, self.R2, self.R3 = 0, r, 2 * r, 2 * r + DCT
        self.R4 = 3 * r + DCT
        self.R4_els = max(4 * T, 4 * D, KC * 512, 2 * D + 3 * 1024 + 128, 2048)
        self.a1_els = self.R4 + self.R4_els
        self.a2_els = max(DCT, 2 * WELS,
                          8 * TH + 2 * DC * max(FF // 8, 128) + 4096,
                          8 * TH + 4 * D, 8192)
        self.A1 = stack.enter_context(nc.sbuf_tensor("A1", [128, self.a1_els], BF16))
        self.A2 = stack.enter_context(nc.sbuf_tensor("A2", [128, self.a2_els], BF16))
        sb = lambda name, shape, dt: stack.enter_context(nc.sbuf_tensor(name, shape, dt))
        self.ident = sb("c_ident", [128, 128], F32)
        self.identb = sb("c_identb", [128, 128], BF16)
        self.ones_row = sb("c_ones_row", [1, 1024], BF16)
        self.ones_row_f = sb("c_ones_row_f", [1, 128], F32)
        self.ones_col_bf = sb("c_ones_col_bf", [128, 1], BF16)
        self.ones_col_f = sb("c_ones_col_f", [128, 1], F32)
        self.stair = sb("c_stair", [128, 896], BF16)
        self.bias = sb("c_bias", [1, 2048], BF16)
        self.rows = sb("c_rows", [1, 2112], F32)
        self.rows_bf = sb("c_rows_bf", [1, 1024], BF16)
        self.invbc = sb("c_invbc", [64, 512], BF16)
        self.smalls = sb("c_smalls", [128, 64], F32)
        self.colbf = sb("c_colbf", [128, 16], BF16)
        self.rwt = sb("c_rwt", [128, DC * NE], BF16)
        self.ustrict = sb("c_ustrict", [128, 128], F32)
        self.iota = sb("c_iota", [128, T // 128], mybir.dt.int32)
        self.ifill = sb("c_ifill", [128, 8], mybir.dt.int32)
        self.rwtf = sb("c_rwtf", [128, DC * NE], F32)
        self.cw = sb("c_cw", [128, KC * NE], F32)
        self.locs = sb("c_locs", [128, 64], mybir.dt.int32)
        b.op("G", lambda e: e.memset(self.ident[:, :], 0.0))
        b.op("G", lambda e: e.affine_select(
            out=self.ident[:, :], in_=self.ident[:, :],
            compare_op=ALU.not_equal, fill=1.0,
            base=0, pattern=[[-1, 128]], channel_multiplier=1))
        b.op("G", lambda e: e.memset(self.identb[:, :], 0.0))
        b.op("G", lambda e: e.affine_select(
            out=self.identb[:, :], in_=self.identb[:, :],
            compare_op=ALU.not_equal, fill=1.0,
            base=0, pattern=[[-1, 128]], channel_multiplier=1))
        b.op("G", lambda e: e.memset(self.ones_row[:, :], 1.0))
        b.op("G", lambda e: e.memset(self.ones_row_f[:, :], 1.0))
        b.op("G", lambda e: e.memset(self.ones_col_bf[:, :], 1.0))
        b.op("G", lambda e: e.memset(self.ones_col_f[:, :], 1.0))
        b.op("G", lambda e: e.memset(self.stair[:, :], 1.0))
        b.op("G", lambda e: e.affine_select(
            out=self.stair[:, :], in_=self.stair[:, :],
            compare_op=ALU.is_ge, fill=0.0,
            base=-384, pattern=[[1, 896]], channel_multiplier=-1))
        b.op("G", lambda e: e.memset(self.ustrict[:, :], 1.0))
        b.op("G", lambda e: e.affine_select(
            out=self.ustrict[:, :], in_=self.ustrict[:, :],
            compare_op=ALU.is_ge, fill=0.0,
            base=-1, pattern=[[1, 128]], channel_multiplier=-1))
        b.op("G", lambda e: e.memset(self.ifill[:, :], T))
        b.op("G", lambda e: e.iota(self.iota[:, :], pattern=[[128, T // 128]],
                                   base=0, channel_multiplier=1))

    def a1(self, lo, n):
        assert 0 <= lo and lo + n <= self.a1_els, (lo, n, self.a1_els)
        return self.A1[:, lo:lo + n]

    def a1f(self, lo, n_f32):
        return self.a1(lo, 2 * n_f32).bitcast(F32)

    def a2(self, lo, n):
        assert 0 <= lo and lo + n <= self.a2_els, (lo, n, self.a2_els)
        return self.A2[:, lo:lo + n]

    def a2f(self, lo, n_f32):
        return self.a2(lo, 2 * n_f32).bitcast(F32)



def layernorm(b: Bld, ev: Env, x_dram, xlnf_dram=None, xtm_dram=None):
    cfg = ev.cfg
    D, T = cfg["D"], cfg["T"]
    DC = D // 128
    DCT = DC * T
    xbf = ev.a1(ev.R0, DCT)
    xf = ev.a1f(ev.R1, 2 * T)
    sqbf = ev.a1(ev.R1 + 4 * T, T)
    bc = ev.a1(ev.R1 + 5 * T, 1024)
    xln = ev.a1(ev.R2, DCT)
    tile_list = list(tiles(T, 512))
    nt = len(tile_list)
    assert nt <= 4
    inv_d = 1.0 / D

    pre = []
    for bk in range(8):
        pre += b.bank_r[bk] + [b.bank_w[bk]]
    buf_free = [None, None]
    last_mm = None
    for dc in range(DC):
        buf = dc % 2
        ld = b.dma("Q", xf[:, buf * T:(buf + 1) * T],
                   x_dram[dc * 128:(dc + 1) * 128, :], waits=[buf_free[buf]])
        cast = b.op("V", (lambda e, dc=dc, buf=buf: e.tensor_copy(
            xbf[:, dc * T:(dc + 1) * T], xf[:, buf * T:(buf + 1) * T])), waits=[ld])
        sq = b.op("S", (lambda e, dc=dc: e.activation(
            sqbf[:, :], xbf[:, dc * T:(dc + 1) * T], AF.Square)),
            waits=[cast, last_mm])
        buf_free[buf] = cast
        mm_w = [cast, sq] + (pre if dc == 0 else [])
        for ti, (o, w) in enumerate(tile_list):
            pa = b.bank(ti, 0, w, np_=1)
            xa_ = xbf[:, dc * T + o: dc * T + o + w]
            b.op("T", (lambda e, pa=pa, xa_=xa_, st=(dc == 0), sp=(dc == DC - 1):
                       e.matmul(pa, ev.ones_col_bf[:, :], xa_, start=st, stop=sp)),
                 waits=mm_w if ti == 0 else ())
            pb = b.bank(4 + ti, 0, w, np_=1)
            sa = sqbf[:, o:o + w]
            last_mm = b.op("T", (lambda e, pb=pb, sa=sa, st=(dc == 0),
                                 sp=(dc == DC - 1):
                                 e.matmul(pb, ev.ones_col_bf[:, :], sa,
                                          start=st, stop=sp)),
                           inc=(ti == nt - 1))
    for bk in range(8):
        b.bank_w[bk] = last_mm
        b.bank_r[bk] = []

    rows, rows_bf = ev.rows, ev.rows_bf
    prev_asm_st = [[] for _ in range(4)]
    asmb_st = [[], []]
    slot_rd = [list(buf_free), list(buf_free)]
    slot_st = [list(buf_free), list(buf_free)]
    for ti, (o, w) in enumerate(tile_list):
        s0 = rows[0:1, 0:w]
        s1 = rows[0:1, 512:512 + w]
        s2 = rows[0:1, 1024:1024 + w]
        rbf = rows_bf[0:1, 0:w]
        mbf = rows_bf[0:1, 512:512 + w]
        b.evac("V", ti, (lambda e, ti=ti, w=w, s0=s0: e.tensor_scalar_mul(
            s0, b.bank(ti, 0, w, np_=1), inv_d)))
        b.evac("V", 4 + ti, (lambda e, ti=ti, w=w, s1=s1: e.tensor_scalar_mul(
            s1, b.bank(4 + ti, 0, w, np_=1), inv_d)))
        b.op("V", lambda e, s0=s0, s2=s2: e.tensor_mul(s2, s0, s0))
        b.op("V", lambda e, s1=s1, s2=s2: e.tensor_sub(s1, s1, s2))
        tve = b.op("V", lambda e, s1=s1: e.tensor_scalar_add(s1, s1, LN_EPS))
        tsq = b.op("S", (lambda e, s1=s1, s2=s2: e.activation(s2, s1, AF.Sqrt)),
                   waits=[tve])
        trec = b.op("V", (lambda e, s1=s1, s2=s2: e.reciprocal(s1, s2)),
                    waits=[tsq])
        bk_r, bk_m = ti, 4 + ti
        if xlnf_dram is None:
            b.op("V", lambda e, s0=s0, s1=s1, s2=s2: e.tensor_mul(s2, s0, s1))
            b.op("V", lambda e, s1=s1, rbf=rbf: e.tensor_copy(rbf, s1))
            cv = b.op("V", lambda e, s2=s2, mbf=mbf: e.tensor_copy(mbf, s2))
            b.mm_group(bk_r, b.bank(bk_r, 0, w),
                       [(ev.ones_row[0:1, 0:128], rbf)], waits=[cv])
            b.mm_group(bk_m, b.bank(bk_m, 0, w),
                       [(ev.ones_row[0:1, 0:128], mbf)], waits=[cv])
            b.evac("V", bk_r, (lambda e, w=w, bk_r=bk_r: e.tensor_copy(
                bc[:, 0:w], b.bank(bk_r, 0, w))))
            b.evac("V", bk_m, (lambda e, w=w, bk_m=bk_m: e.tensor_copy(
                bc[:, 512:512 + w], b.bank(bk_m, 0, w))))
            for dc in range(DC):
                xo = xln[:, dc * T + o: dc * T + o + w]
                xi = xbf[:, dc * T + o: dc * T + o + w]
                b.op("V", (lambda e, xo=xo, xi=xi, w=w: e.tensor_mul(
                    xo, xi, bc[:, 0:w])))
                b.op("V", (lambda e, xo=xo, w=w: e.tensor_sub(
                    xo, xo, bc[:, 512:512 + w])))
        else:
            b.mm_group(bk_r, b.bank(bk_r, 0, w),
                       [(ev.ones_row_f[0:1, 0:128], s1)], waits=[trec])
            b.mm_group(bk_m, b.bank(bk_m, 0, w),
                       [(ev.ones_row_f[0:1, 0:128], s0)], waits=[trec])
            xft = [ev.a1f(ev.R1, 512), ev.a1f(ev.R1 + 1024, 512)]
            xlt = [ev.a1f(ev.R1 + 2048, 512), ev.a1f(ev.R1 + 3072, 512)]
            asm = ev.a1f(ev.R1 + 4096, 4096)
            bcf = ev.a1f(ev.R1 + 12288, 1024)
            b.evac("V", bk_r, (lambda e, w=w, bk_r=bk_r: e.tensor_copy(
                bcf[:, 0:w], b.bank(bk_r, 0, w))))
            cvb = b.evac("V", bk_m, (lambda e, w=w, bk_m=bk_m: e.tensor_copy(
                bcf[:, 512:512 + w], b.bank(bk_m, 0, w))))
            slot_rd[0] = slot_rd[0] + [cvb]
            slot_rd[1] = slot_rd[1] + [cvb]
            asm_toks = [[] for _ in range(4)]
            for dc in range(DC):
                par = dc % 2
                ldx = b.dma("Q", xft[par][:, 0:w],
                            x_dram[dc * 128:(dc + 1) * 128, o:o + w],
                            waits=slot_rd[par])
                b.op("V", (lambda e, par=par, w=w:
                           e.tensor_sub(xlt[par][:, 0:w], xft[par][:, 0:w],
                                        bcf[:, 512:512 + w])),
                     waits=[ldx] + slot_st[par])
                t2 = b.op("V", (lambda e, par=par, w=w:
                                e.tensor_mul(xlt[par][:, 0:w], xlt[par][:, 0:w],
                                             bcf[:, 0:w])))
                slot_rd[par] = [t2]
                xo = xln[:, dc * T + o: dc * T + o + w]
                b.op("V", (lambda e, xo=xo, par=par, w=w: e.tensor_copy(
                    xo, xlt[par][:, 0:w])))
                slot_st[par] = [b.dma(
                    "Q", xlnf_dram[dc * 128:(dc + 1) * 128, o:o + w],
                    xlt[par][:, 0:w], waits=[t2])]
                if xtm_dram is not None:
                    for j in range(w // 128):
                        bt = bk_r if j % 2 == 0 else bk_m
                        src = xlt[par][:, j * 128: j * 128 + 128]
                        b.transpose_to(bt, b.bank(bt, 0, 128), src,
                                       ev.ident[:, :],
                                       waits=[t2] + prev_asm_st[j])
                        dsta = asm[:, j * 1024 + dc * 128:
                                   j * 1024 + dc * 128 + 128]
                        asm_toks[j].append(b.evac("V", bt, (
                            lambda e, dsta=dsta, bt=bt: e.tensor_copy(
                                dsta, b.bank(bt, 0, 128)))))
                        slot_rd[par] = slot_rd[par] + [asm_toks[j][-1]]
            if xtm_dram is not None:
                asmb = ev.a1(ev.R1 + 14336, 2048)
                for j in range(w // 128):
                    jp = j % 2
                    cvt = b.op("V", (lambda e, j=j, jp=jp: e.tensor_copy(
                        asmb[:, jp * 1024: jp * 1024 + D],
                        asm[:, j * 1024: j * 1024 + D])),
                        waits=asm_toks[j] + asmb_st[jp])
                    prev_asm_st[j] = [cvt]
                    asmb_st[jp] = [b.dma(
                        "Q", xtm_dram[o + j * 128: o + j * 128 + 128, :],
                        asmb[:, jp * 1024: jp * 1024 + D], waits=[cvt])]



def pre_transpose(b: Bld, ev: Env, x_tm_dram, x_fm_dram):
    cfg = ev.cfg
    D, T = cfg["D"], cfg["T"]
    DC = D // 128
    xin = ev.a1f(ev.R0, 2 * D)
    xtr = ev.a1f(ev.R1, 2 * DC * 128)
    buf_free = [None, None]
    out_free = [None, None]
    for tc in range(T // 128):
        buf = tc % 2
        ld = b.dma("Q", xin[:, buf * D:(buf + 1) * D],
                   x_tm_dram[tc * 128:(tc + 1) * 128, :], waits=[buf_free[buf]])
        cp_toks = []
        for dc in range(DC):
            bk = 4 + (dc % 2)
            src = xin[:, buf * D + dc * 128: buf * D + dc * 128 + 128]
            dst = xtr[:, buf * DC * 128 + dc * 128: buf * DC * 128 + dc * 128 + 128]
            b.transpose_to(bk, b.bank(bk, 0, 128), src, ev.ident[:, :], waits=[ld])
            cp = b.evac("V", bk, (lambda e, dst=dst, bk=bk: e.tensor_copy(
                dst, b.bank(bk, 0, 128))), waits=[out_free[buf]])
            cp_toks.append(cp)
        st = b.dma("Q",
                   x_fm_dram[:, tc * 128:(tc + 1) * 128].rearrange(
                       "(c p) n -> p c n", p=128),
                   xtr[:, buf * DC * 128:(buf + 1) * DC * 128].rearrange(
                       "p (c n) -> p c n", n=128),
                   waits=cp_toks)
        buf_free[buf] = cp_toks[-1]
        out_free[buf] = st



def proj_fm(b: Bld, ev: Env, xln, wsl, out, scale=1.0, bias_row=None, wait0=()):
    cfg = ev.cfg
    D, T = cfg["D"], cfg["T"]
    DC = D // 128
    rot = [0]
    last = []
    for ic in range(DC):
        for (o, w) in tiles(T, 512):
            bk = rot[0] % 4
            rot[0] += 1
            pairs = [(wsl(dc, ic), xln[:, dc * T + o: dc * T + o + w])
                     for dc in range(DC)]
            if bias_row is not None:
                pairs.append((bias_row[0:1, ic * 128: ic * 128 + 128],
                              ev.ones_row[0:1, 0:w]))
            b.mm_group(bk, b.bank(bk, 0, w), pairs, waits=wait0)
            dst = out[:, ic * T + o: ic * T + o + w]
            if scale != 1.0:
                last.append(b.evac("V", bk, (lambda e, dst=dst, bk=bk, w=w:
                                             e.tensor_scalar_mul(
                                                 dst, b.bank(bk, 0, w), scale))))
            else:
                last.append(b.evac("S", bk, (lambda e, dst=dst, bk=bk, w=w:
                                             e.activation(dst, b.bank(bk, 0, w),
                                                          AF.Copy))))
    return last


def proj_tm_vones(b: Bld, ev: Env, xln, wsl_r, v_ones, bias_row=None, wait0=()):
    cfg = ev.cfg
    D, T, H, HD = cfg["D"], cfg["T"], cfg["H"], cfg["HD"]
    assert HD == 64
    DC = D // 128
    rot = [0]
    last = []
    for tc in range(T // 128):
        for (io, iw) in tiles(D, 512):
            bk = rot[0] % 4
            rot[0] += 1
            pairs = [(xln[:, dc * T + tc * 128: dc * T + tc * 128 + 128],
                      wsl_r(dc, io, iw)) for dc in range(DC)]
            if bias_row is not None:
                pairs.append((ev.ones_row[0:1, 0:128],
                              bias_row[0:1, io: io + iw]))
            b.mm_group(bk, b.bank(bk, 0, iw), pairs, waits=wait0)
            nh = iw // 64
            base = tc * H * 65 + (io // 64) * 65
            dst = v_ones[:, base: base + nh * 65].rearrange(
                "p (h c) -> p h c", h=nh)[:, :, 0:64]
            last.append(b.evac("V", bk, (lambda e, dst=dst, bk=bk, iw=iw, nh=nh:
                                         e.tensor_copy(dst, b.bank(bk, 0, iw).rearrange(
                                             "p (h d) -> p h d", h=nh)))))
    return last



def attention_core(b: Bld, ev: Env, sc_lh, sc_rh, v_ones, y, causal, ptil2):
    cfg = ev.cfg
    T, H, HD = cfg["T"], cfg["H"], cfg["HD"]
    KC = T // 128
    ptils = [ev.a1(ev.R4, KC * 512), ptil2]
    inv_f = ev.rows[0:1, 1536:2048]
    inv_bf = ev.rows_bf[0:1, 0:512]
    items = [(h, go, gw) for h in range(H) for (go, gw) in tiles(T, 512)]
    exp_q = {}
    av_toks = {}

    def stage_a(i):
        h, go, gw = items[i]
        ptil = ptils[i % 2]
        nkc = (go + gw) // 128 if causal else KC
        war = [av_toks.get(i - 2)]
        toks = []
        for kc in range(nkc):
            bk = kc % 6
            b.mm_group(bk, b.bank(bk, 0, gw),
                       [(sc_lh(h, kc), sc_rh(h, go, gw))])
            pdst = ptil[:, kc * 512: kc * 512 + gw]
            tok = b.evac("S", bk, (lambda e, pdst=pdst, bk=bk, gw=gw:
                                   e.activation(pdst, b.bank(bk, 0, gw),
                                                AF.Exp)),
                         waits=war if kc == 0 else ())
            if causal and kc * 128 + 127 >= go:
                off = 384 + go - kc * 128
                assert 0 <= off and off + gw <= 896, (off, gw)
                msl = ev.stair[:, off:off + gw]
                tok = b.op("V", (lambda e, pdst=pdst, msl=msl:
                                 e.tensor_mul(pdst, pdst, msl)),
                           waits=[tok])
            toks.append(tok)
        exp_q[i] = toks

    def stage_b(i):
        h, go, gw = items[i]
        ptil = ptils[i % 2]
        nkc = (go + gw) // 128 if causal else KC
        pairs = []
        for kc in range(nkc):
            vsl = v_ones[:, (kc * H + h) * 65: (kc * H + h) * 65 + 65]
            pairs.append((vsl, ptil[:, kc * 512: kc * 512 + gw]))
        av_toks[i] = b.mm_group(6, b.bank(6, 0, gw, np_=65), pairs,
                                waits=exp_q.pop(i))
        b.evac("V", 6, (lambda e, gw=gw: e.reciprocal(
            inv_f[0:1, 0:gw], b.bank(6, 0, gw, np_=1, p0=64))))
        cv = b.op("V", lambda e, gw=gw: e.tensor_copy(inv_bf[0:1, 0:gw],
                                                      inv_f[0:1, 0:gw]))
        b.mm_group(7, b.bank(7, 0, gw, np_=64),
                   [(ev.ones_row[0:1, 0:64], inv_bf[0:1, 0:gw])], waits=[cv])
        b.evac("V", 7, (lambda e, gw=gw: e.tensor_copy(
            ev.invbc[0:64, 0:gw], b.bank(7, 0, gw, np_=64))))
        dc = (h * HD) // 128
        p0 = (h * HD) % 128
        ydst = y[p0:p0 + 64, dc * T + go: dc * T + go + gw]
        b.evac("V", 6, (lambda e, ydst=ydst, gw=gw: e.tensor_mul(
            ydst, b.bank(6, 0, gw, np_=64), ev.invbc[0:64, 0:gw])))

    stage_a(0)
    for i in range(1, len(items)):
        stage_a(i)
        stage_b(i - 1)
    stage_b(len(items) - 1)



def out_proj_residual(b: Bld, ev: Env, y, wsl, x_in, x_out, bias_row=None,
                      wait0=()):
    cfg = ev.cfg
    D, T = cfg["D"], cfg["T"]
    DC = D // 128
    xi = ev.a1f(ev.R4, T)
    xo = ev.a1f(ev.R4 + 2 * T, T)
    st_tok = [None]
    tt_last = [None]
    rot = [0]
    for dco in range(DC):
        ld = b.dma("Q", xi[:, :], x_in[dco * 128:(dco + 1) * 128, :],
                   waits=[tt_last[0]])
        tts = []
        for (o, w) in tiles(T, 512):
            bk = rot[0] % 4
            rot[0] += 1
            pairs = [(wsl(dc, dco), y[:, dc * T + o: dc * T + o + w])
                     for dc in range(DC)]
            if bias_row is not None:
                pairs.append((bias_row[0:1, dco * 128: dco * 128 + 128],
                              ev.ones_row[0:1, 0:w]))
            b.mm_group(bk, b.bank(bk, 0, w), pairs, waits=wait0)
            dst = xo[:, o:o + w]
            src1 = xi[:, o:o + w]
            tts.append(b.evac("V", bk, (lambda e, dst=dst, src1=src1, bk=bk, w=w:
                                        e.tensor_add(dst, b.bank(bk, 0, w), src1)),
                              waits=[ld, st_tok[0]]))
        tt_last[0] = tts[-1]
        st_tok[0] = b.dma("Q", x_out[dco * 128:(dco + 1) * 128, :], xo[:, :],
                          waits=tts)
    return st_tok[0]



def block_l0attn(b: Bld, ev: Env, x_in, x_out, W):
    cfg = ev.cfg
    D, T, H, HD = cfg["D"], cfg["T"], cfg["H"], cfg["HD"]
    DC = D // 128
    DCT = DC * T
    KC = T // 128
    WELS = DC * D
    wA = ev.a2(0, WELS)
    wB = ev.a2(WELS, WELS)
    xln = ev.a1(ev.R2, DCT)
    q = ev.a1(ev.R0, DCT)
    k = ev.a1(ev.R3, DCT)
    v_ones = ev.a1(ev.R1, KC * H * 65)
    y = ev.a1(ev.R2, DCT)

    def wload(dst, nm, waits=()):
        return b.dma("G", dst.rearrange("p (c n) -> p c n", c=DC),
                     W[nm].rearrange("(c p) n -> p c n", p=128), waits=waits)

    ldq = wload(wA, "Wq")
    ldk = wload(wB, "Wk")
    layernorm(b, ev, x_in)
    b.barrier()
    wsl = lambda Wt: (lambda dc, ic: Wt[:, dc * D + ic * 128: dc * D + ic * 128 + 128])
    q_toks = proj_fm(b, ev, xln, wsl(wA), q, scale=1.0 / float(np.sqrt(HD)),
                     wait0=[ldq])
    ldv = wload(wA, "Wv", waits=[q_toks[-1]])
    k_toks = proj_fm(b, ev, xln, wsl(wB), k, wait0=[ldk])
    ldo = wload(wB, "Wo", waits=[k_toks[-1]])
    von_ms = b.op("G", lambda e: e.memset(v_ones[:, :], 1.0))
    wslA_r = lambda dc, io, iw: wA[:, dc * D + io: dc * D + io + iw]
    proj_tm_vones(b, ev, xln, wslA_r, v_ones, wait0=[ldv, von_ms])
    b.barrier()

    hpc = 128 // HD
    sc_lh = lambda h, kc: k[(h % hpc) * HD:(h % hpc) * HD + HD,
                            (h // hpc) * T + kc * 128:(h // hpc) * T + kc * 128 + 128]
    sc_rh = lambda h, go, gw: q[(h % hpc) * HD:(h % hpc) * HD + HD,
                                (h // hpc) * T + go:(h // hpc) * T + go + gw]
    if 2 * WELS + KC * 512 <= ev.a2_els:
        ptil2 = ev.a2(2 * WELS, KC * 512)
    else:
        assert KC * 512 <= WELS
        ptil2 = ev.a2(0, KC * 512)
    attention_core(b, ev, sc_lh, sc_rh, v_ones, y, True, ptil2)
    b.barrier()
    wslo = lambda dc, dco: wB[:, dc * D + dco * 128: dc * D + dco * 128 + 128]
    st = out_proj_residual(b, ev, y, wslo, x_in, x_out, wait0=[ldo])
    if getattr(ev, "l0f_W", None) is not None:
        sc_stage = ev.a2f(ev.a2_els - 4096, 2048)
        t1 = tern_scale_pass(b, ev, ev.l0f_W["W1"], cfg["FF"], D, None,
                             thrcol=0, bank=6, eng="V", stage=sc_stage,
                             cw_=1024)
        tern_scale_pass(b, ev, ev.l0f_W["W2"], D, cfg["FF"], None,
                        thrcol=1, bank=7, war=t1, eng="V", stage=sc_stage,
                        cw_=1024)
    b.barrier()
    return st



def tern_scale_pass(b: Bld, ev: Env, Wdram, nrows, ncols, stage_off,
                    thrcol, bank=6, war=None, eng="V", stage=None, cw_=1024):
    cw_ = min(cw_, ncols)
    if stage is None:
        stage = ev.a1f(stage_off, 2 * cw_)
    asum = ev.smalls[:, 60:61]
    tmp = ev.smalls[:, 61:62]
    thr = ev.smalls[:, thrcol:thrcol + 1]
    zz = b.op(eng, lambda e: e.memset(asum, 0.0))
    buf_free = [war, war]
    ad = zz
    i = 0
    for g in range(nrows // 128):
        for (co, cn) in tiles(ncols, cw_):
            buf = i % 2
            i += 1
            ld = b.dma("Q", stage[:, buf * cw_: buf * cw_ + cn],
                       Wdram[g * 128:(g + 1) * 128, co:co + cn],
                       waits=[buf_free[buf]])
            rd = b.op(eng, (lambda e, buf=buf, cn=cn: e.reduce_sum(
                tmp, stage[:, buf * cw_: buf * cw_ + cn], axis=AX.X,
                apply_absolute_value=True)), waits=[ld, zz])
            ad = b.op(eng, lambda e: e.tensor_add(asum, asum, tmp))
            buf_free[buf] = rd
    b.mm_group(bank, b.bank(bank, 0, 1, np_=1),
               [(asum, ev.ones_col_f[:, :])], waits=[ad])
    srow = ev.rows[0:1, 2048:2049]
    b.evac(eng, bank, (lambda e: e.tensor_scalar(
        srow, b.bank(bank, 0, 1, np_=1), 1.0 / float(nrows * ncols), 1e-8,
        ALU.mult, ALU.max)))
    hrow = ev.rows[0:1, 2049:2050]
    cv = b.op(eng, lambda e: e.tensor_scalar_mul(hrow, srow, 0.5))
    b.mm_group(bank, b.bank(bank, 0, 1),
               [(ev.ones_row_f[0:1, 0:128], hrow)], waits=[cv])
    return b.evac(eng, bank, (lambda e: e.tensor_copy(thr, b.bank(bank, 0, 1))))


def ternarize(b: Bld, ev: Env, src_f32, thrcol, sgn, ab, tern_f, waits=()):
    t1 = b.op("S", (lambda e: e.activation(sgn, src_f32, AF.Sign)), waits=waits)
    t2 = b.op("S", (lambda e: e.activation(ab, src_f32, AF.Abs)))
    thr = ev.smalls[:, thrcol:thrcol + 1]
    b.op("V", (lambda e: e.tensor_scalar(ab, ab, thr, None, ALU.is_gt)),
         waits=[t2])
    return b.op("V", (lambda e: e.tensor_mul(tern_f, ab, sgn)), waits=[t1])



def block_l0ffn(b: Bld, ev: Env, x_in, x_out, W):
    cfg = ev.cfg
    D, T, FF = cfg["D"], cfg["T"], cfg["FF"]
    DC = D // 128
    DCT = DC * T
    TH = T // 2
    FC = FF // 128
    FHW = min(1024, FF)
    GR = DC * 128
    r = ev.RSZ
    xln = ev.a1(ev.R2, DCT)
    w1c = ev.w1c_dram
    w2c = ev.w2c_dram

    def h_sl(fc, o, w):
        if fc < 16:
            return ev.a1(ev.R1 + fc * TH + o, w)
        if fc < 24:
            return ev.a1(ev.R4 + (fc - 16) * TH + o, w)
        return ev.a2((fc - 24) * TH + o, w)

    if getattr(ev, "l0f_W", None) is None:
        t1 = tern_scale_pass(b, ev, W["W1"], FF, D, ev.R3, thrcol=0)
        tern_scale_pass(b, ev, W["W2"], D, FF, ev.R3, thrcol=1, bank=7,
                        war=t1)
    layernorm(b, ev, x_in)
    b.barrier()

    b1row = W["b1"].rearrange("(a d) -> a d", a=1)
    b2row = W["b2"].rearrange("(a d) -> a d", a=1)

    stage = ev.a1f(ev.R3, 2 * D)
    sgn = ev.a1f(ev.R3 + 4 * D, D)
    ab = ev.a1f(ev.R3 + 6 * D, D)
    ternf = ev.a1f(ev.R3 + 8 * D, D)
    gbuf = [ev.a1(ev.R0, GR), ev.a1(ev.R0 + GR, GR)]
    buf_free = [None, None]
    gb_free = [None, None]
    w1_cache = []
    prev_tk = [None]
    for fc in range(FC):
        buf = fc % 2
        ld = b.dma("Q", stage[:, buf * D:(buf + 1) * D],
                   W["W1"][fc * 128:(fc + 1) * 128, :], waits=[buf_free[buf]])
        tk = ternarize(b, ev, stage[:, buf * D:(buf + 1) * D], 0,
                       sgn, ab, ternf, waits=[ld, prev_tk[0]])
        prev_tk[0] = tk
        buf_free[buf] = tk
        cps = []
        for dc in range(DC):
            bk = 4 + (dc % 2)
            b.transpose_to(bk, b.bank(bk, 0, 128),
                           ternf[:, dc * 128: dc * 128 + 128],
                           ev.ident[:, :], waits=[tk])
            dst = gbuf[buf][:, dc * 128: dc * 128 + 128]
            cps.append(b.evac("V", bk, (lambda e, dst=dst, bk=bk:
                                        e.tensor_copy(dst, b.bank(bk, 0, 128))),
                              waits=[gb_free[buf]]))
        w1_cache.append(b.dma("Q", w1c[fc], gbuf[buf], waits=cps))
        gb_free[buf] = w1_cache[-1]

    ring = [ev.a1(ev.R3 + 10 * D + i * GR, GR) for i in range(3)]

    def h_steps(th, war):
        rot = [0]
        b1_half = [0]
        b1_tok = [b.dma("G", ev.bias[0:1, 0:min(FF, 2048)],
                        b1row[0:1, 0:min(FF, 2048)], waits=war)]
        slot_mm = [list(war), list(war), list(war)]
        h_toks = [None]
        for fc in range(FC):
            sl = fc % 3
            ld = b.dma("Q", ring[sl], w1c[fc],
                       waits=slot_mm[sl] + [w1_cache[fc]])
            if fc * 128 >= (b1_half[0] + 1) * 2048:
                b1_half[0] += 1
                b1_tok[0] = b.dma(
                    "G", ev.bias[0:1, 0:2048],
                    b1row[0:1, b1_half[0] * 2048:(b1_half[0] + 1) * 2048],
                    waits=[h_toks[-1]])
            gts = []
            for (o, w) in tiles(TH, 512):
                bk = rot[0] % 4
                rot[0] += 1
                pairs = [(ring[sl][:, dc * 128: dc * 128 + 128],
                          xln[:, dc * T + th * TH + o: dc * T + th * TH + o + w])
                         for dc in range(DC)]
                boff = fc * 128 - b1_half[0] * 2048
                pairs.append((ev.bias[0:1, boff:boff + 128],
                              ev.ones_row[0:1, 0:w]))
                gts.append(b.mm_group(bk, b.bank(bk, 0, w), pairs,
                                      waits=[ld, b1_tok[0]]))
                dst = h_sl(fc, o, w)
                h_toks.append(b.evac("S", bk, (lambda e, dst=dst, bk=bk, w=w:
                                               e.activation(dst, b.bank(bk, 0, w),
                                                            AF.Gelu))))
            slot_mm[sl] = gts
            yield h_toks[-1]

    w2stage = ev.a1f(ev.R0, 2 * FHW)
    ternf2 = ev.a1f(ev.R0 + 4 * FHW, FHW)
    sgn2 = ev.a1f(ev.R0 + 6 * FHW, FHW)
    ab2 = ev.a1f(ev.R3, FHW)
    strip = ev.a2(8 * TH, FC * 128)

    def w2_steps(war):
        buf2 = [list(war), list(war)]
        prev2 = [prev_tk[0]] + war
        strip_free = [None]
        cache = []
        for dcg in range(DC):
            w2_done = []
            for fh, fhw in tiles(FF, FHW):
                pbuf = (fh // FHW) % 2
                ld = b.dma("Q", w2stage[:, pbuf * FHW: pbuf * FHW + fhw],
                           W["W2"][dcg * 128:(dcg + 1) * 128, fh:fh + fhw],
                           waits=buf2[pbuf] + prev2)
                tk = ternarize(b, ev, w2stage[:, pbuf * FHW:pbuf * FHW + fhw],
                               1, sgn2[:, 0:fhw], ab2[:, 0:fhw],
                               ternf2[:, 0:fhw], waits=[ld] + prev2)
                prev2 = [tk]
                buf2[pbuf] = [tk]
                for j in range(fhw // 128):
                    fc = (fh + j * 128) // 128
                    bk = 4 + (j % 2)
                    b.transpose_to(bk, b.bank(bk, 0, 128),
                                   ternf2[:, j * 128: j * 128 + 128],
                                   ev.ident[:, :], waits=[tk])
                    dst = strip[:, fc * 128: fc * 128 + 128]
                    w2_done.append(b.evac("V", bk, (lambda e, dst=dst, bk=bk:
                                                    e.tensor_copy(dst, b.bank(bk, 0, 128))),
                                          waits=[strip_free[0]]))
            strip_free[0] = b.dma(
                "Q", w2c[:, dcg * 128:(dcg + 1) * 128].rearrange(
                    "(c p) n -> p c n", p=128),
                strip.rearrange("p (c n) -> p c n", n=128), waits=w2_done)
            cache.append(strip_free[0])
            yield cache

    g_w2 = w2_steps(w1_cache[-2:])
    g_h0 = h_steps(0, [])
    w2_cache = []
    h_last = None
    hs_per = max(1, FC // DC)
    try:
        for dcg in range(DC):
            w2_cache = next(g_w2)
            for _ in range(hs_per):
                h_last = next(g_h0, h_last)
    except StopIteration:
        pass
    for t in g_h0:
        h_last = t

    st_tok = [None]
    dh_list = list(tiles(D, 512))
    for th in range(2):
        if th == 1:
            gen = h_steps(1, [bar_tok[0]] if False else [])
            for t in gen:
                h_last = t
        ldb2 = b.dma("G", ev.bias[0:1, 0:D], b2row, waits=[h_last])
        w2th = ev.a1(ev.R0, FC * 512)
        xi = ev.a1f(ev.R3, TH)
        xo = ev.a1f(ev.R3 + 2 * TH, TH)
        last_dh_mm = [h_last]
        tt_last = [h_last]
        for (dho, dhw) in dh_list:
            ldh = []
            for fg in range(0, FC, 8):
                fgn = min(8, FC - fg)
                ldh.append(b.dma(
                    "Q",
                    w2th[:, fg * dhw: (fg + fgn) * dhw].rearrange(
                        "p (c n) -> p c n", c=fgn),
                    w2c[fg * 128:(fg + fgn) * 128, dho:dho + dhw].rearrange(
                        "(c p) n -> p c n", p=128),
                    waits=[last_dh_mm[0]] + w2_cache))
            for dcg in range(dho // 128, (dho + dhw) // 128):
                jj = dcg - dho // 128
                ld_x = b.dma("Q", xi[:, 0:TH],
                             x_in[dcg * 128:(dcg + 1) * 128,
                                  th * TH:(th + 1) * TH],
                             waits=[tt_last[0]])
                tts = []
                for (o, w) in tiles(TH, 512):
                    bk = (dcg * 2 + o // 512) % 4
                    pairs = [(w2th[:, fc * dhw + jj * 128:
                                   fc * dhw + jj * 128 + 128], h_sl(fc, o, w))
                             for fc in range(FC)]
                    pairs.append((ev.bias[0:1, dcg * 128: dcg * 128 + 128],
                                  ev.ones_row[0:1, 0:w]))
                    gt = b.mm_group(bk, b.bank(bk, 0, w), pairs,
                                    waits=ldh + [ldb2])
                    last_dh_mm[0] = gt
                    dst = xo[:, o:o + w]
                    src1 = xi[:, o:o + w]
                    tts.append(b.evac("V", bk, (lambda e, dst=dst, src1=src1,
                                                bk=bk, w=w: e.tensor_add(
                        dst, b.bank(bk, 0, w), src1)), waits=[ld_x, st_tok[0]]))
                tt_last[0] = tts[-1]
                st_tok[0] = b.dma("Q", x_out[dcg * 128:(dcg + 1) * 128,
                                             th * TH:(th + 1) * TH],
                                  xo[:, 0:TH], waits=tts)
        b.barrier()
    return st_tok[0]



def block_l1attn(b: Bld, ev: Env, x_in, x_out, W):
    cfg = ev.cfg
    D, T, H, HD, R = cfg["D"], cfg["T"], cfg["H"], cfg["HD"], cfg["R"]
    assert HD == 64
    DC = D // 128
    DCT = DC * T
    KC = T // 128
    WELS = DC * D
    HH = H // 2
    xln = ev.a1(ev.R2, DCT)
    qk = ev.a1(ev.R0, DCT)
    v_ones = ev.a1(ev.R0, KC * H * 65)
    zq = ev.a1(ev.R3, HH * T)
    zk = ev.a2(0, HH * T)
    y = ev.a1(ev.R2, DCT)
    wA = ev.a1(ev.R1, WELS)
    wB = ev.a1(ev.R1 + WELS, WELS)
    tiny = ev.R1 + 2 * WELS
    vt_bf = ev.a1(tiny, 64)
    ut_bf = ev.a1(tiny + 64, 64)
    vut2 = ev.a1(tiny + 128, 64)
    v2 = ev.a1(tiny + 192, 32)
    v2q = ev.a1(tiny + 224, 32)
    stage = ev.a1f(ev.R4, 2 * D)
    gran_bf = ev.a1(ev.R4, 2 * D)

    layernorm(b, ev, x_in)
    b.barrier()

    pads = ev.a1f(ev.R4, 128)
    prev_ld, prev_tr = None, None
    for (nm, dst) in (("U", ut_bf), ("V", vt_bf)):
        ms = b.op("G", (lambda e: e.memset(pads[:, :], 0.0)),
                  waits=[prev_ld, prev_tr])
        ld = b.dma("Q", pads[0:HD, 0:R], W[nm], waits=[ms])
        tr = b.transpose_to(6, b.bank(6, 0, 128), pads[:, :], ev.ident[:, :],
                            waits=[ld, ms])
        cp = b.evac("V", 6, (lambda e, dst=dst: e.tensor_copy(
            dst[0:R, 0:HD], b.bank(6, 0, HD, np_=R))))
        prev_ld, prev_tr = ld, tr
    b.mm_group(7, b.bank(7, 0, HD, np_=HD),
               [(vt_bf[0:R, 0:HD], ut_bf[0:R, 0:HD])], waits=[cp])
    b.evac("V", 7, (lambda e: e.tensor_copy(vut2[0:64, :],
                                            b.bank(7, 0, HD, np_=HD))))
    b.evac("V", 7, (lambda e: e.tensor_copy(vut2[64:128, :],
                                            b.bank(7, 0, HD, np_=HD))))
    ldv2a = b.dma("G", v2[0:64, 0:R], W["V"])
    ldv2b = b.dma("G", v2[64:128, 0:R], W["V"])
    b.op("V", (lambda e: e.tensor_scalar_mul(v2q[:, :], v2[:, :],
                                             float(R) ** -0.5)),
         waits=[ldv2a, ldv2b])
    b.barrier()

    def wt_prep(nm, slot, war=None):
        toks = []
        buf_free = [war, war]
        for g in range(DC):
            buf = g % 2
            ld = b.dma("Q", stage[:, buf * D:(buf + 1) * D],
                       W[nm][g * 128:(g + 1) * 128, :], waits=[buf_free[buf]])
            for dc in range(DC):
                bk = 4 + (dc % 2)
                b.transpose_to(bk, b.bank(bk, 0, 128),
                               stage[:, buf * D + dc * 128: buf * D + dc * 128 + 128],
                               ev.ident[:, :], waits=[ld])
                dst = slot[:, dc * D + g * 128: dc * D + g * 128 + 128]
                toks.append(b.evac("V", bk, (lambda e, dst=dst, bk=bk:
                                             e.tensor_copy(dst, b.bank(bk, 0, 128)))))
            buf_free[buf] = toks[-1]
        return toks

    def z_proj(zdst, vsrc, src_toks):
        toks = []
        rot = [0]
        for h in range(H):
            for (o, w) in tiles(T, 512):
                bk = rot[0] % 4
                rot[0] += 1
                p0 = (h % 2) * 64
                lh = vsrc[p0:p0 + 64, 0:R]
                rh = qk[p0:p0 + 64, (h // 2) * T + o:(h // 2) * T + o + w]
                b.mm_group(bk, b.bank(bk, 0, w, np_=R), [(lh, rh)],
                           waits=[src_toks[-1]])
                dst = zdst[(h % 2) * 32:(h % 2) * 32 + 32,
                           (h // 2) * T + o:(h // 2) * T + o + w]
                toks.append(b.evac("V", bk, (lambda e, dst=dst, bk=bk, w=w:
                                             e.tensor_copy(dst, b.bank(bk, 0, w, np_=R)))))
        return toks

    wsl = lambda Wt: (lambda dc, ic: Wt[:, dc * D + ic * 128: dc * D + ic * 128 + 128])
    wtq = wt_prep("Wq", wA)
    ldbq = b.dma("G", ev.bias[0:1, 0:D], W["bq"].rearrange("(a d) -> a d", a=1))
    q_toks = proj_fm(b, ev, xln, wsl(wA), qk, bias_row=ev.bias,
                     wait0=[wtq[-1], ldbq])
    zq_toks = z_proj(zq, v2q, q_toks)
    wtk = wt_prep("Wk", wB, war=wtq[-1])
    ldbk = b.dma("G", ev.bias[0:1, 0:D], W["bk"].rearrange("(a d) -> a d", a=1),
                 waits=[q_toks[-1]])
    k_toks = proj_fm(b, ev, xln, wsl(wB), qk, bias_row=ev.bias,
                     wait0=[wtk[-1], ldbk, zq_toks[-1]])
    zk_toks = z_proj(zk, v2, k_toks)

    wveff_toks = [None]
    bufv = [None, None]
    for g in range(DC):
        buf = g % 2
        ld = b.dma("G", gran_bf[:, buf * D:(buf + 1) * D],
                   W["Wv"][g * 128:(g + 1) * 128, :],
                   waits=[bufv[buf], q_toks[-1], wtk[-1], zk_toks[-1]])
        for half in range(2):
            h_ = 2 * g + half
            for dc in range(DC):
                bk = dc % 4
                lh = gran_bf[half * 64:(half + 1) * 64,
                             buf * D + dc * 128: buf * D + dc * 128 + 128]
                rh = vut2[half * 64:(half + 1) * 64, :]
                b.mm_group(bk, b.bank(bk, 0, 64), [(lh, rh)], waits=[ld])
                dst = wA[:, dc * D + h_ * 64: dc * D + h_ * 64 + 64]
                wveff_toks.append(b.evac("V", bk, (lambda e, dst=dst, bk=bk:
                                                   e.tensor_copy(dst, b.bank(bk, 0, 64)))))
        bufv[buf] = wveff_toks[-1]

    ldbv = b.dma("G", ev.bias[0:1, 0:D], W["bv"].rearrange("(a d) -> a d", a=1),
                 waits=[k_toks[-1]])
    cb_toks = []
    for dcb in range(DC):
        b.mm_group(6, b.bank(6, 0, 1),
                   [(ev.bias[0:1, dcb * 128: dcb * 128 + 128],
                     ev.ones_row[0:1, 0:1])], waits=[ldbv])
        dst = ev.colbf[:, dcb:dcb + 1]
        cb_toks.append(b.evac("V", 6, (lambda e, dst=dst: e.tensor_copy(
            dst, b.bank(6, 0, 1)))))
    bve_toks = []
    for h_ in range(H):
        p0 = (h_ % 2) * 64
        lh = ev.colbf[p0:p0 + 64, h_ // 2: h_ // 2 + 1]
        rh = vut2[p0:p0 + 64, :]
        b.mm_group(7, b.bank(7, 0, 64, np_=1), [(lh, rh)], waits=[cb_toks[-1]])
        dst = ev.bias[0:1, h_ * 64: h_ * 64 + 64]
        bve_toks.append(b.evac("V", 7, (lambda e, dst=dst: e.tensor_copy(
            dst, b.bank(7, 0, 64, np_=1)))))

    von_ms = b.op("G", lambda e: e.memset(v_ones[:, :], 1.0),
                  waits=[zk_toks[-1]])
    wslA_r = lambda dc, io, iw: wA[:, dc * D + io: dc * D + io + iw]
    v_toks = proj_tm_vones(b, ev, xln, wslA_r, v_ones, bias_row=ev.bias,
                           wait0=[wveff_toks[-1], bve_toks[-1], von_ms])
    b.barrier()

    wto = wt_prep("Wo", wB)
    ldbo = b.dma("G", ev.bias[0:1, 0:D], W["bo"].rearrange("(a d) -> a d", a=1),
                 waits=[v_toks[-1]])
    b.barrier()

    sc_lh = lambda h, kc: zk[(h % 2) * 32:(h % 2) * 32 + 32,
                             (h // 2) * T + kc * 128:(h // 2) * T + kc * 128 + 128]
    sc_rh = lambda h, go, gw: zq[(h % 2) * 32:(h % 2) * 32 + 32,
                                 (h // 2) * T + go:(h // 2) * T + go + gw]
    if KC * 512 <= WELS:
        ptil2 = ev.a1(ev.R1, KC * 512)
    else:
        ptil2 = ev.a1(ev.R1 + 2 * WELS + 256, KC * 512)
    attention_core(b, ev, sc_lh, sc_rh, v_ones, y, False, ptil2)
    b.barrier()
    wslo = lambda dc, dco: wB[:, dc * D + dco * 128: dc * D + dco * 128 + 128]
    st = out_proj_residual(b, ev, y, wslo, x_in, x_out, bias_row=ev.bias,
                           wait0=[wto[-1], ldbo])
    b.barrier()
    return st



def block_moe(b: Bld, ev: Env, x_in, out_tm, W):
    cfg = ev.cfg
    D, T, FF, NE = cfg["D"], cfg["T"], cfg["FF"], cfg["NE"]
    DC = D // 128
    DCT = DC * T
    TH = T // 2
    FC = FF // 128
    FQ = FF // 4
    KC = T // 128
    TCH = TH // 128
    xln = ev.a1(ev.R2, DCT)
    wq_els = DC * FQ

    def acc_sl(tc):
        if tc < 8:
            return ev.a1f(ev.R0 + tc * 2 * D, D)
        return ev.a1f(ev.R1 + (tc - 8) * 2 * D, D)

    def h_sl(fc, o, w):
        if fc < 16:
            return ev.a1(ev.R3 + fc * TH + o, w)
        if fc < 24:
            return ev.a1(ev.R4 + (fc - 16) * TH + o, w)
        return ev.a2((fc - 24) * TH + o, w)

    wdring = ev.a2(8 * TH + 2 * DC * max(FF // 8, 128), 4096)
    xlnf = ev.xlnf_dram

    layernorm(b, ev, x_in, xlnf_dram=xlnf)
    b.barrier()

    pads = ev.a1f(ev.R4, D)
    asb = ev.a1f(ev.R4 + 2 * D, 512)
    xlf = [ev.a1f(ev.R4 + 2 * D + 1024, 512),
           ev.a1f(ev.R4 + 2 * D + 2048, 512)]
    ms = b.op("G", lambda e: e.memset(pads[:, :], 0.0))
    msa = b.op("G", lambda e: e.memset(asb[:, :], 0.0))
    ldr = b.dma("Q", pads[0:NE, 0:D], W["router_w"], waits=[ms])
    rw_toks = []
    for dc in range(DC):
        bk = 4 + (dc % 2)
        b.transpose_to(bk, b.bank(bk, 0, 128),
                       pads[:, dc * 128: dc * 128 + 128], ev.ident[:, :],
                       waits=[ldr, ms])
        dstf = ev.rwtf[:, dc * NE: dc * NE + NE]
        rw_toks.append(b.evac("V", bk, (lambda e, dstf=dstf, bk=bk:
                                        e.tensor_copy(dstf, b.bank(bk, 0, NE)))))
    sm = ev.smalls
    slot_rd = [None, None]
    for ti, (o, wti) in enumerate(tiles(T, 512)):
        Abk = 2 + (ti % 2)
        pre_w = b.bank_r[Abk] + [b.bank_w[Abk], rw_toks[-1]]
        for dc in range(DC):
            par = dc % 2
            ldx = b.dma("Q", xlf[par][:, 0:wti],
                        xlnf[dc * 128:(dc + 1) * 128, o:o + wti],
                        waits=[slot_rd[par]])
            lh = ev.rwtf[:, dc * NE: dc * NE + NE]
            slot_rd[par] = b.op(
                "T", (lambda e, lh=lh, par=par, wti=wti, Abk=Abk,
                      st=(dc == 0), sp=(dc == DC - 1):
                      e.matmul(b.bank(Abk, 0, wti, np_=NE), lh,
                               xlf[par][:, 0:wti], start=st, stop=sp)),
                waits=[ldx] + (pre_w if dc == 0 else []), inc=True)
        b.bank_w[Abk] = slot_rd[(DC - 1) % 2]
        b.bank_r[Abk] = []
        cpA = b.evac("V", Abk, (lambda e, wti=wti, Abk=Abk: e.tensor_copy(
            asb[0:NE, 0:wti], b.bank(Abk, 0, wti, np_=NE))), waits=[msa])
        for j in range(wti // 128):
            tc = (o + j * 128) // 128
            bk = 6 + (j % 2)
            b.transpose_to(bk, b.bank(bk, 0, 128),
                           asb[:, j * 128: j * 128 + 128], ev.ident[:, :],
                           waits=[cpA])
            _moe_top2(b, ev, bk, tc)
    b.barrier()

    z0 = b.op("G", lambda e: e.memset(ev.a1f(ev.R0, 8 * D)[:, :], 0.0))
    z1 = b.op("G", lambda e: e.memset(ev.a1f(ev.R1, 8 * D)[:, :], 0.0))
    b.barrier()

    FF8 = max(FF // 8, 128)
    NPH = FF // FF8
    w8els = DC * FF8
    wq8 = [ev.a2(8 * TH, w8els), ev.a2(8 * TH + w8els, w8els)]
    slot_free = [None, None]
    wq_tok = [None]
    for e_ in range(NE):
        for th in range(2):
            for gate in (True, False):
                nm = "w_gate" if gate else "w_up"
                for ph in range(NPH):
                    slot = ph % 2
                    ld = b.dma("G", wq8[slot].rearrange("p (c n) -> p c n", c=DC),
                               W[nm][e_].rearrange("(c p) n -> p c n", p=128)[
                                   :, :, ph * FF8:(ph + 1) * FF8],
                               waits=[slot_free[slot]])
                    rot = [0]
                    tok = None
                    for fcl in range(FF8 // 128):
                        fc = ph * (FF8 // 128) + fcl
                        for (o, w) in tiles(TH, 512):
                            bk = rot[0] % 2
                            rot[0] += 1
                            pairs = [(wq8[slot][:, dc * FF8 + fcl * 128:
                                                dc * FF8 + fcl * 128 + 128],
                                      xln[:, dc * T + th * TH + o:
                                          dc * T + th * TH + o + w])
                                     for dc in range(DC)]
                            b.mm_group(bk, b.bank(bk, 0, w), pairs, waits=[ld])
                            dst = h_sl(fc, o, w)
                            if gate:
                                tok = b.evac("S", bk, (lambda e, dst=dst, bk=bk,
                                                       w=w: e.activation(
                                    dst, b.bank(bk, 0, w), AF.Silu)))
                            else:
                                tok = b.evac("V", bk, (lambda e, dst=dst, bk=bk,
                                                       w=w: e.tensor_mul(
                                    dst, b.bank(bk, 0, w), dst)))
                    slot_free[slot] = tok
                    wq_tok[0] = tok
            ring_tok = [None, None]
            for dh in range(2):
                dhw = min(512, D - dh * 512)
                if dhw <= 0:
                    break
                first_done = [False] * TCH
                last_mm = [None]
                for fj in range(FC // 4):
                    slot = fj % 2
                    ld = b.dma("G",
                               wdring[:, slot * 2048: slot * 2048 + 4 * dhw].rearrange(
                                   "p (c n) -> p c n", c=4),
                               W["w_down"][e_].rearrange("(c p) n -> p c n", p=128)[
                                   :, 4 * fj: 4 * fj + 4, dh * 512: dh * 512 + dhw],
                               waits=[ring_tok[slot]])
                    for fc4 in range(4):
                        fc = fj * 4 + fc4
                        for tc in range(TCH):
                            lh = h_sl(fc, tc * 128, 128)
                            rh = wdring[:, slot * 2048 + fc4 * dhw:
                                        slot * 2048 + fc4 * dhw + dhw]
                            st_ = (fc == 0)
                            sp_ = (fc == FC - 1)
                            w_ = [ld, wq_tok[0]]
                            if st_:
                                w_ += b.bank_r[tc] + [b.bank_w[tc]]
                            tok = b.op("T", (lambda e, lh=lh, rh=rh, st_=st_,
                                             sp_=sp_, tc=tc, dhw=dhw:
                                             e.matmul(b.bank(tc, 0, dhw), lh, rh,
                                                      start=st_, stop=sp_)),
                                       waits=w_,
                                       inc=(fc4 == 3 and tc == TCH - 1))
                    ring_tok[slot] = tok
                    last_mm[0] = tok
                for tc in range(TCH):
                    b.bank_w[tc] = last_mm[0]
                    b.bank_r[tc] = []
                    gtc = th * TCH + tc
                    a_sl = acc_sl(gtc)[:, dh * 512: dh * 512 + dhw]
                    c_sl = ev.cw[:, gtc * NE + e_: gtc * NE + e_ + 1]
                    b.evac("V", tc, (lambda e, a_sl=a_sl, c_sl=c_sl, tc=tc,
                                     dhw=dhw: e.scalar_tensor_tensor(
                        a_sl, b.bank(tc, 0, dhw), c_sl, a_sl,
                        ALU.mult, ALU.add)), waits=[z0, z1])
    b.barrier()

    def xfull(dc):
        if dc < 4:
            return ev.a1f(ev.R3 + dc * 2 * T, T)
        if dc < 6:
            return ev.a1f(ev.R4 + (dc - 4) * 2 * T, T)
        return ev.a2f((dc - 6) * 2 * T, T)

    ld_toks = []
    for dc in range(DC):
        ld_toks.append(b.dma("Q", xfull(dc)[:, :],
                             x_in[dc * 128:(dc + 1) * 128, :]))
    obuf = ev.a2f(8 * TH, 2 * D)
    st_tok = [None, None]
    for tc in range(KC):
        par = tc % 2
        tts = []
        for dc in range(DC):
            bk = 4 + (dc % 2)
            b.transpose_to(bk, b.bank(bk, 0, 128),
                           xfull(dc)[:, tc * 128: tc * 128 + 128],
                           ev.ident[:, :], waits=[ld_toks[dc]])
            dst = obuf[:, par * D + dc * 128: par * D + dc * 128 + 128]
            a_sl = acc_sl(tc)[:, dc * 128: dc * 128 + 128]
            tts.append(b.evac("V", bk, (lambda e, dst=dst, a_sl=a_sl, bk=bk:
                                        e.tensor_add(dst, b.bank(bk, 0, 128), a_sl)),
                              waits=[st_tok[par]]))
        st_tok[par] = b.dma("Q", out_tm[tc * 128:(tc + 1) * 128, :],
                            obuf[:, par * D:(par + 1) * D], waits=tts)
    b.barrier()
    return [t for t in st_tok if t]


def _moe_top2(b, ev, bk, tc):
    NE = ev.cfg["NE"]
    sm = ev.smalls
    if True:
        b.evac("V", bk, (lambda e, bk=bk: e.reduce_max(
            sm[:, 0:1], b.bank(bk, 0, NE), axis=AX.X)))
        tv = b.evac("V", bk, (lambda e, bk=bk: e.tensor_scalar(
            sm[:, 8:8 + NE], b.bank(bk, 0, NE), sm[:, 0:1], None, ALU.subtract)))
        ts = b.op("S", (lambda e: e.activation(sm[:, 16:16 + NE],
                                               sm[:, 8:8 + NE], AF.Exp)),
                  waits=[tv])
        b.op("V", (lambda e: e.reduce_sum(sm[:, 1:2], sm[:, 16:16 + NE],
                                          axis=AX.X)), waits=[ts])
        b.op("V", lambda e: e.reciprocal(sm[:, 2:3], sm[:, 1:2]))
        b.op("V", lambda e: e.tensor_scalar(
            sm[:, 24:24 + NE], sm[:, 16:16 + NE], sm[:, 2:3], None, ALU.mult))
        b.op("V", lambda e: e.reduce_max(sm[:, 3:4], sm[:, 24:24 + NE], axis=AX.X))
        b.op("V", lambda e: e.tensor_scalar(
            sm[:, 32:32 + NE], sm[:, 24:24 + NE], sm[:, 3:4], 1e9,
            ALU.is_equal, ALU.mult))
        b.op("V", lambda e: e.tensor_tensor(
            sm[:, 40:40 + NE], sm[:, 24:24 + NE], sm[:, 32:32 + NE], ALU.subtract))
        b.op("V", lambda e: e.reduce_max(sm[:, 4:5], sm[:, 40:40 + NE], axis=AX.X))
        b.op("V", lambda e: e.tensor_scalar(
            sm[:, 48:48 + NE], sm[:, 24:24 + NE], sm[:, 4:5], None, ALU.is_ge))
        b.op("V", lambda e: e.tensor_tensor(
            sm[:, 48:48 + NE], sm[:, 48:48 + NE], sm[:, 24:24 + NE], ALU.mult))
        b.op("V", lambda e: e.tensor_add(sm[:, 5:6], sm[:, 3:4], sm[:, 4:5]))
        b.op("V", lambda e: e.reciprocal(sm[:, 6:7], sm[:, 5:6]))
        dst = ev.cw[:, tc * NE:(tc + 1) * NE]
        b.op("V", (lambda e, dst=dst: e.tensor_scalar(
            dst, sm[:, 48:48 + NE], sm[:, 6:7], None, ALU.mult)))



def block_moe_sparse(b: Bld, ev: Env, x_in, out_tm, W):
    cfg = ev.cfg
    D, T, FF, NE = cfg["D"], cfg["T"], cfg["FF"], cfg["NE"]
    DC = D // 128
    DCT = DC * T
    C = moe_cap(T)
    JCN = C // 128
    FC = FF // 128
    KC = T // 128
    xln = ev.a1(ev.R2, DCT)
    xlnf = ev.xlnf_dram
    xtm = ev.xtm_dram
    yed = ev.yed_dram
    t1d = ev.t1_dram
    t2d = ev.t2_dram
    idxd = ev.idx_dram
    cwd = ev.cw_dram
    xga = ev.xga_dram
    cwsl = ev.cwsl_dram

    def h_sl(fc, o, w):
        if fc < 16:
            return ev.a1(ev.R3 + fc * C + o, w)
        if fc < 24:
            return ev.a1(ev.R4 + (fc - 16) * C + o, w)
        return ev.a2((fc - 24) * C + o, w)

    wdring = ev.a2(8 * C + 2 * DC * max(FF // 8, 128), 4096)

    bregs = {}

    def _mk_regs(e):
        bregs["C"] = e.to_reg(C - 1)
        bregs["T"] = e.to_reg(T - 1)
        bregs["Y"] = e.to_reg(NE * C + 127)
        bregs["S"] = e.to_reg(NE * C - 1)
        return e.engine_nop()
    b.op("G", _mk_regs)

    zsrc = ev.a1(ev.R3, 2048)
    zmb = b.op("V", lambda e: e.memset(zsrc[:, :], 0.0))
    zga_toks = []
    for r0 in range(0, NE * C, 128):
        zga_toks.append(b.dma("Q", xga[r0:r0 + 128, :], zsrc[:, 0:D],
                              waits=[zmb]))
    zga_toks.append(b.dma(
        "Q", cwsl.rearrange("(p c) a -> p (c a)", p=128),
        zsrc.bitcast(F32)[:, 0:NE * C // 128], waits=[zmb]))

    layernorm(b, ev, x_in, xlnf_dram=xlnf, xtm_dram=xtm)
    b.barrier()

    pads = ev.a1f(ev.R4, D)
    asb = ev.a1f(ev.R4 + 2 * D, 512)
    xlf = [ev.a1f(ev.R4 + 2 * D + 1024, 512),
           ev.a1f(ev.R4 + 2 * D + 2048, 512)]
    ms = b.op("G", lambda e: e.memset(pads[:, :], 0.0))
    msa = b.op("G", lambda e: e.memset(asb[:, :], 0.0))
    ldr = b.dma("Q", pads[0:NE, 0:D], W["router_w"], waits=[ms])
    rw_toks = []
    for dc in range(DC):
        bk = 4 + (dc % 2)
        b.transpose_to(bk, b.bank(bk, 0, 128),
                       pads[:, dc * 128: dc * 128 + 128], ev.ident[:, :],
                       waits=[ldr, ms])
        dstf = ev.rwtf[:, dc * NE: dc * NE + NE]
        rw_toks.append(b.evac("V", bk, (lambda e, dstf=dstf, bk=bk:
                                        e.tensor_copy(dstf, b.bank(bk, 0, NE)))))
    sm = ev.smalls
    prefill_toks = []
    for e_ in range(NE):
        prefill_toks.append(b.dma(
            "Q", idxd[e_].rearrange("(p a) -> p a", p=128),
            ev.ifill[:, 0:C // 128]))
    carry = ev.rows[0:1, 2050:2050 + NE]
    bx = ev.R4 + 2 * D + 3072
    mk = ev.a1f(bx, NE)
    rank = ev.a1f(bx + 16, NE)
    r2f = ev.a1f(bx + 32, NE)
    ecadd = ev.a1f(bx + 48, NE)
    ri2b = [ev.locs[:, 8 * i: 8 * i + NE] for i in range(4)]
    rib = [ev.locs[:, 32 + 8 * i: 32 + 8 * i + NE] for i in range(4)]
    ec_t = None
    for e_ in range(NE):
        ec_t = b.op("V", (lambda e, e_=e_: e.memset(
            ecadd[:, e_:e_ + 1], float(e_ * C))))
    cz = b.op("V", lambda e: e.memset(carry, 0.0))
    st = {"carry_t": cz, "idx": [], "up": [],
          "ri_rd": [[] for _ in range(4)], "ri2_rd": [[] for _ in range(4)],
          "xrow_war": [[] for _ in range(4)]}
    idx_sc_toks = st["idx"]
    up_toks = st["up"]

    def idx_stage(tc):
        xpar = tc % 4
        ldx = b.dma("Q", ev.a1(ev.R0 + xpar * 1024, 1024)[:, 0:D],
                    xtm[tc * 128:(tc + 1) * 128, :], waits=st["xrow_war"][xpar])
        cwt = ev.cw[:, tc * NE:(tc + 1) * NE]
        tmk = b.op("V", (lambda e, cwt=cwt: e.tensor_single_scalar(
            mk, cwt, 0.0, ALU.is_gt)))
        bkr = tc % 2
        b.mm_group(bkr, b.bank(bkr, 0, NE),
                   [(ev.ustrict[:, :], mk),
                    (ev.ones_row_f[0:1, 0:128], carry)],
                   waits=[tmk, st["carry_t"]])
        b.evac("V", bkr, (lambda e, bkr=bkr: e.tensor_copy(
            rank, b.bank(bkr, 0, NE))))
        bkc = 4
        b.mm_group(bkc, b.bank(bkc, 0, NE, np_=1),
                   [(ev.ones_col_f[:, :], mk)], waits=[tmk])
        st["carry_t"] = b.evac("V", bkc, (lambda e: e.tensor_tensor(
            carry, carry, b.bank(bkc, 0, NE, np_=1), ALU.add)))
        b.op("V", lambda e: e.tensor_single_scalar(rank, rank, 65536.0,
                                                   ALU.subtract))
        b.op("V", lambda e: e.tensor_tensor(rank, rank, mk, ALU.mult))
        b.op("V", lambda e: e.tensor_single_scalar(rank, rank, 65536.0, ALU.add))
        ri = rib[xpar]
        cvi = b.op("V", (lambda e, ri=ri: e.tensor_copy(ri, rank)),
                   waits=st["ri_rd"][xpar])
        b.op("V", lambda e: e.tensor_tensor(r2f, rank, ecadd, ALU.add),
             waits=[ec_t])
        ri2 = ri2b[xpar]
        cvi2 = b.op("V", (lambda e, ri2=ri2: e.tensor_copy(ri2, r2f)),
                    waits=st["ri2_rd"][xpar])
        st["ri_rd"][xpar] = []
        st["ri2_rd"][xpar] = []
        for e_ in range(NE):
            tk = b.idma(
                (lambda e, e_=e_, tc=tc, ri=ri: e.indirect_dma_start(
                    idxd.rearrange("e (c a) -> (e c) a", a=1),
                    IndirectOffsetOnAxis(ri[:, e_:e_ + 1], 0),
                    ev.iota[:, tc:tc + 1], None,
                    element_offset=e_ * C,
                    bounds_check=bregs["C"], oob_is_err=False)),
                waits=[cvi] + prefill_toks)
            st["idx"].append(tk)
            st["ri_rd"][xpar].append(tk)
        for e_ in range(NE):
            sx = b.idma(
                (lambda e, e_=e_, xpar=xpar, ri2=ri2: e.indirect_dma_start(
                    xga[:, :], IndirectOffsetOnAxis(ri2[:, e_:e_ + 1], 0),
                    ev.a1(ev.R0 + xpar * 1024, 1024)[:, 0:D], None,
                    bounds_check=bregs["S"], oob_is_err=False)),
                waits=[cvi2, ldx])
            sc = b.idma(
                (lambda e, e_=e_, tc=tc, ri2=ri2: e.indirect_dma_start(
                    cwsl[:, :], IndirectOffsetOnAxis(ri2[:, e_:e_ + 1], 0),
                    ev.cw[:, tc * NE + e_: tc * NE + e_ + 1], None,
                    bounds_check=bregs["S"], oob_is_err=False)),
                waits=[cvi2])
            st["up"] += [sx, sc]
            st["ri2_rd"][xpar] += [sx, sc]
        st["xrow_war"][xpar] = st["up"][-2 * NE:]

    slot_rd = [None, None]
    for ti, (o, wti) in enumerate(tiles(T, 512)):
        Abk = 2 + (ti % 2)
        pre_w = b.bank_r[Abk] + [b.bank_w[Abk], rw_toks[-1]]
        for dc in range(DC):
            par = dc % 2
            ldx = b.dma("Q", xlf[par][:, 0:wti],
                        xlnf[dc * 128:(dc + 1) * 128, o:o + wti],
                        waits=[slot_rd[par]])
            lh = ev.rwtf[:, dc * NE: dc * NE + NE]
            slot_rd[par] = b.op(
                "T", (lambda e, lh=lh, par=par, wti=wti, Abk=Abk,
                      st=(dc == 0), sp=(dc == DC - 1):
                      e.matmul(b.bank(Abk, 0, wti, np_=NE), lh,
                               xlf[par][:, 0:wti], start=st, stop=sp)),
                waits=[ldx] + (pre_w if dc == 0 else []), inc=True)
        b.bank_w[Abk] = slot_rd[(DC - 1) % 2]
        b.bank_r[Abk] = []
        cpA = b.evac("V", Abk, (lambda e, wti=wti, Abk=Abk: e.tensor_copy(
            asb[0:NE, 0:wti], b.bank(Abk, 0, wti, np_=NE))), waits=[msa])
        for j in range(wti // 128):
            tc = (o + j * 128) // 128
            bk = 6 + (j % 2)
            b.transpose_to(bk, b.bank(bk, 0, 128),
                           asb[:, j * 128: j * 128 + 128], ev.ident[:, :],
                           waits=[cpA])
            _moe_top2(b, ev, bk, tc)
            idx_stage(tc)

    st_cw = b.dma("Q", cwd.rearrange("(c p) n -> p c n", p=128),
                  ev.cw.rearrange("p (c n) -> p c n", n=NE),
                  waits=up_toks[-1:])
    locs = ev.locs
    NJ = NE * JCN
    lfil = b.op("G", lambda e: e.memset(locs[:, 0:KC], NE * C),
                waits=up_toks[-NE:] + idx_sc_toks[-NE:])
    zbuf = ev.a1f(ev.R1 + 8192, 1024)
    zz = b.op("V", lambda e: e.memset(zbuf[:, :], 0.0))
    zpad = b.dma("Q", yed[NE * C:NE * C + 128, :],
                 zbuf.bitcast(BF16)[:, 0:D], waits=[zz])
    pf1 = b.dma("Q", t1d.rearrange("(p c) a -> p (c a)", p=128),
                locs[:, 0:KC], waits=[lfil])
    pf2 = b.dma("Q", t2d.rearrange("(p c) a -> p (c a)", p=128),
                locs[:, 0:KC], waits=[lfil])
    lio = b.op("G", lambda e: e.iota(locs[:, 0:NJ], pattern=[[128, NJ]],
                                     base=0, channel_multiplier=1),
               waits=[pf1, pf2] + up_toks[-NE:])
    i2sb = ev.smalls[:, 0:16].bitcast(mybir.dt.int32)
    tb_buf = {1: i2sb[:, 8:16], 2: i2sb[:, 0:8]}
    tb_prev = {1: [pf1, lio], 2: [pf2, lio]}

    def tb_round(k):
        for cid, tbl, e_ in ((2, t2d, k), (1, t1d, NE - 1 - k)):
            buf = tb_buf[cid]
            toks = []
            for jc in range(JCN):
                lde = b.dma("Q", buf[:, jc:jc + 1],
                            idxd[e_, jc * 128:(jc + 1) * 128].rearrange(
                                "(p a) -> p a", p=128),
                            waits=tb_prev[cid] + idx_sc_toks[-NE:])
                toks.append(b.idma(
                    (lambda e, tbl=tbl, buf=buf, jc=jc, e_=e_:
                     e.indirect_dma_start(
                         tbl[:, :],
                         IndirectOffsetOnAxis(buf[:, jc:jc + 1], 0),
                         locs[:, e_ * JCN + jc: e_ * JCN + jc + 1], None,
                         bounds_check=bregs["T"], oob_is_err=False)),
                    waits=[lde] + tb_prev[cid]))
            tb_prev[cid] = toks
    b.barrier()

    FF8 = max(FF // 8, 128)
    NPH = FF // FF8
    w8els = DC * FF8
    wq8 = [ev.a2(8 * C, w8els), ev.a2(8 * C + w8els, w8els)]
    xgt = [ev.a1(ev.R0, JCN * D), ev.a1(ev.R0 + 8192, JCN * D)]
    xgf = [ev.a1(ev.R1, DC * C), ev.a1(ev.R2, DC * C)]
    y_sb = ev.a1(ev.R1 + DC * C, 4 * 1024)
    cwcp = [ev.smalls[:, 24 + 8 * i: 24 + 8 * i + JCN] for i in range(4)]
    slot_free = [None, None]
    wq_tok = [None]
    y_war = [[] for _ in range(JCN)]
    up_w = up_toks[-NE:]
    xgt_war = [list(up_w), list(up_w)]
    fm_war = [[], []]
    cw_war = [list(up_w) for _ in range(4)]
    ld_x = {}
    ld_cw = {}
    trd = {}
    idx_sb = ev.colbf[:, 0:16].bitcast(mybir.dt.int32)
    STG = globals().get("DBG_MOE_STG", 99)
    NEXP = NE if DBG_MOE_NEXP is None else DBG_MOE_NEXP

    def load(e_):
        par, p4 = e_ % 2, e_ % 4
        xts, cts = [], []
        for jc in range(JCN):
            xts.append(b.dma(
                "Q", xgt[par][:, jc * D: jc * D + D],
                xga[e_ * C + jc * 128: e_ * C + jc * 128 + 128, :],
                waits=(xgt_war[par] + idx_sc_toks[-NE:]) if jc == 0 else ()))
            cts.append(b.dma(
                "Q", cwcp[p4][:, jc:jc + 1],
                cwsl[e_ * C + jc * 128: e_ * C + jc * 128 + 128, :],
                waits=cw_war[p4] if jc == 0 else ()))
        ld_x[e_], ld_cw[e_] = xts, cts

    def xpose(e_):
        par = e_ % 2
        jlast = None
        for jc in range(JCN):
            for dc in range(DC):
                bt = 6 + (dc % 2)
                src = xgt[par][:, jc * D + dc * 128: jc * D + dc * 128 + 128]
                b.transpose_to(bt, b.bankb(bt, 0, 128), src, ev.identb[:, :],
                               waits=[ld_x[e_][jc]])
                dst = xgf[par][:, dc * C + jc * 128: dc * C + jc * 128 + 128]
                jlast = b.evac("V", bt, (lambda e, dst=dst, bt=bt:
                                         e.tensor_copy(dst, b.bankb(bt, 0, 128))),
                               waits=fm_war[par])
        trd[e_] = jlast
        xgt_war[par] = [jlast]

    def compute(e_):
        par, p4 = e_ % 2, e_ % 4
        xg_fm = xgf[par]
        cwc = cwcp[p4]
        for gate in (True, False):
            nm = "w_gate" if gate else "w_up"
            for ph in range(NPH):
                slot = ph % 2
                ld = b.dma("G", wq8[slot].rearrange("p (c n) -> p c n", c=DC),
                           W[nm][e_].rearrange("(c p) n -> p c n", p=128)[
                               :, :, ph * FF8:(ph + 1) * FF8],
                           waits=[slot_free[slot]])
                rot = [0]
                tok = None
                for fcl in range(FF8 // 128):
                    fc = ph * (FF8 // 128) + fcl
                    for (o, w) in tiles(C, 512):
                        bk = rot[0] % 4
                        rot[0] += 1
                        pairs = [(wq8[slot][:, dc * FF8 + fcl * 128:
                                            dc * FF8 + fcl * 128 + 128],
                                  xg_fm[:, dc * C + o: dc * C + o + w])
                                 for dc in range(DC)]
                        b.mm_group(bk, b.bank(bk, 0, w), pairs,
                                   waits=[ld, trd[e_]])
                        dst = h_sl(fc, o, w)
                        if gate:
                            tok = b.evac("S", bk, (lambda e, dst=dst, bk=bk,
                                                   w=w: e.activation(
                                dst, b.bank(bk, 0, w), AF.Silu)))
                        else:
                            tok = b.evac("V", bk, (lambda e, dst=dst, bk=bk,
                                                   w=w: e.tensor_mul(
                                dst, b.bank(bk, 0, w), dst)))
                slot_free[slot] = tok
                wq_tok[0] = tok
        fm_war[par] = [wq_tok[0]]
        if STG < 5:
            return
        ring_tok = [None, None]
        ev_last = None
        for dh in range(2):
            dhw = min(512, D - dh * 512)
            if dhw <= 0:
                break
            last_mm = [None]
            for fj in range(FC // 4):
                slot = fj % 2
                ld = b.dma("G",
                           wdring[:, slot * 2048: slot * 2048 + 4 * dhw].rearrange(
                               "p (c n) -> p c n", c=4),
                           W["w_down"][e_].rearrange("(c p) n -> p c n", p=128)[
                               :, 4 * fj: 4 * fj + 4, dh * 512: dh * 512 + dhw],
                           waits=[ring_tok[slot]])
                for fc4 in range(4):
                    fc = fj * 4 + fc4
                    for jc in range(JCN):
                        lh = h_sl(fc, jc * 128, 128)
                        rh = wdring[:, slot * 2048 + fc4 * dhw:
                                    slot * 2048 + fc4 * dhw + dhw]
                        st_ = (fc == 0)
                        sp_ = (fc == FC - 1)
                        w_ = [ld, wq_tok[0]]
                        if st_:
                            w_ += b.bank_r[jc] + [b.bank_w[jc]]
                        tok = b.op("T", (lambda e, lh=lh, rh=rh, st_=st_,
                                         sp_=sp_, jc=jc, dhw=dhw:
                                         e.matmul(b.bank(jc, 0, dhw), lh, rh,
                                                  start=st_, stop=sp_)),
                                   waits=w_,
                                   inc=(fc4 == 3 and jc == JCN - 1))
                ring_tok[slot] = tok
                last_mm[0] = tok
            for jc in range(JCN):
                b.bank_w[jc] = last_mm[0]
                b.bank_r[jc] = []
                ydst = y_sb[:, jc * 512: jc * 512 + dhw]
                ev_t = b.evac("V", jc, (lambda e, ydst=ydst, jc=jc, dhw=dhw,
                                        cwc=cwc:
                                        e.tensor_scalar(ydst, b.bank(jc, 0, dhw),
                                                        cwc[:, jc:jc + 1], None,
                                                        ALU.mult)),
                              waits=y_war[jc] + [ld_cw[e_][jc]])
                ev_last = ev_t
                if STG < 6:
                    continue
                y_war[jc] = [b.dma(
                    "Q", yed[e_ * C + jc * 128: e_ * C + jc * 128 + 128,
                             dh * 512: dh * 512 + dhw],
                    ydst, waits=[ev_t])]
        cw_war[p4] = [ev_last]
        tb_round(e_)

    for k in range(min(2, NEXP)):
        load(k)
    if NEXP > 0:
        xpose(0)
    for e_ in range(NEXP):
        if e_ + 2 < NEXP:
            load(e_ + 2)
        if e_ + 1 < NEXP:
            xpose(e_ + 1)
        if STG < 4:
            continue
        compute(e_)
    b.barrier()

    def xfull(dc):
        if dc < 4:
            return ev.a1f(ev.R3 + dc * 2 * T, T)
        if dc < 6:
            return ev.a1f(ev.R4 + (dc - 4) * 2 * T, T)
        return ev.a2f((dc - 6) * 2 * T, T)

    ld_toks = []
    for dc in range(DC):
        ld_toks.append(b.dma("Q", xfull(dc)[:, :],
                             x_in[dc * 128:(dc + 1) * 128, :]))
    obuf = ev.a2f(max(8 * C, max(DC - 6, 0) * 2 * T), 2 * D)
    abuf = [ev.a1(ev.R0, D), ev.a1(ev.R0 + D, D)]
    bbuf = [ev.a1(ev.R0 + 2 * D, D), ev.a1(ev.R0 + 3 * D, D)]
    gsum = [ev.a1f(ev.R0 + 4 * D, D), ev.a1f(ev.R0 + 6 * D, D)]
    st_tok = [None, None]
    tc_war = [[], []]
    for tc in range(KC):
        par = tc % 2
        ldt = b.dma("Q", idx_sb[:, 2 * par: 2 * par + 1],
                    t1d[tc * 128:(tc + 1) * 128, :], waits=tc_war[par])
        ldt2 = b.dma("Q", idx_sb[:, 2 * par + 1: 2 * par + 2],
                     t2d[tc * 128:(tc + 1) * 128, :], waits=tc_war[par])
        g1 = b.idma((lambda e, par=par: e.indirect_dma_start(
            abuf[par][:, :], None, yed[:, :],
            IndirectOffsetOnAxis(idx_sb[:, 2 * par: 2 * par + 1], 0),
            bounds_check=bregs["Y"], oob_is_err=False)),
            waits=[ldt, st_tok[par]])
        g2 = b.idma((lambda e, par=par: e.indirect_dma_start(
            bbuf[par][:, :], None, yed[:, :],
            IndirectOffsetOnAxis(idx_sb[:, 2 * par + 1: 2 * par + 2], 0),
            bounds_check=bregs["Y"], oob_is_err=False)),
            waits=[ldt2])
        g12 = b.op("V", (lambda e, par=par: e.tensor_add(
            gsum[par][:, :], abuf[par][:, :], bbuf[par][:, :])),
            waits=[g1, g2])
        tc_war[par] = [g12]
        tts = []
        for dc in range(DC):
            bk = 4 + (dc % 2)
            b.transpose_to(bk, b.bank(bk, 0, 128),
                           xfull(dc)[:, tc * 128: tc * 128 + 128],
                           ev.ident[:, :], waits=[ld_toks[dc]])
            dst = obuf[:, par * D + dc * 128: par * D + dc * 128 + 128]
            a_sl = gsum[par][:, dc * 128: dc * 128 + 128]
            tts.append(b.evac("V", bk, (lambda e, dst=dst, a_sl=a_sl, bk=bk:
                                        e.tensor_add(dst, b.bank(bk, 0, 128), a_sl)),
                              waits=[g12, st_tok[par]]))
        st_tok[par] = b.dma("Q", out_tm[tc * 128:(tc + 1) * 128, :],
                            obuf[:, par * D:(par + 1) * D], waits=tts)
    b.barrier()
    return [t for t in st_tok if t]



BLOCK_WEIGHTS = {
    "l0a": [("Wq", "L0_attn_Wq"), ("Wk", "L0_attn_Wk"), ("Wv", "L0_attn_Wv"),
            ("Wo", "L0_attn_Wo")],
    "l0f": [("W1", "L0_ffn_W1"), ("b1", "L0_ffn_b1"), ("W2", "L0_ffn_W2"),
            ("b2", "L0_ffn_b2")],
    "l1a": [("Wq", "L1_attn_Wq"), ("bq", "L1_attn_bq"), ("Wk", "L1_attn_Wk"),
            ("bk", "L1_attn_bk"), ("Wv", "L1_attn_Wv"), ("bv", "L1_attn_bv"),
            ("Wo", "L1_attn_Wo"), ("bo", "L1_attn_bo"), ("U", "L1_attn_U"),
            ("V", "L1_attn_V")],
    "moe": [("router_w", "L1_ffn_router_w"), ("w_gate", "L1_ffn_w_gate"),
            ("w_up", "L1_ffn_w_up"), ("w_down", "L1_ffn_w_down")],
}

WSHAPES = {
    "L0_attn_Wq": ("D", "D"), "L0_attn_Wk": ("D", "D"),
    "L0_attn_Wv": ("D", "D"), "L0_attn_Wo": ("D", "D"),
    "L0_ffn_W1": ("FF", "D"), "L0_ffn_b1": ("FF",),
    "L0_ffn_W2": ("D", "FF"), "L0_ffn_b2": ("D",),
    "L1_attn_Wq": ("D", "D"), "L1_attn_bq": ("D",),
    "L1_attn_Wk": ("D", "D"), "L1_attn_bk": ("D",),
    "L1_attn_Wv": ("D", "D"), "L1_attn_bv": ("D",),
    "L1_attn_Wo": ("D", "D"), "L1_attn_bo": ("D",),
    "L1_attn_U": ("HD", "R"), "L1_attn_V": ("HD", "R"),
    "L1_ffn_router_w": ("NE", "D"),
    "L1_ffn_w_gate": ("NE", "D", "FF"), "L1_ffn_w_up": ("NE", "D", "FF"),
    "L1_ffn_w_down": ("NE", "FF", "D"),
}

BLOCK_FN = {}


def build_program(cfg, parts):
    nc = bass.Bass("TRN2", target_bir_lowering=False)
    stack = contextlib.ExitStack()
    b = Bld(nc, stack)
    ev = Env(b, stack, cfg)
    b.barrier()
    D, T = cfg["D"], cfg["T"]

    ev.xlnf_dram = None
    ev.xtm_dram = ev.yed_dram = ev.idx_dram = ev.cw_dram = None
    ev.t1_dram = ev.t2_dram = None
    if "moe" in parts:
        ev.xlnf_dram = nc.dram_tensor("xlnf", [D, T], F32, kind="Internal").ap()
        if SPARSE_MOE:
            NE = cfg["NE"]
            _k = "ExternalOutput" if globals().get("DBG_SPARSE") else "Internal"
            CP = moe_cap(T)
            ev.xtm_dram = nc.dram_tensor("xtm", [T, D], BF16, kind=_k).ap()
            ev.yed_dram = nc.dram_tensor("yed", [NE * CP + 128, D], BF16,
                                         kind=_k).ap()
            ev.t1_dram = nc.dram_tensor("t1d", [T, 1], mybir.dt.int32,
                                        kind=_k).ap()
            ev.t2_dram = nc.dram_tensor("t2d", [T, 1], mybir.dt.int32,
                                        kind=_k).ap()
            ev.idx_dram = nc.dram_tensor("idxd", [NE, CP], mybir.dt.int32,
                                         kind=_k).ap()
            ev.cw_dram = nc.dram_tensor("cwd", [T, NE], F32, kind=_k).ap()
            ev.xga_dram = nc.dram_tensor("xga", [NE * CP, D], BF16,
                                         kind=_k).ap()
            ev.cwsl_dram = nc.dram_tensor("cwsl", [NE * CP, 1], F32,
                                          kind=_k).ap()

    needed = sorted({nm for p in parts for (_, nm) in BLOCK_WEIGHTS[p]})
    wdict = {nm: nc.dram_tensor(nm, [cfg[s] for s in WSHAPES[nm]], F32,
                                kind="ExternalInput").ap() for nm in needed}

    first, last = parts[0], parts[-1]
    if first == "l0a":
        x0 = nc.dram_tensor("x0", [T, D], F32, kind="ExternalInput").ap()
        xa = nc.dram_tensor("xa", [D, T], F32, kind="Internal").ap()
    else:
        xa = nc.dram_tensor("xin", [D, T], F32, kind="ExternalInput").ap()
    xb = nc.dram_tensor("xb", [D, T], F32, kind="Internal").ap()
    if last == "moe":
        out = nc.dram_tensor("out", [T, D], F32, kind="ExternalOutput").ap()
    else:
        out = nc.dram_tensor("xout", [D, T], F32, kind="ExternalOutput").ap()
    ev.w1c_dram = ev.w2c_dram = None
    if "l0f" in parts:
        FF = cfg["FF"]
        ev.w1c_dram = nc.dram_tensor(
            "w1c", [cfg["FF"] // 128, 128, (D // 128) * 128], BF16,
            kind="Internal").ap()
        ev.w2c_dram = nc.dram_tensor("w2c", [FF, D], BF16, kind="Internal").ap()

    if first == "l0a":
        pre_transpose(b, ev, x0, xa)
        b.barrier()

    ev.l0f_W = None
    fns = {"l0a": block_l0attn, "l0f": block_l0ffn, "l1a": block_l1attn,
           "moe": block_moe_sparse if SPARSE_MOE else block_moe}
    cur = xa
    final_toks = []
    for i, p in enumerate(parts):
        Wp = {k: wdict[nm] for (k, nm) in BLOCK_WEIGHTS[p]}
        is_last = i == len(parts) - 1
        dst = out if is_last else (xb if cur is xa else xa)
        tok = fns[p](b, ev, cur, dst, Wp)
        final_toks = tok if isinstance(tok, list) else [tok]
        cur = dst
    b.finish(final_toks)
    return nc, stack


_PART_SETS = {
    "fused": [["l0a", "l0f", "l1a", "moe"]],
    "blocks": [["l0a"], ["l0f"], ["l1a"], ["moe"]],
}


def kernel(**inputs):
    cfg = cfg_full()
    part_sets = _PART_SETS[MODE]
    x_full = np.ascontiguousarray(np.asarray(inputs["L0_attn_x"], np.float32))
    B = x_full.shape[0]
    assert B == N_CORES
    cur_per_core = [np.ascontiguousarray(x_full[i]) for i in range(B)]
    for parts in part_sets:
        nc, stack = build_program(cfg, parts)
        needed = sorted({nm for p in parts for (_, nm) in BLOCK_WEIGHTS[p]})
        wvals = {nm: np.ascontiguousarray(np.asarray(inputs[nm], np.float32))
                 for nm in needed}
        xkey = "x0" if parts[0] == "l0a" else "xin"
        in_maps = [dict(wvals, **{xkey: cur_per_core[c]}) for c in range(B)]
        res = run_bass_kernel_spmd(nc, in_maps, core_ids=list(range(B)))
        stack.close()
        okey = "out" if parts[-1] == "moe" else "xout"
        cur_per_core = [np.ascontiguousarray(np.asarray(res.results[c][okey],
                                                        np.float32))
                        for c in range(B)]
    return np.stack(cur_per_core, axis=0).astype(np.float32)


# version 24
# speedup vs baseline: 1.1451x; 1.0094x over previous
import contextlib
import numpy as np
import ml_dtypes

import concourse.bass as bass
import concourse.mybir as mybir
from concourse.bass_utils import run_bass_kernel_spmd
from concourse.bass_core_rust import IndirectOffsetOnAxis

F32 = mybir.dt.float32
BF16 = mybir.dt.bfloat16
AF = mybir.ActivationFunctionType
ALU = mybir.AluOpType
AX = mybir.AxisListType

N_CORES = 8
LN_EPS = 1e-5
MODE = "fused"
SPARSE_MOE = True
DBG_MOE_NEXP = None


def cfg_full():
    return dict(D=1024, T=2048, H=16, HD=64, R=32, FF=4096, NE=8)


def tiles(total, w):
    o = 0
    while o < total:
        yield o, min(w, total - o)
        o += w


def moe_cap(T):
    return max(128, (3 * T // 8) // 128 * 128)



class Bld:
    NLANE = 8

    def __init__(self, nc, stack):
        self.nc = nc
        self.ops = {e: [] for e in ("T", "V", "S", "G", "Q")}
        self.sem_names = ["sT", "sV", "sS", "sG"]
        for q in ("Q", "G"):
            self.sem_names += [f"q{q}{i}" for i in range(self.NLANE)]
        self.sems = {n: stack.enter_context(nc.semaphore(n)) for n in self.sem_names}
        self.cnt = {n: 0 for n in self.sem_names}
        self.rr = {"Q": 0, "G": 0}
        self.lastw = {}
        self.eng_sem = {"T": "sT", "V": "sV", "S": "sS", "G": "sG"}
        self.psum = stack.enter_context(nc.psum_tensor("psA", [128, 4096], F32))
        self.bank_w = [None] * 8
        self.bank_r = [[] for _ in range(8)]
        self.t_dirty = False

    def bank(self, bk, lo=0, hi=512, np_=128, p0=0):
        return self.psum[p0:p0 + np_, bk * 512 + lo: bk * 512 + hi]

    def bankb(self, bk, lo=0, hi=128, np_=128, p0=0):
        return self.psum.bitcast(BF16)[p0:p0 + np_, bk * 1024 + lo: bk * 1024 + hi]

    def _wait_list(self, ek, waits):
        out = []
        for t in waits:
            if t is None:
                continue
            s, v = t
            if v <= 0:
                continue
            key = (ek, s)
            if self.lastw.get(key, 0) < v:
                self.lastw[key] = v
                out.append((self.sems[s], v))
        return out

    def op(self, ek, fn, waits=(), inc=False):
        waits = list(waits)
        sem_name = self.eng_sem.get(ek)
        if ek in ("V", "S", "G"):
            waits.append((sem_name, self.cnt[sem_name]))
            inc = True
        wl = self._wait_list(ek, waits)
        tok = None
        if inc:
            self.cnt[sem_name] += 1
            tok = (sem_name, self.cnt[sem_name])
            if ek == "T":
                self.t_dirty = False
        elif ek == "T":
            self.t_dirty = True
        sems = self.sems

        def cl(eng, fn=fn, wl=wl, inc=inc, sem_name=sem_name):
            for (h, v) in wl:
                eng.wait_ge(h, v)
            ins = fn(eng)
            if inc:
                ins.then_inc(sems[sem_name], 1)
        self.ops[ek].append(cl)
        return tok

    def dma(self, q, out, in_, waits=()):
        lane = self.rr[q] % self.NLANE
        self.rr[q] += 1
        sem_name = f"q{q}{lane}"
        wl = self._wait_list(q, list(waits) + [(sem_name, self.cnt[sem_name])])
        self.cnt[sem_name] += 16
        tok = (sem_name, self.cnt[sem_name])
        sems = self.sems

        def cl(eng, out=out, in_=in_, wl=wl, sem_name=sem_name):
            for (h, v) in wl:
                eng.wait_ge(h, v)
            eng.dma_start(out=out, in_=in_).then_inc(sems[sem_name], 16)
        self.ops[q].append(cl)
        return tok

    def idma(self, fn, waits=()):
        lane = self.rr["G"] % self.NLANE
        self.rr["G"] += 1
        sem_name = f"qG{lane}"
        wl = self._wait_list("G", list(waits) + [(sem_name, self.cnt[sem_name])])
        self.cnt[sem_name] += 16
        tok = (sem_name, self.cnt[sem_name])
        sems = self.sems

        def cl(eng, fn=fn, wl=wl, sem_name=sem_name):
            for (h, v) in wl:
                eng.wait_ge(h, v)
            fn(eng).then_inc(sems[sem_name], 16)
        self.ops["G"].append(cl)
        return tok

    def barrier(self, engs="TVSGQ"):
        assert not self.t_dirty, "barrier with un-inc'd T ops"
        snap = [(s, self.cnt[s]) for s in self.sem_names]
        for ek in engs:
            own = self.eng_sem.get(ek)
            wl = self._wait_list(ek, [(s, v) for (s, v) in snap if s != own])
            if not wl:
                continue

            def cl(eng, wl=wl):
                for (h, v) in wl:
                    eng.wait_ge(h, v)
            self.ops[ek].append(cl)

    def mm_group(self, bk, out_ap, pairs, waits=()):
        w = list(waits) + self.bank_r[bk] + [self.bank_w[bk]]
        n = len(pairs)
        tok = None
        for i, (l, r) in enumerate(pairs):
            tok = self.op(
                "T",
                (lambda e, l=l, r=r, st=(i == 0), sp=(i == n - 1), o=out_ap:
                 e.matmul(o, l, r, start=st, stop=sp)),
                waits=w if i == 0 else (), inc=(i == n - 1))
        self.bank_w[bk] = tok
        self.bank_r[bk] = []
        return tok

    def transpose_to(self, bk, out_ap, in_ap, ident, waits=()):
        w = list(waits) + self.bank_r[bk] + [self.bank_w[bk]]
        tok = self.op("T", (lambda e, o=out_ap, i=in_ap, id_=ident:
                            e.transpose(o, i, id_)), waits=w, inc=True)
        self.bank_w[bk] = tok
        self.bank_r[bk] = []
        return tok

    def evac(self, ek, bks, fn, waits=()):
        if isinstance(bks, int):
            bks = [bks]
        w = list(waits) + [self.bank_w[bk] for bk in bks]
        tok = self.op(ek, fn, waits=w, inc=True)
        for bk in bks:
            self.bank_r[bk].append(tok)
        return tok

    def finish(self, out_tokens=()):
        snap = [(s, self.cnt[s]) for s in self.sem_names]
        wl = self._wait_list("Q", list(out_tokens) + snap)

        def cl(eng, wl=wl):
            for (h, v) in wl:
                eng.wait_ge(h, v)
        self.ops["Q"].append(cl)
        with self.nc.Block() as block:
            for ek, reg in (("T", block.tensor), ("V", block.vector),
                            ("S", block.scalar), ("G", block.gpsimd),
                            ("Q", block.sync)):
                ops = self.ops[ek]

                def runner(eng, ops=ops):
                    for fn in ops:
                        fn(eng)
                reg(runner)



class Env:
    def __init__(self, b: Bld, stack, cfg):
        nc = b.nc
        self.cfg = cfg
        D, T, H, FF, NE = cfg["D"], cfg["T"], cfg["H"], cfg["FF"], cfg["NE"]
        DC = D // 128
        DCT = DC * T
        TH = T // 2
        KC = T // 128
        VON = KC * H * 65
        WELS = DC * D
        FHW = min(2048, FF)
        FC = FF // 128
        r = max(DCT, VON, 5 * T + 2048, 2 * WELS + 256, min(16, FC) * TH,
                8 * FHW, 4 * TH + 2 * FHW + FC * 128, 8 * 2 * D, 16384)
        self.RSZ = r
        self.R0, self.R1, self.R2, self.R3 = 0, r, 2 * r, 2 * r + DCT
        self.R4 = 3 * r + DCT
        self.R4_els = max(4 * T, 4 * D, KC * 512, 2 * D + 3 * 1024 + 128, 2048)
        self.a1_els = self.R4 + self.R4_els
        self.a2_els = max(DCT, 2 * WELS,
                          8 * TH + 2 * DC * max(FF // 8, 128) + 4096,
                          8 * TH + 4 * D, 8192)
        self.A1 = stack.enter_context(nc.sbuf_tensor("A1", [128, self.a1_els], BF16))
        self.A2 = stack.enter_context(nc.sbuf_tensor("A2", [128, self.a2_els], BF16))
        sb = lambda name, shape, dt: stack.enter_context(nc.sbuf_tensor(name, shape, dt))
        self.ident = sb("c_ident", [128, 128], F32)
        self.identb = sb("c_identb", [128, 128], BF16)
        self.ones_row = sb("c_ones_row", [1, 1024], BF16)
        self.ones_row_f = sb("c_ones_row_f", [1, 128], F32)
        self.ones_col_bf = sb("c_ones_col_bf", [128, 1], BF16)
        self.ones_col_f = sb("c_ones_col_f", [128, 1], F32)
        self.stair = sb("c_stair", [128, 896], BF16)
        self.bias = sb("c_bias", [1, 2048], BF16)
        self.rows = sb("c_rows", [1, 2112], F32)
        self.rows_bf = sb("c_rows_bf", [1, 1024], BF16)
        self.invbc = sb("c_invbc", [64, 512], BF16)
        self.smalls = sb("c_smalls", [128, 64], F32)
        self.colbf = sb("c_colbf", [128, 16], BF16)
        self.rwt = sb("c_rwt", [128, DC * NE], BF16)
        self.ustrict = sb("c_ustrict", [128, 128], F32)
        self.iota = sb("c_iota", [128, T // 128], mybir.dt.int32)
        self.ifill = sb("c_ifill", [128, 8], mybir.dt.int32)
        self.rwtf = sb("c_rwtf", [128, DC * NE], F32)
        self.cw = sb("c_cw", [128, KC * NE], F32)
        self.locs = sb("c_locs", [128, 64], mybir.dt.int32)
        b.op("G", lambda e: e.memset(self.ident[:, :], 0.0))
        b.op("G", lambda e: e.affine_select(
            out=self.ident[:, :], in_=self.ident[:, :],
            compare_op=ALU.not_equal, fill=1.0,
            base=0, pattern=[[-1, 128]], channel_multiplier=1))
        b.op("G", lambda e: e.memset(self.identb[:, :], 0.0))
        b.op("G", lambda e: e.affine_select(
            out=self.identb[:, :], in_=self.identb[:, :],
            compare_op=ALU.not_equal, fill=1.0,
            base=0, pattern=[[-1, 128]], channel_multiplier=1))
        b.op("G", lambda e: e.memset(self.ones_row[:, :], 1.0))
        b.op("G", lambda e: e.memset(self.ones_row_f[:, :], 1.0))
        b.op("G", lambda e: e.memset(self.ones_col_bf[:, :], 1.0))
        b.op("G", lambda e: e.memset(self.ones_col_f[:, :], 1.0))
        b.op("G", lambda e: e.memset(self.stair[:, :], 1.0))
        b.op("G", lambda e: e.affine_select(
            out=self.stair[:, :], in_=self.stair[:, :],
            compare_op=ALU.is_ge, fill=0.0,
            base=-384, pattern=[[1, 896]], channel_multiplier=-1))
        b.op("G", lambda e: e.memset(self.ustrict[:, :], 1.0))
        b.op("G", lambda e: e.affine_select(
            out=self.ustrict[:, :], in_=self.ustrict[:, :],
            compare_op=ALU.is_ge, fill=0.0,
            base=-1, pattern=[[1, 128]], channel_multiplier=-1))
        b.op("G", lambda e: e.memset(self.ifill[:, :], T))
        b.op("G", lambda e: e.iota(self.iota[:, :], pattern=[[128, T // 128]],
                                   base=0, channel_multiplier=1))

    def a1(self, lo, n):
        assert 0 <= lo and lo + n <= self.a1_els, (lo, n, self.a1_els)
        return self.A1[:, lo:lo + n]

    def a1f(self, lo, n_f32):
        return self.a1(lo, 2 * n_f32).bitcast(F32)

    def a2(self, lo, n):
        assert 0 <= lo and lo + n <= self.a2_els, (lo, n, self.a2_els)
        return self.A2[:, lo:lo + n]

    def a2f(self, lo, n_f32):
        return self.a2(lo, 2 * n_f32).bitcast(F32)



def layernorm(b: Bld, ev: Env, x_dram, xlnf_dram=None, xtm_dram=None):
    cfg = ev.cfg
    D, T = cfg["D"], cfg["T"]
    DC = D // 128
    DCT = DC * T
    xbf = ev.a1(ev.R0, DCT)
    xf = ev.a1f(ev.R1, 2 * T)
    sqbf = ev.a1(ev.R1 + 4 * T, T)
    bc = ev.a1(ev.R1 + 5 * T, 1024)
    xln = ev.a1(ev.R2, DCT)
    tile_list = list(tiles(T, 512))
    nt = len(tile_list)
    assert nt <= 4
    inv_d = 1.0 / D

    pre = []
    for bk in range(8):
        pre += b.bank_r[bk] + [b.bank_w[bk]]
    buf_free = [None, None]
    last_mm = None
    for dc in range(DC):
        buf = dc % 2
        ld = b.dma("Q", xf[:, buf * T:(buf + 1) * T],
                   x_dram[dc * 128:(dc + 1) * 128, :], waits=[buf_free[buf]])
        cast = b.op("V", (lambda e, dc=dc, buf=buf: e.tensor_copy(
            xbf[:, dc * T:(dc + 1) * T], xf[:, buf * T:(buf + 1) * T])), waits=[ld])
        sq = b.op("S", (lambda e, dc=dc: e.activation(
            sqbf[:, :], xbf[:, dc * T:(dc + 1) * T], AF.Square)),
            waits=[cast, last_mm])
        buf_free[buf] = cast
        mm_w = [cast, sq] + (pre if dc == 0 else [])
        for ti, (o, w) in enumerate(tile_list):
            pa = b.bank(ti, 0, w, np_=1)
            xa_ = xbf[:, dc * T + o: dc * T + o + w]
            b.op("T", (lambda e, pa=pa, xa_=xa_, st=(dc == 0), sp=(dc == DC - 1):
                       e.matmul(pa, ev.ones_col_bf[:, :], xa_, start=st, stop=sp)),
                 waits=mm_w if ti == 0 else ())
            pb = b.bank(4 + ti, 0, w, np_=1)
            sa = sqbf[:, o:o + w]
            last_mm = b.op("T", (lambda e, pb=pb, sa=sa, st=(dc == 0),
                                 sp=(dc == DC - 1):
                                 e.matmul(pb, ev.ones_col_bf[:, :], sa,
                                          start=st, stop=sp)),
                           inc=(ti == nt - 1))
    for bk in range(8):
        b.bank_w[bk] = last_mm
        b.bank_r[bk] = []

    rows, rows_bf = ev.rows, ev.rows_bf
    prev_asm_st = [[] for _ in range(4)]
    asmb_st = [[], []]
    slot_rd = [list(buf_free), list(buf_free)]
    slot_st = [list(buf_free), list(buf_free)]
    for ti, (o, w) in enumerate(tile_list):
        s0 = rows[0:1, 0:w]
        s1 = rows[0:1, 512:512 + w]
        s2 = rows[0:1, 1024:1024 + w]
        rbf = rows_bf[0:1, 0:w]
        mbf = rows_bf[0:1, 512:512 + w]
        b.evac("V", ti, (lambda e, ti=ti, w=w, s0=s0: e.tensor_scalar_mul(
            s0, b.bank(ti, 0, w, np_=1), inv_d)))
        b.evac("V", 4 + ti, (lambda e, ti=ti, w=w, s1=s1: e.tensor_scalar_mul(
            s1, b.bank(4 + ti, 0, w, np_=1), inv_d)))
        b.op("V", lambda e, s0=s0, s2=s2: e.tensor_mul(s2, s0, s0))
        b.op("V", lambda e, s1=s1, s2=s2: e.tensor_sub(s1, s1, s2))
        tve = b.op("V", lambda e, s1=s1: e.tensor_scalar_add(s1, s1, LN_EPS))
        tsq = b.op("S", (lambda e, s1=s1, s2=s2: e.activation(s2, s1, AF.Sqrt)),
                   waits=[tve])
        trec = b.op("V", (lambda e, s1=s1, s2=s2: e.reciprocal(s1, s2)),
                    waits=[tsq])
        bk_r, bk_m = ti, 4 + ti
        if xlnf_dram is None:
            b.op("V", lambda e, s0=s0, s1=s1, s2=s2: e.tensor_mul(s2, s0, s1))
            b.op("V", lambda e, s1=s1, rbf=rbf: e.tensor_copy(rbf, s1))
            cv = b.op("V", lambda e, s2=s2, mbf=mbf: e.tensor_copy(mbf, s2))
            b.mm_group(bk_r, b.bank(bk_r, 0, w),
                       [(ev.ones_row[0:1, 0:128], rbf)], waits=[cv])
            b.mm_group(bk_m, b.bank(bk_m, 0, w),
                       [(ev.ones_row[0:1, 0:128], mbf)], waits=[cv])
            b.evac("V", bk_r, (lambda e, w=w, bk_r=bk_r: e.tensor_copy(
                bc[:, 0:w], b.bank(bk_r, 0, w))))
            b.evac("V", bk_m, (lambda e, w=w, bk_m=bk_m: e.tensor_copy(
                bc[:, 512:512 + w], b.bank(bk_m, 0, w))))
            for dc in range(DC):
                xo = xln[:, dc * T + o: dc * T + o + w]
                xi = xbf[:, dc * T + o: dc * T + o + w]
                b.op("V", (lambda e, xo=xo, xi=xi, w=w: e.tensor_mul(
                    xo, xi, bc[:, 0:w])))
                b.op("V", (lambda e, xo=xo, w=w: e.tensor_sub(
                    xo, xo, bc[:, 512:512 + w])))
        else:
            b.mm_group(bk_r, b.bank(bk_r, 0, w),
                       [(ev.ones_row_f[0:1, 0:128], s1)], waits=[trec])
            b.mm_group(bk_m, b.bank(bk_m, 0, w),
                       [(ev.ones_row_f[0:1, 0:128], s0)], waits=[trec])
            xft = [ev.a1f(ev.R1, 512), ev.a1f(ev.R1 + 1024, 512)]
            xlt = [ev.a1f(ev.R1 + 2048, 512), ev.a1f(ev.R1 + 3072, 512)]
            asm = ev.a1f(ev.R1 + 4096, 4096)
            bcf = ev.a1f(ev.R1 + 12288, 1024)
            b.evac("V", bk_r, (lambda e, w=w, bk_r=bk_r: e.tensor_copy(
                bcf[:, 0:w], b.bank(bk_r, 0, w))))
            cvb = b.evac("V", bk_m, (lambda e, w=w, bk_m=bk_m: e.tensor_copy(
                bcf[:, 512:512 + w], b.bank(bk_m, 0, w))))
            slot_rd[0] = slot_rd[0] + [cvb]
            slot_rd[1] = slot_rd[1] + [cvb]
            asm_toks = [[] for _ in range(4)]
            for dc in range(DC):
                par = dc % 2
                ldx = b.dma("Q", xft[par][:, 0:w],
                            x_dram[dc * 128:(dc + 1) * 128, o:o + w],
                            waits=slot_rd[par])
                b.op("V", (lambda e, par=par, w=w:
                           e.tensor_sub(xlt[par][:, 0:w], xft[par][:, 0:w],
                                        bcf[:, 512:512 + w])),
                     waits=[ldx] + slot_st[par])
                t2 = b.op("V", (lambda e, par=par, w=w:
                                e.tensor_mul(xlt[par][:, 0:w], xlt[par][:, 0:w],
                                             bcf[:, 0:w])))
                slot_rd[par] = [t2]
                xo = xln[:, dc * T + o: dc * T + o + w]
                b.op("V", (lambda e, xo=xo, par=par, w=w: e.tensor_copy(
                    xo, xlt[par][:, 0:w])))
                slot_st[par] = [b.dma(
                    "Q", xlnf_dram[dc * 128:(dc + 1) * 128, o:o + w],
                    xlt[par][:, 0:w], waits=[t2])]
                if xtm_dram is not None:
                    for j in range(w // 128):
                        bt = bk_r if j % 2 == 0 else bk_m
                        src = xlt[par][:, j * 128: j * 128 + 128]
                        b.transpose_to(bt, b.bank(bt, 0, 128), src,
                                       ev.ident[:, :],
                                       waits=[t2] + prev_asm_st[j])
                        dsta = asm[:, j * 1024 + dc * 128:
                                   j * 1024 + dc * 128 + 128]
                        asm_toks[j].append(b.evac("V", bt, (
                            lambda e, dsta=dsta, bt=bt: e.tensor_copy(
                                dsta, b.bank(bt, 0, 128)))))
                        slot_rd[par] = slot_rd[par] + [asm_toks[j][-1]]
            if xtm_dram is not None:
                asmb = ev.a1(ev.R1 + 14336, 2048)
                for j in range(w // 128):
                    jp = j % 2
                    cvt = b.op("V", (lambda e, j=j, jp=jp: e.tensor_copy(
                        asmb[:, jp * 1024: jp * 1024 + D],
                        asm[:, j * 1024: j * 1024 + D])),
                        waits=asm_toks[j] + asmb_st[jp])
                    prev_asm_st[j] = [cvt]
                    asmb_st[jp] = [b.dma(
                        "Q", xtm_dram[o + j * 128: o + j * 128 + 128, :],
                        asmb[:, jp * 1024: jp * 1024 + D], waits=[cvt])]



def pre_transpose(b: Bld, ev: Env, x_tm_dram, x_fm_dram):
    cfg = ev.cfg
    D, T = cfg["D"], cfg["T"]
    DC = D // 128
    xin = ev.a1f(ev.R0, 2 * D)
    xtr = ev.a1f(ev.R1, 2 * DC * 128)
    buf_free = [None, None]
    out_free = [None, None]
    for tc in range(T // 128):
        buf = tc % 2
        ld = b.dma("Q", xin[:, buf * D:(buf + 1) * D],
                   x_tm_dram[tc * 128:(tc + 1) * 128, :], waits=[buf_free[buf]])
        cp_toks = []
        for dc in range(DC):
            bk = 4 + (dc % 2)
            src = xin[:, buf * D + dc * 128: buf * D + dc * 128 + 128]
            dst = xtr[:, buf * DC * 128 + dc * 128: buf * DC * 128 + dc * 128 + 128]
            b.transpose_to(bk, b.bank(bk, 0, 128), src, ev.ident[:, :], waits=[ld])
            cp = b.evac("V", bk, (lambda e, dst=dst, bk=bk: e.tensor_copy(
                dst, b.bank(bk, 0, 128))), waits=[out_free[buf]])
            cp_toks.append(cp)
        st = b.dma("Q",
                   x_fm_dram[:, tc * 128:(tc + 1) * 128].rearrange(
                       "(c p) n -> p c n", p=128),
                   xtr[:, buf * DC * 128:(buf + 1) * DC * 128].rearrange(
                       "p (c n) -> p c n", n=128),
                   waits=cp_toks)
        buf_free[buf] = cp_toks[-1]
        out_free[buf] = st



def proj_fm(b: Bld, ev: Env, xln, wsl, out, scale=1.0, bias_row=None, wait0=()):
    cfg = ev.cfg
    D, T = cfg["D"], cfg["T"]
    DC = D // 128
    rot = [0]
    last = []
    for ic in range(DC):
        for (o, w) in tiles(T, 512):
            bk = rot[0] % 4
            rot[0] += 1
            pairs = [(wsl(dc, ic), xln[:, dc * T + o: dc * T + o + w])
                     for dc in range(DC)]
            if bias_row is not None:
                pairs.append((bias_row[0:1, ic * 128: ic * 128 + 128],
                              ev.ones_row[0:1, 0:w]))
            b.mm_group(bk, b.bank(bk, 0, w), pairs, waits=wait0)
            dst = out[:, ic * T + o: ic * T + o + w]
            if scale != 1.0:
                last.append(b.evac("V", bk, (lambda e, dst=dst, bk=bk, w=w:
                                             e.tensor_scalar_mul(
                                                 dst, b.bank(bk, 0, w), scale))))
            else:
                last.append(b.evac("S", bk, (lambda e, dst=dst, bk=bk, w=w:
                                             e.activation(dst, b.bank(bk, 0, w),
                                                          AF.Copy))))
    return last


def proj_tm_vones(b: Bld, ev: Env, xln, wsl_r, v_ones, bias_row=None, wait0=()):
    cfg = ev.cfg
    D, T, H, HD = cfg["D"], cfg["T"], cfg["H"], cfg["HD"]
    assert HD == 64
    DC = D // 128
    rot = [0]
    last = []
    for tc in range(T // 128):
        for (io, iw) in tiles(D, 512):
            bk = rot[0] % 4
            rot[0] += 1
            pairs = [(xln[:, dc * T + tc * 128: dc * T + tc * 128 + 128],
                      wsl_r(dc, io, iw)) for dc in range(DC)]
            if bias_row is not None:
                pairs.append((ev.ones_row[0:1, 0:128],
                              bias_row[0:1, io: io + iw]))
            b.mm_group(bk, b.bank(bk, 0, iw), pairs, waits=wait0)
            nh = iw // 64
            base = tc * H * 65 + (io // 64) * 65
            dst = v_ones[:, base: base + nh * 65].rearrange(
                "p (h c) -> p h c", h=nh)[:, :, 0:64]
            last.append(b.evac("V", bk, (lambda e, dst=dst, bk=bk, iw=iw, nh=nh:
                                         e.tensor_copy(dst, b.bank(bk, 0, iw).rearrange(
                                             "p (h d) -> p h d", h=nh)))))
    return last



def attention_core(b: Bld, ev: Env, sc_lh, sc_rh, v_ones, y, causal, ptil2):
    cfg = ev.cfg
    T, H, HD = cfg["T"], cfg["H"], cfg["HD"]
    KC = T // 128
    ptils = [ev.a1(ev.R4, KC * 512), ptil2]
    inv_f = ev.rows[0:1, 1536:2048]
    inv_bf = ev.rows_bf[0:1, 0:512]
    items = [(h, go, gw) for h in range(H) for (go, gw) in tiles(T, 512)]
    exp_q = {}
    av_toks = {}

    def stage_a(i):
        h, go, gw = items[i]
        ptil = ptils[i % 2]
        nkc = (go + gw) // 128 if causal else KC
        war = [av_toks.get(i - 2)]
        toks = []
        for kc in range(nkc):
            bk = kc % 6
            b.mm_group(bk, b.bank(bk, 0, gw),
                       [(sc_lh(h, kc), sc_rh(h, go, gw))])
            pdst = ptil[:, kc * 512: kc * 512 + gw]
            tok = b.evac("S", bk, (lambda e, pdst=pdst, bk=bk, gw=gw:
                                   e.activation(pdst, b.bank(bk, 0, gw),
                                                AF.Exp)),
                         waits=war if kc == 0 else ())
            if causal and kc * 128 + 127 >= go:
                off = 384 + go - kc * 128
                assert 0 <= off and off + gw <= 896, (off, gw)
                msl = ev.stair[:, off:off + gw]
                tok = b.op("V", (lambda e, pdst=pdst, msl=msl:
                                 e.tensor_mul(pdst, pdst, msl)),
                           waits=[tok])
            toks.append(tok)
        exp_q[i] = toks

    def stage_b(i):
        h, go, gw = items[i]
        ptil = ptils[i % 2]
        nkc = (go + gw) // 128 if causal else KC
        pairs = []
        for kc in range(nkc):
            vsl = v_ones[:, (kc * H + h) * 65: (kc * H + h) * 65 + 65]
            pairs.append((vsl, ptil[:, kc * 512: kc * 512 + gw]))
        av_toks[i] = b.mm_group(6, b.bank(6, 0, gw, np_=65), pairs,
                                waits=exp_q.pop(i))
        b.evac("V", 6, (lambda e, gw=gw: e.reciprocal(
            inv_f[0:1, 0:gw], b.bank(6, 0, gw, np_=1, p0=64))))
        cv = b.op("V", lambda e, gw=gw: e.tensor_copy(inv_bf[0:1, 0:gw],
                                                      inv_f[0:1, 0:gw]))
        b.mm_group(7, b.bank(7, 0, gw, np_=64),
                   [(ev.ones_row[0:1, 0:64], inv_bf[0:1, 0:gw])], waits=[cv])
        b.evac("V", 7, (lambda e, gw=gw: e.tensor_copy(
            ev.invbc[0:64, 0:gw], b.bank(7, 0, gw, np_=64))))
        dc = (h * HD) // 128
        p0 = (h * HD) % 128
        ydst = y[p0:p0 + 64, dc * T + go: dc * T + go + gw]
        b.evac("V", 6, (lambda e, ydst=ydst, gw=gw: e.tensor_mul(
            ydst, b.bank(6, 0, gw, np_=64), ev.invbc[0:64, 0:gw])))

    stage_a(0)
    for i in range(1, len(items)):
        stage_a(i)
        stage_b(i - 1)
    stage_b(len(items) - 1)



def out_proj_residual(b: Bld, ev: Env, y, wsl, x_in, x_out, bias_row=None,
                      wait0=()):
    cfg = ev.cfg
    D, T = cfg["D"], cfg["T"]
    DC = D // 128
    xi = ev.a1f(ev.R4, T)
    xo = ev.a1f(ev.R4 + 2 * T, T)
    st_tok = [None]
    tt_last = [None]
    rot = [0]
    for dco in range(DC):
        ld = b.dma("Q", xi[:, :], x_in[dco * 128:(dco + 1) * 128, :],
                   waits=[tt_last[0]])
        tts = []
        for (o, w) in tiles(T, 512):
            bk = rot[0] % 4
            rot[0] += 1
            pairs = [(wsl(dc, dco), y[:, dc * T + o: dc * T + o + w])
                     for dc in range(DC)]
            if bias_row is not None:
                pairs.append((bias_row[0:1, dco * 128: dco * 128 + 128],
                              ev.ones_row[0:1, 0:w]))
            b.mm_group(bk, b.bank(bk, 0, w), pairs, waits=wait0)
            dst = xo[:, o:o + w]
            src1 = xi[:, o:o + w]
            tts.append(b.evac("V", bk, (lambda e, dst=dst, src1=src1, bk=bk, w=w:
                                        e.tensor_add(dst, b.bank(bk, 0, w), src1)),
                              waits=[ld, st_tok[0]]))
        tt_last[0] = tts[-1]
        st_tok[0] = b.dma("Q", x_out[dco * 128:(dco + 1) * 128, :], xo[:, :],
                          waits=tts)
    return st_tok[0]



def block_l0attn(b: Bld, ev: Env, x_in, x_out, W):
    cfg = ev.cfg
    D, T, H, HD = cfg["D"], cfg["T"], cfg["H"], cfg["HD"]
    DC = D // 128
    DCT = DC * T
    KC = T // 128
    WELS = DC * D
    wA = ev.a2(0, WELS)
    wB = ev.a2(WELS, WELS)
    xln = ev.a1(ev.R2, DCT)
    q = ev.a1(ev.R0, DCT)
    k = ev.a1(ev.R3, DCT)
    v_ones = ev.a1(ev.R1, KC * H * 65)
    y = ev.a1(ev.R2, DCT)

    def wload(dst, nm, waits=()):
        return b.dma("G", dst.rearrange("p (c n) -> p c n", c=DC),
                     W[nm].rearrange("(c p) n -> p c n", p=128), waits=waits)

    ldq = wload(wA, "Wq")
    ldk = wload(wB, "Wk")
    layernorm(b, ev, x_in)
    b.barrier()
    wsl = lambda Wt: (lambda dc, ic: Wt[:, dc * D + ic * 128: dc * D + ic * 128 + 128])
    q_toks = proj_fm(b, ev, xln, wsl(wA), q, scale=1.0 / float(np.sqrt(HD)),
                     wait0=[ldq])
    ldv = wload(wA, "Wv", waits=[q_toks[-1]])
    k_toks = proj_fm(b, ev, xln, wsl(wB), k, wait0=[ldk])
    ldo = wload(wB, "Wo", waits=[k_toks[-1]])
    von_ms = b.op("G", lambda e: e.memset(v_ones[:, :], 1.0))
    wslA_r = lambda dc, io, iw: wA[:, dc * D + io: dc * D + io + iw]
    proj_tm_vones(b, ev, xln, wslA_r, v_ones, wait0=[ldv, von_ms])
    b.barrier()

    hpc = 128 // HD
    sc_lh = lambda h, kc: k[(h % hpc) * HD:(h % hpc) * HD + HD,
                            (h // hpc) * T + kc * 128:(h // hpc) * T + kc * 128 + 128]
    sc_rh = lambda h, go, gw: q[(h % hpc) * HD:(h % hpc) * HD + HD,
                                (h // hpc) * T + go:(h // hpc) * T + go + gw]
    if 2 * WELS + KC * 512 <= ev.a2_els:
        ptil2 = ev.a2(2 * WELS, KC * 512)
    else:
        assert KC * 512 <= WELS
        ptil2 = ev.a2(0, KC * 512)
    attention_core(b, ev, sc_lh, sc_rh, v_ones, y, True, ptil2)
    b.barrier()
    wslo = lambda dc, dco: wB[:, dc * D + dco * 128: dc * D + dco * 128 + 128]
    st = out_proj_residual(b, ev, y, wslo, x_in, x_out, wait0=[ldo])
    if getattr(ev, "l0f_W", None) is not None:
        sc_stage = ev.a2f(ev.a2_els - 4096, 2048)
        t1 = tern_scale_pass(b, ev, ev.l0f_W["W1"], cfg["FF"], D, None,
                             thrcol=0, bank=6, eng="V", stage=sc_stage,
                             cw_=1024)
        tern_scale_pass(b, ev, ev.l0f_W["W2"], D, cfg["FF"], None,
                        thrcol=1, bank=7, war=t1, eng="V", stage=sc_stage,
                        cw_=1024)
    b.barrier()
    return st



def tern_scale_pass(b: Bld, ev: Env, Wdram, nrows, ncols, stage_off,
                    thrcol, bank=6, war=None, eng="V", stage=None, cw_=1024):
    cw_ = min(cw_, ncols)
    if stage is None:
        stage = ev.a1f(stage_off, 2 * cw_)
    asum = ev.smalls[:, 60:61]
    tmp = ev.smalls[:, 61:62]
    thr = ev.smalls[:, thrcol:thrcol + 1]
    zz = b.op(eng, lambda e: e.memset(asum, 0.0))
    buf_free = [war, war]
    ad = zz
    i = 0
    for g in range(nrows // 128):
        for (co, cn) in tiles(ncols, cw_):
            buf = i % 2
            i += 1
            ld = b.dma("Q", stage[:, buf * cw_: buf * cw_ + cn],
                       Wdram[g * 128:(g + 1) * 128, co:co + cn],
                       waits=[buf_free[buf]])
            rd = b.op(eng, (lambda e, buf=buf, cn=cn: e.reduce_sum(
                tmp, stage[:, buf * cw_: buf * cw_ + cn], axis=AX.X,
                apply_absolute_value=True)), waits=[ld, zz])
            ad = b.op(eng, lambda e: e.tensor_add(asum, asum, tmp))
            buf_free[buf] = rd
    b.mm_group(bank, b.bank(bank, 0, 1, np_=1),
               [(asum, ev.ones_col_f[:, :])], waits=[ad])
    srow = ev.rows[0:1, 2048:2049]
    b.evac(eng, bank, (lambda e: e.tensor_scalar(
        srow, b.bank(bank, 0, 1, np_=1), 1.0 / float(nrows * ncols), 1e-8,
        ALU.mult, ALU.max)))
    hrow = ev.rows[0:1, 2049:2050]
    cv = b.op(eng, lambda e: e.tensor_scalar_mul(hrow, srow, 0.5))
    b.mm_group(bank, b.bank(bank, 0, 1),
               [(ev.ones_row_f[0:1, 0:128], hrow)], waits=[cv])
    return b.evac(eng, bank, (lambda e: e.tensor_copy(thr, b.bank(bank, 0, 1))))


def ternarize(b: Bld, ev: Env, src_f32, thrcol, sgn, ab, tern_f, waits=()):
    t1 = b.op("S", (lambda e: e.activation(sgn, src_f32, AF.Sign)), waits=waits)
    t2 = b.op("S", (lambda e: e.activation(ab, src_f32, AF.Abs)))
    thr = ev.smalls[:, thrcol:thrcol + 1]
    b.op("V", (lambda e: e.tensor_scalar(ab, ab, thr, None, ALU.is_gt)),
         waits=[t2])
    return b.op("V", (lambda e: e.tensor_mul(tern_f, ab, sgn)), waits=[t1])



def block_l0ffn(b: Bld, ev: Env, x_in, x_out, W):
    cfg = ev.cfg
    D, T, FF = cfg["D"], cfg["T"], cfg["FF"]
    DC = D // 128
    DCT = DC * T
    TH = T // 2
    FC = FF // 128
    FHW = min(1024, FF)
    GR = DC * 128
    r = ev.RSZ
    xln = ev.a1(ev.R2, DCT)
    w1c = ev.w1c_dram
    w2c = ev.w2c_dram

    def h_sl(fc, o, w):
        if fc < 16:
            return ev.a1(ev.R1 + fc * TH + o, w)
        if fc < 24:
            return ev.a1(ev.R4 + (fc - 16) * TH + o, w)
        return ev.a2((fc - 24) * TH + o, w)

    if getattr(ev, "l0f_W", None) is None:
        t1 = tern_scale_pass(b, ev, W["W1"], FF, D, ev.R3, thrcol=0)
        tern_scale_pass(b, ev, W["W2"], D, FF, ev.R3, thrcol=1, bank=7,
                        war=t1)
    layernorm(b, ev, x_in)
    b.barrier()

    b1row = W["b1"].rearrange("(a d) -> a d", a=1)
    b2row = W["b2"].rearrange("(a d) -> a d", a=1)

    stage = ev.a1f(ev.R3, 2 * D)
    sgn = ev.a1f(ev.R3 + 4 * D, D)
    ab = ev.a1f(ev.R3 + 6 * D, D)
    ternf = ev.a1f(ev.R3 + 8 * D, D)
    gbuf = [ev.a1(ev.R0, GR), ev.a1(ev.R0 + GR, GR)]
    buf_free = [None, None]
    gb_free = [None, None]
    w1_cache = []
    prev_tk = [None]
    for fc in range(FC):
        buf = fc % 2
        ld = b.dma("Q", stage[:, buf * D:(buf + 1) * D],
                   W["W1"][fc * 128:(fc + 1) * 128, :], waits=[buf_free[buf]])
        tk = ternarize(b, ev, stage[:, buf * D:(buf + 1) * D], 0,
                       sgn, ab, ternf, waits=[ld, prev_tk[0]])
        prev_tk[0] = tk
        buf_free[buf] = tk
        cps = []
        for dc in range(DC):
            bk = 4 + (dc % 2)
            b.transpose_to(bk, b.bank(bk, 0, 128),
                           ternf[:, dc * 128: dc * 128 + 128],
                           ev.ident[:, :], waits=[tk])
            dst = gbuf[buf][:, dc * 128: dc * 128 + 128]
            cps.append(b.evac("V", bk, (lambda e, dst=dst, bk=bk:
                                        e.tensor_copy(dst, b.bank(bk, 0, 128))),
                              waits=[gb_free[buf]]))
        w1_cache.append(b.dma("Q", w1c[fc], gbuf[buf], waits=cps))
        gb_free[buf] = w1_cache[-1]

    ring = [ev.a1(ev.R3 + 10 * D + i * GR, GR) for i in range(3)]

    def h_steps(th, war):
        rot = [0]
        b1_half = [0]
        b1_tok = [b.dma("G", ev.bias[0:1, 0:min(FF, 2048)],
                        b1row[0:1, 0:min(FF, 2048)], waits=war)]
        slot_mm = [list(war), list(war), list(war)]
        h_toks = [None]
        for fc in range(FC):
            sl = fc % 3
            ld = b.dma("Q", ring[sl], w1c[fc],
                       waits=slot_mm[sl] + [w1_cache[fc]])
            if fc * 128 >= (b1_half[0] + 1) * 2048:
                b1_half[0] += 1
                b1_tok[0] = b.dma(
                    "G", ev.bias[0:1, 0:2048],
                    b1row[0:1, b1_half[0] * 2048:(b1_half[0] + 1) * 2048],
                    waits=[h_toks[-1]])
            gts = []
            for (o, w) in tiles(TH, 512):
                bk = rot[0] % 4
                rot[0] += 1
                pairs = [(ring[sl][:, dc * 128: dc * 128 + 128],
                          xln[:, dc * T + th * TH + o: dc * T + th * TH + o + w])
                         for dc in range(DC)]
                boff = fc * 128 - b1_half[0] * 2048
                pairs.append((ev.bias[0:1, boff:boff + 128],
                              ev.ones_row[0:1, 0:w]))
                gts.append(b.mm_group(bk, b.bank(bk, 0, w), pairs,
                                      waits=[ld, b1_tok[0]]))
                dst = h_sl(fc, o, w)
                h_toks.append(b.evac("S", bk, (lambda e, dst=dst, bk=bk, w=w:
                                               e.activation(dst, b.bank(bk, 0, w),
                                                            AF.Gelu))))
            slot_mm[sl] = gts
            yield h_toks[-1]

    w2stage = ev.a1f(ev.R0, 2 * FHW)
    ternf2 = ev.a1f(ev.R0 + 4 * FHW, FHW)
    sgn2 = ev.a1f(ev.R0 + 6 * FHW, FHW)
    ab2 = ev.a1f(ev.R3, FHW)
    strip = ev.a2(8 * TH, FC * 128)

    def w2_steps(war):
        buf2 = [list(war), list(war)]
        prev2 = [prev_tk[0]] + war
        strip_free = [None]
        cache = []
        for dcg in range(DC):
            w2_done = []
            for fh, fhw in tiles(FF, FHW):
                pbuf = (fh // FHW) % 2
                ld = b.dma("Q", w2stage[:, pbuf * FHW: pbuf * FHW + fhw],
                           W["W2"][dcg * 128:(dcg + 1) * 128, fh:fh + fhw],
                           waits=buf2[pbuf] + prev2)
                tk = ternarize(b, ev, w2stage[:, pbuf * FHW:pbuf * FHW + fhw],
                               1, sgn2[:, 0:fhw], ab2[:, 0:fhw],
                               ternf2[:, 0:fhw], waits=[ld] + prev2)
                prev2 = [tk]
                buf2[pbuf] = [tk]
                for j in range(fhw // 128):
                    fc = (fh + j * 128) // 128
                    bk = 4 + (j % 2)
                    b.transpose_to(bk, b.bank(bk, 0, 128),
                                   ternf2[:, j * 128: j * 128 + 128],
                                   ev.ident[:, :], waits=[tk])
                    dst = strip[:, fc * 128: fc * 128 + 128]
                    w2_done.append(b.evac("V", bk, (lambda e, dst=dst, bk=bk:
                                                    e.tensor_copy(dst, b.bank(bk, 0, 128))),
                                          waits=[strip_free[0]]))
            strip_free[0] = b.dma(
                "Q", w2c[:, dcg * 128:(dcg + 1) * 128].rearrange(
                    "(c p) n -> p c n", p=128),
                strip.rearrange("p (c n) -> p c n", n=128), waits=w2_done)
            cache.append(strip_free[0])
            yield cache

    g_w2 = w2_steps(w1_cache[-2:])
    g_h0 = h_steps(0, [])
    w2_cache = []
    h_last = None
    hs_per = max(1, FC // DC)
    try:
        for dcg in range(DC):
            w2_cache = next(g_w2)
            for _ in range(hs_per):
                h_last = next(g_h0, h_last)
    except StopIteration:
        pass
    for t in g_h0:
        h_last = t

    st_tok = [None]
    dh_list = list(tiles(D, 512))
    for th in range(2):
        if th == 1:
            gen = h_steps(1, [bar_tok[0]] if False else [])
            for t in gen:
                h_last = t
        ldb2 = b.dma("G", ev.bias[0:1, 0:D], b2row, waits=[h_last])
        w2th = ev.a1(ev.R0, FC * 512)
        xi = ev.a1f(ev.R3, TH)
        xo = ev.a1f(ev.R3 + 2 * TH, TH)
        last_dh_mm = [h_last]
        tt_last = [h_last]
        for (dho, dhw) in dh_list:
            ldh = []
            for fg in range(0, FC, 8):
                fgn = min(8, FC - fg)
                ldh.append(b.dma(
                    "Q",
                    w2th[:, fg * dhw: (fg + fgn) * dhw].rearrange(
                        "p (c n) -> p c n", c=fgn),
                    w2c[fg * 128:(fg + fgn) * 128, dho:dho + dhw].rearrange(
                        "(c p) n -> p c n", p=128),
                    waits=[last_dh_mm[0]] + w2_cache))
            for dcg in range(dho // 128, (dho + dhw) // 128):
                jj = dcg - dho // 128
                ld_x = b.dma("Q", xi[:, 0:TH],
                             x_in[dcg * 128:(dcg + 1) * 128,
                                  th * TH:(th + 1) * TH],
                             waits=[tt_last[0]])
                tts = []
                for (o, w) in tiles(TH, 512):
                    bk = (dcg * 2 + o // 512) % 4
                    pairs = [(w2th[:, fc * dhw + jj * 128:
                                   fc * dhw + jj * 128 + 128], h_sl(fc, o, w))
                             for fc in range(FC)]
                    pairs.append((ev.bias[0:1, dcg * 128: dcg * 128 + 128],
                                  ev.ones_row[0:1, 0:w]))
                    gt = b.mm_group(bk, b.bank(bk, 0, w), pairs,
                                    waits=ldh + [ldb2])
                    last_dh_mm[0] = gt
                    dst = xo[:, o:o + w]
                    src1 = xi[:, o:o + w]
                    tts.append(b.evac("V", bk, (lambda e, dst=dst, src1=src1,
                                                bk=bk, w=w: e.tensor_add(
                        dst, b.bank(bk, 0, w), src1)), waits=[ld_x, st_tok[0]]))
                tt_last[0] = tts[-1]
                st_tok[0] = b.dma("Q", x_out[dcg * 128:(dcg + 1) * 128,
                                             th * TH:(th + 1) * TH],
                                  xo[:, 0:TH], waits=tts)
        b.barrier()
    return st_tok[0]



def block_l1attn(b: Bld, ev: Env, x_in, x_out, W):
    cfg = ev.cfg
    D, T, H, HD, R = cfg["D"], cfg["T"], cfg["H"], cfg["HD"], cfg["R"]
    assert HD == 64
    DC = D // 128
    DCT = DC * T
    KC = T // 128
    WELS = DC * D
    HH = H // 2
    xln = ev.a1(ev.R2, DCT)
    qk = ev.a1(ev.R0, DCT)
    v_ones = ev.a1(ev.R0, KC * H * 65)
    zq = ev.a1(ev.R3, HH * T)
    zk = ev.a2(0, HH * T)
    y = ev.a1(ev.R2, DCT)
    wA = ev.a1(ev.R1, WELS)
    wB = ev.a1(ev.R1 + WELS, WELS)
    tiny = ev.R1 + 2 * WELS
    vt_bf = ev.a1(tiny, 64)
    ut_bf = ev.a1(tiny + 64, 64)
    vut2 = ev.a1(tiny + 128, 64)
    v2 = ev.a1(tiny + 192, 32)
    v2q = ev.a1(tiny + 224, 32)
    stage = ev.a1f(ev.R4, 2 * D)
    gran_bf = ev.a1(ev.R4, 2 * D)

    layernorm(b, ev, x_in)
    b.barrier()

    pads = ev.a1f(ev.R4, 128)
    prev_ld, prev_tr = None, None
    for (nm, dst) in (("U", ut_bf), ("V", vt_bf)):
        ms = b.op("G", (lambda e: e.memset(pads[:, :], 0.0)),
                  waits=[prev_ld, prev_tr])
        ld = b.dma("Q", pads[0:HD, 0:R], W[nm], waits=[ms])
        tr = b.transpose_to(6, b.bank(6, 0, 128), pads[:, :], ev.ident[:, :],
                            waits=[ld, ms])
        cp = b.evac("V", 6, (lambda e, dst=dst: e.tensor_copy(
            dst[0:R, 0:HD], b.bank(6, 0, HD, np_=R))))
        prev_ld, prev_tr = ld, tr
    b.mm_group(7, b.bank(7, 0, HD, np_=HD),
               [(vt_bf[0:R, 0:HD], ut_bf[0:R, 0:HD])], waits=[cp])
    b.evac("V", 7, (lambda e: e.tensor_copy(vut2[0:64, :],
                                            b.bank(7, 0, HD, np_=HD))))
    b.evac("V", 7, (lambda e: e.tensor_copy(vut2[64:128, :],
                                            b.bank(7, 0, HD, np_=HD))))
    ldv2a = b.dma("G", v2[0:64, 0:R], W["V"])
    ldv2b = b.dma("G", v2[64:128, 0:R], W["V"])
    b.op("V", (lambda e: e.tensor_scalar_mul(v2q[:, :], v2[:, :],
                                             float(R) ** -0.5)),
         waits=[ldv2a, ldv2b])
    b.barrier()

    def wt_prep(nm, slot, war=None):
        toks = []
        buf_free = [war, war]
        for g in range(DC):
            buf = g % 2
            ld = b.dma("Q", stage[:, buf * D:(buf + 1) * D],
                       W[nm][g * 128:(g + 1) * 128, :], waits=[buf_free[buf]])
            for dc in range(DC):
                bk = 4 + (dc % 2)
                b.transpose_to(bk, b.bank(bk, 0, 128),
                               stage[:, buf * D + dc * 128: buf * D + dc * 128 + 128],
                               ev.ident[:, :], waits=[ld])
                dst = slot[:, dc * D + g * 128: dc * D + g * 128 + 128]
                toks.append(b.evac("V", bk, (lambda e, dst=dst, bk=bk:
                                             e.tensor_copy(dst, b.bank(bk, 0, 128)))))
            buf_free[buf] = toks[-1]
        return toks

    def z_proj(zdst, vsrc, src_toks):
        toks = []
        rot = [0]
        for h in range(H):
            for (o, w) in tiles(T, 512):
                bk = rot[0] % 4
                rot[0] += 1
                p0 = (h % 2) * 64
                lh = vsrc[p0:p0 + 64, 0:R]
                rh = qk[p0:p0 + 64, (h // 2) * T + o:(h // 2) * T + o + w]
                b.mm_group(bk, b.bank(bk, 0, w, np_=R), [(lh, rh)],
                           waits=[src_toks[-1]])
                dst = zdst[(h % 2) * 32:(h % 2) * 32 + 32,
                           (h // 2) * T + o:(h // 2) * T + o + w]
                toks.append(b.evac("V", bk, (lambda e, dst=dst, bk=bk, w=w:
                                             e.tensor_copy(dst, b.bank(bk, 0, w, np_=R)))))
        return toks

    wsl = lambda Wt: (lambda dc, ic: Wt[:, dc * D + ic * 128: dc * D + ic * 128 + 128])
    wtq = wt_prep("Wq", wA)
    ldbq = b.dma("G", ev.bias[0:1, 0:D], W["bq"].rearrange("(a d) -> a d", a=1))
    q_toks = proj_fm(b, ev, xln, wsl(wA), qk, bias_row=ev.bias,
                     wait0=[wtq[-1], ldbq])
    zq_toks = z_proj(zq, v2q, q_toks)
    wtk = wt_prep("Wk", wB, war=wtq[-1])
    ldbk = b.dma("G", ev.bias[0:1, 0:D], W["bk"].rearrange("(a d) -> a d", a=1),
                 waits=[q_toks[-1]])
    k_toks = proj_fm(b, ev, xln, wsl(wB), qk, bias_row=ev.bias,
                     wait0=[wtk[-1], ldbk, zq_toks[-1]])
    zk_toks = z_proj(zk, v2, k_toks)

    wveff_toks = [None]
    bufv = [None, None]
    for g in range(DC):
        buf = g % 2
        ld = b.dma("G", gran_bf[:, buf * D:(buf + 1) * D],
                   W["Wv"][g * 128:(g + 1) * 128, :],
                   waits=[bufv[buf], q_toks[-1], wtk[-1], zk_toks[-1]])
        for half in range(2):
            h_ = 2 * g + half
            for dc in range(DC):
                bk = dc % 4
                lh = gran_bf[half * 64:(half + 1) * 64,
                             buf * D + dc * 128: buf * D + dc * 128 + 128]
                rh = vut2[half * 64:(half + 1) * 64, :]
                b.mm_group(bk, b.bank(bk, 0, 64), [(lh, rh)], waits=[ld])
                dst = wA[:, dc * D + h_ * 64: dc * D + h_ * 64 + 64]
                wveff_toks.append(b.evac("V", bk, (lambda e, dst=dst, bk=bk:
                                                   e.tensor_copy(dst, b.bank(bk, 0, 64)))))
        bufv[buf] = wveff_toks[-1]

    ldbv = b.dma("G", ev.bias[0:1, 0:D], W["bv"].rearrange("(a d) -> a d", a=1),
                 waits=[k_toks[-1]])
    cb_toks = []
    for dcb in range(DC):
        b.mm_group(6, b.bank(6, 0, 1),
                   [(ev.bias[0:1, dcb * 128: dcb * 128 + 128],
                     ev.ones_row[0:1, 0:1])], waits=[ldbv])
        dst = ev.colbf[:, dcb:dcb + 1]
        cb_toks.append(b.evac("V", 6, (lambda e, dst=dst: e.tensor_copy(
            dst, b.bank(6, 0, 1)))))
    bve_toks = []
    for h_ in range(H):
        p0 = (h_ % 2) * 64
        lh = ev.colbf[p0:p0 + 64, h_ // 2: h_ // 2 + 1]
        rh = vut2[p0:p0 + 64, :]
        b.mm_group(7, b.bank(7, 0, 64, np_=1), [(lh, rh)], waits=[cb_toks[-1]])
        dst = ev.bias[0:1, h_ * 64: h_ * 64 + 64]
        bve_toks.append(b.evac("V", 7, (lambda e, dst=dst: e.tensor_copy(
            dst, b.bank(7, 0, 64, np_=1)))))

    von_ms = b.op("G", lambda e: e.memset(v_ones[:, :], 1.0),
                  waits=[zk_toks[-1]])
    wslA_r = lambda dc, io, iw: wA[:, dc * D + io: dc * D + io + iw]
    v_toks = proj_tm_vones(b, ev, xln, wslA_r, v_ones, bias_row=ev.bias,
                           wait0=[wveff_toks[-1], bve_toks[-1], von_ms])
    b.barrier()

    wto = wt_prep("Wo", wB)
    ldbo = b.dma("G", ev.bias[0:1, 0:D], W["bo"].rearrange("(a d) -> a d", a=1),
                 waits=[v_toks[-1]])
    b.barrier()

    sc_lh = lambda h, kc: zk[(h % 2) * 32:(h % 2) * 32 + 32,
                             (h // 2) * T + kc * 128:(h // 2) * T + kc * 128 + 128]
    sc_rh = lambda h, go, gw: zq[(h % 2) * 32:(h % 2) * 32 + 32,
                                 (h // 2) * T + go:(h // 2) * T + go + gw]
    if KC * 512 <= WELS:
        ptil2 = ev.a1(ev.R1, KC * 512)
    else:
        ptil2 = ev.a1(ev.R1 + 2 * WELS + 256, KC * 512)
    attention_core(b, ev, sc_lh, sc_rh, v_ones, y, False, ptil2)
    b.barrier()
    wslo = lambda dc, dco: wB[:, dc * D + dco * 128: dc * D + dco * 128 + 128]
    st = out_proj_residual(b, ev, y, wslo, x_in, x_out, bias_row=ev.bias,
                           wait0=[wto[-1], ldbo])
    b.barrier()
    return st



def block_moe(b: Bld, ev: Env, x_in, out_tm, W):
    cfg = ev.cfg
    D, T, FF, NE = cfg["D"], cfg["T"], cfg["FF"], cfg["NE"]
    DC = D // 128
    DCT = DC * T
    TH = T // 2
    FC = FF // 128
    FQ = FF // 4
    KC = T // 128
    TCH = TH // 128
    xln = ev.a1(ev.R2, DCT)
    wq_els = DC * FQ

    def acc_sl(tc):
        if tc < 8:
            return ev.a1f(ev.R0 + tc * 2 * D, D)
        return ev.a1f(ev.R1 + (tc - 8) * 2 * D, D)

    def h_sl(fc, o, w):
        if fc < 16:
            return ev.a1(ev.R3 + fc * TH + o, w)
        if fc < 24:
            return ev.a1(ev.R4 + (fc - 16) * TH + o, w)
        return ev.a2((fc - 24) * TH + o, w)

    wdring = ev.a2(8 * TH + 2 * DC * max(FF // 8, 128), 4096)
    xlnf = ev.xlnf_dram

    layernorm(b, ev, x_in, xlnf_dram=xlnf)
    b.barrier()

    pads = ev.a1f(ev.R4, D)
    asb = ev.a1f(ev.R4 + 2 * D, 512)
    xlf = [ev.a1f(ev.R4 + 2 * D + 1024, 512),
           ev.a1f(ev.R4 + 2 * D + 2048, 512)]
    ms = b.op("G", lambda e: e.memset(pads[:, :], 0.0))
    msa = b.op("G", lambda e: e.memset(asb[:, :], 0.0))
    ldr = b.dma("Q", pads[0:NE, 0:D], W["router_w"], waits=[ms])
    rw_toks = []
    for dc in range(DC):
        bk = 4 + (dc % 2)
        b.transpose_to(bk, b.bank(bk, 0, 128),
                       pads[:, dc * 128: dc * 128 + 128], ev.ident[:, :],
                       waits=[ldr, ms])
        dstf = ev.rwtf[:, dc * NE: dc * NE + NE]
        rw_toks.append(b.evac("V", bk, (lambda e, dstf=dstf, bk=bk:
                                        e.tensor_copy(dstf, b.bank(bk, 0, NE)))))
    sm = ev.smalls
    slot_rd = [None, None]
    for ti, (o, wti) in enumerate(tiles(T, 512)):
        Abk = 2 + (ti % 2)
        pre_w = b.bank_r[Abk] + [b.bank_w[Abk], rw_toks[-1]]
        for dc in range(DC):
            par = dc % 2
            ldx = b.dma("Q", xlf[par][:, 0:wti],
                        xlnf[dc * 128:(dc + 1) * 128, o:o + wti],
                        waits=[slot_rd[par]])
            lh = ev.rwtf[:, dc * NE: dc * NE + NE]
            slot_rd[par] = b.op(
                "T", (lambda e, lh=lh, par=par, wti=wti, Abk=Abk,
                      st=(dc == 0), sp=(dc == DC - 1):
                      e.matmul(b.bank(Abk, 0, wti, np_=NE), lh,
                               xlf[par][:, 0:wti], start=st, stop=sp)),
                waits=[ldx] + (pre_w if dc == 0 else []), inc=True)
        b.bank_w[Abk] = slot_rd[(DC - 1) % 2]
        b.bank_r[Abk] = []
        cpA = b.evac("V", Abk, (lambda e, wti=wti, Abk=Abk: e.tensor_copy(
            asb[0:NE, 0:wti], b.bank(Abk, 0, wti, np_=NE))), waits=[msa])
        for j in range(wti // 128):
            tc = (o + j * 128) // 128
            bk = 6 + (j % 2)
            b.transpose_to(bk, b.bank(bk, 0, 128),
                           asb[:, j * 128: j * 128 + 128], ev.ident[:, :],
                           waits=[cpA])
            _moe_top2(b, ev, bk, tc)
    b.barrier()

    z0 = b.op("G", lambda e: e.memset(ev.a1f(ev.R0, 8 * D)[:, :], 0.0))
    z1 = b.op("G", lambda e: e.memset(ev.a1f(ev.R1, 8 * D)[:, :], 0.0))
    b.barrier()

    FF8 = max(FF // 8, 128)
    NPH = FF // FF8
    w8els = DC * FF8
    wq8 = [ev.a2(8 * TH, w8els), ev.a2(8 * TH + w8els, w8els)]
    slot_free = [None, None]
    wq_tok = [None]
    for e_ in range(NE):
        for th in range(2):
            for gate in (True, False):
                nm = "w_gate" if gate else "w_up"
                for ph in range(NPH):
                    slot = ph % 2
                    ld = b.dma("G", wq8[slot].rearrange("p (c n) -> p c n", c=DC),
                               W[nm][e_].rearrange("(c p) n -> p c n", p=128)[
                                   :, :, ph * FF8:(ph + 1) * FF8],
                               waits=[slot_free[slot]])
                    rot = [0]
                    tok = None
                    for fcl in range(FF8 // 128):
                        fc = ph * (FF8 // 128) + fcl
                        for (o, w) in tiles(TH, 512):
                            bk = rot[0] % 2
                            rot[0] += 1
                            pairs = [(wq8[slot][:, dc * FF8 + fcl * 128:
                                                dc * FF8 + fcl * 128 + 128],
                                      xln[:, dc * T + th * TH + o:
                                          dc * T + th * TH + o + w])
                                     for dc in range(DC)]
                            b.mm_group(bk, b.bank(bk, 0, w), pairs, waits=[ld])
                            dst = h_sl(fc, o, w)
                            if gate:
                                tok = b.evac("S", bk, (lambda e, dst=dst, bk=bk,
                                                       w=w: e.activation(
                                    dst, b.bank(bk, 0, w), AF.Silu)))
                            else:
                                tok = b.evac("V", bk, (lambda e, dst=dst, bk=bk,
                                                       w=w: e.tensor_mul(
                                    dst, b.bank(bk, 0, w), dst)))
                    slot_free[slot] = tok
                    wq_tok[0] = tok
            ring_tok = [None, None]
            for dh in range(2):
                dhw = min(512, D - dh * 512)
                if dhw <= 0:
                    break
                first_done = [False] * TCH
                last_mm = [None]
                for fj in range(FC // 4):
                    slot = fj % 2
                    ld = b.dma("G",
                               wdring[:, slot * 2048: slot * 2048 + 4 * dhw].rearrange(
                                   "p (c n) -> p c n", c=4),
                               W["w_down"][e_].rearrange("(c p) n -> p c n", p=128)[
                                   :, 4 * fj: 4 * fj + 4, dh * 512: dh * 512 + dhw],
                               waits=[ring_tok[slot]])
                    for fc4 in range(4):
                        fc = fj * 4 + fc4
                        for tc in range(TCH):
                            lh = h_sl(fc, tc * 128, 128)
                            rh = wdring[:, slot * 2048 + fc4 * dhw:
                                        slot * 2048 + fc4 * dhw + dhw]
                            st_ = (fc == 0)
                            sp_ = (fc == FC - 1)
                            w_ = [ld, wq_tok[0]]
                            if st_:
                                w_ += b.bank_r[tc] + [b.bank_w[tc]]
                            tok = b.op("T", (lambda e, lh=lh, rh=rh, st_=st_,
                                             sp_=sp_, tc=tc, dhw=dhw:
                                             e.matmul(b.bank(tc, 0, dhw), lh, rh,
                                                      start=st_, stop=sp_)),
                                       waits=w_,
                                       inc=(fc4 == 3 and tc == TCH - 1))
                    ring_tok[slot] = tok
                    last_mm[0] = tok
                for tc in range(TCH):
                    b.bank_w[tc] = last_mm[0]
                    b.bank_r[tc] = []
                    gtc = th * TCH + tc
                    a_sl = acc_sl(gtc)[:, dh * 512: dh * 512 + dhw]
                    c_sl = ev.cw[:, gtc * NE + e_: gtc * NE + e_ + 1]
                    b.evac("V", tc, (lambda e, a_sl=a_sl, c_sl=c_sl, tc=tc,
                                     dhw=dhw: e.scalar_tensor_tensor(
                        a_sl, b.bank(tc, 0, dhw), c_sl, a_sl,
                        ALU.mult, ALU.add)), waits=[z0, z1])
    b.barrier()

    def xfull(dc):
        if dc < 4:
            return ev.a1f(ev.R3 + dc * 2 * T, T)
        if dc < 6:
            return ev.a1f(ev.R4 + (dc - 4) * 2 * T, T)
        return ev.a2f((dc - 6) * 2 * T, T)

    ld_toks = []
    for dc in range(DC):
        ld_toks.append(b.dma("Q", xfull(dc)[:, :],
                             x_in[dc * 128:(dc + 1) * 128, :]))
    obuf = ev.a2f(8 * TH, 2 * D)
    st_tok = [None, None]
    for tc in range(KC):
        par = tc % 2
        tts = []
        for dc in range(DC):
            bk = 4 + (dc % 2)
            b.transpose_to(bk, b.bank(bk, 0, 128),
                           xfull(dc)[:, tc * 128: tc * 128 + 128],
                           ev.ident[:, :], waits=[ld_toks[dc]])
            dst = obuf[:, par * D + dc * 128: par * D + dc * 128 + 128]
            a_sl = acc_sl(tc)[:, dc * 128: dc * 128 + 128]
            tts.append(b.evac("V", bk, (lambda e, dst=dst, a_sl=a_sl, bk=bk:
                                        e.tensor_add(dst, b.bank(bk, 0, 128), a_sl)),
                              waits=[st_tok[par]]))
        st_tok[par] = b.dma("Q", out_tm[tc * 128:(tc + 1) * 128, :],
                            obuf[:, par * D:(par + 1) * D], waits=tts)
    b.barrier()
    return [t for t in st_tok if t]


def _moe_top2(b, ev, bk, tc):
    NE = ev.cfg["NE"]
    sm = ev.smalls
    if True:
        b.evac("V", bk, (lambda e, bk=bk: e.reduce_max(
            sm[:, 0:1], b.bank(bk, 0, NE), axis=AX.X)))
        tv = b.evac("V", bk, (lambda e, bk=bk: e.tensor_scalar(
            sm[:, 8:8 + NE], b.bank(bk, 0, NE), sm[:, 0:1], None, ALU.subtract)))
        ts = b.op("S", (lambda e: e.activation(sm[:, 16:16 + NE],
                                               sm[:, 8:8 + NE], AF.Exp)),
                  waits=[tv])
        b.op("V", (lambda e: e.reduce_sum(sm[:, 1:2], sm[:, 16:16 + NE],
                                          axis=AX.X)), waits=[ts])
        b.op("V", lambda e: e.reciprocal(sm[:, 2:3], sm[:, 1:2]))
        b.op("V", lambda e: e.tensor_scalar(
            sm[:, 24:24 + NE], sm[:, 16:16 + NE], sm[:, 2:3], None, ALU.mult))
        b.op("V", lambda e: e.reduce_max(sm[:, 3:4], sm[:, 24:24 + NE], axis=AX.X))
        b.op("V", lambda e: e.tensor_scalar(
            sm[:, 32:32 + NE], sm[:, 24:24 + NE], sm[:, 3:4], 1e9,
            ALU.is_equal, ALU.mult))
        b.op("V", lambda e: e.tensor_tensor(
            sm[:, 40:40 + NE], sm[:, 24:24 + NE], sm[:, 32:32 + NE], ALU.subtract))
        b.op("V", lambda e: e.reduce_max(sm[:, 4:5], sm[:, 40:40 + NE], axis=AX.X))
        b.op("V", lambda e: e.tensor_scalar(
            sm[:, 48:48 + NE], sm[:, 24:24 + NE], sm[:, 4:5], None, ALU.is_ge))
        b.op("V", lambda e: e.tensor_tensor(
            sm[:, 48:48 + NE], sm[:, 48:48 + NE], sm[:, 24:24 + NE], ALU.mult))
        b.op("V", lambda e: e.tensor_add(sm[:, 5:6], sm[:, 3:4], sm[:, 4:5]))
        b.op("V", lambda e: e.reciprocal(sm[:, 6:7], sm[:, 5:6]))
        dst = ev.cw[:, tc * NE:(tc + 1) * NE]
        b.op("V", (lambda e, dst=dst: e.tensor_scalar(
            dst, sm[:, 48:48 + NE], sm[:, 6:7], None, ALU.mult)))



def block_moe_sparse(b: Bld, ev: Env, x_in, out_tm, W):
    cfg = ev.cfg
    D, T, FF, NE = cfg["D"], cfg["T"], cfg["FF"], cfg["NE"]
    DC = D // 128
    DCT = DC * T
    C = moe_cap(T)
    JCN = C // 128
    FC = FF // 128
    KC = T // 128
    xln = ev.a1(ev.R2, DCT)
    xlnf = ev.xlnf_dram
    xtm = ev.xtm_dram
    yed = ev.yed_dram
    t1d = ev.t1_dram
    t2d = ev.t2_dram
    idxd = ev.idx_dram
    cwd = ev.cw_dram

    def h_sl(fc, o, w):
        if fc < 16:
            return ev.a1(ev.R3 + fc * C + o, w)
        if fc < 24:
            return ev.a1(ev.R4 + (fc - 16) * C + o, w)
        return ev.a2((fc - 24) * C + o, w)

    wdring = ev.a2(8 * C + 2 * DC * max(FF // 8, 128), 4096)

    bregs = {}

    def _mk_regs(e):
        bregs["C"] = e.to_reg(C - 1)
        bregs["T"] = e.to_reg(T - 1)
        bregs["Y"] = e.to_reg(NE * C + 127)
        return e.engine_nop()
    b.op("G", _mk_regs)

    layernorm(b, ev, x_in, xlnf_dram=xlnf, xtm_dram=xtm)
    b.barrier()

    pads = ev.a1f(ev.R4, D)
    asb = ev.a1f(ev.R4 + 2 * D, 512)
    xlf = [ev.a1f(ev.R4 + 2 * D + 1024, 512),
           ev.a1f(ev.R4 + 2 * D + 2048, 512)]
    ms = b.op("G", lambda e: e.memset(pads[:, :], 0.0))
    msa = b.op("G", lambda e: e.memset(asb[:, :], 0.0))
    ldr = b.dma("Q", pads[0:NE, 0:D], W["router_w"], waits=[ms])
    rw_toks = []
    for dc in range(DC):
        bk = 4 + (dc % 2)
        b.transpose_to(bk, b.bank(bk, 0, 128),
                       pads[:, dc * 128: dc * 128 + 128], ev.ident[:, :],
                       waits=[ldr, ms])
        dstf = ev.rwtf[:, dc * NE: dc * NE + NE]
        rw_toks.append(b.evac("V", bk, (lambda e, dstf=dstf, bk=bk:
                                        e.tensor_copy(dstf, b.bank(bk, 0, NE)))))
    sm = ev.smalls
    prefill_toks = []
    for e_ in range(NE):
        prefill_toks.append(b.dma(
            "Q", idxd[e_].rearrange("(p a) -> p a", p=128),
            ev.ifill[:, 0:C // 128]))
    carry = ev.rows[0:1, 2050:2050 + NE]
    bx = ev.R4 + 2 * D + 3072
    mk = ev.a1f(bx, NE)
    rank = ev.a1f(bx + 16, NE)
    rib = [ev.locs[:, 32 + 8 * i: 32 + 8 * i + NE] for i in range(4)]
    cz = b.op("V", lambda e: e.memset(carry, 0.0))
    st = {"carry_t": cz, "idx": [],
          "ri_rd": [[] for _ in range(4)]}
    idx_sc_toks = st["idx"]

    def idx_stage(tc):
        xpar = tc % 4
        cwt = ev.cw[:, tc * NE:(tc + 1) * NE]
        tmk = b.op("V", (lambda e, cwt=cwt: e.tensor_single_scalar(
            mk, cwt, 0.0, ALU.is_gt)))
        bkr = tc % 2
        b.mm_group(bkr, b.bank(bkr, 0, NE),
                   [(ev.ustrict[:, :], mk),
                    (ev.ones_row_f[0:1, 0:128], carry)],
                   waits=[tmk, st["carry_t"]])
        b.evac("V", bkr, (lambda e, bkr=bkr: e.tensor_copy(
            rank, b.bank(bkr, 0, NE))))
        bkc = 4
        b.mm_group(bkc, b.bank(bkc, 0, NE, np_=1),
                   [(ev.ones_col_f[:, :], mk)], waits=[tmk])
        st["carry_t"] = b.evac("V", bkc, (lambda e: e.tensor_tensor(
            carry, carry, b.bank(bkc, 0, NE, np_=1), ALU.add)))
        b.op("V", lambda e: e.tensor_single_scalar(rank, rank, 65536.0,
                                                   ALU.subtract))
        b.op("V", lambda e: e.tensor_tensor(rank, rank, mk, ALU.mult))
        b.op("V", lambda e: e.tensor_single_scalar(rank, rank, 65536.0, ALU.add))
        ri = rib[xpar]
        cvi = b.op("V", (lambda e, ri=ri: e.tensor_copy(ri, rank)),
                   waits=st["ri_rd"][xpar])
        st["ri_rd"][xpar] = []
        for e_ in range(NE):
            tk = b.idma(
                (lambda e, e_=e_, tc=tc, ri=ri: e.indirect_dma_start(
                    idxd.rearrange("e (c a) -> (e c) a", a=1),
                    IndirectOffsetOnAxis(ri[:, e_:e_ + 1], 0),
                    ev.iota[:, tc:tc + 1], None,
                    element_offset=e_ * C,
                    bounds_check=bregs["C"], oob_is_err=False)),
                waits=[cvi] + prefill_toks)
            st["idx"].append(tk)
            st["ri_rd"][xpar].append(tk)

    slot_rd = [None, None]
    for ti, (o, wti) in enumerate(tiles(T, 512)):
        Abk = 2 + (ti % 2)
        pre_w = b.bank_r[Abk] + [b.bank_w[Abk], rw_toks[-1]]
        for dc in range(DC):
            par = dc % 2
            ldx = b.dma("Q", xlf[par][:, 0:wti],
                        xlnf[dc * 128:(dc + 1) * 128, o:o + wti],
                        waits=[slot_rd[par]])
            lh = ev.rwtf[:, dc * NE: dc * NE + NE]
            slot_rd[par] = b.op(
                "T", (lambda e, lh=lh, par=par, wti=wti, Abk=Abk,
                      st=(dc == 0), sp=(dc == DC - 1):
                      e.matmul(b.bank(Abk, 0, wti, np_=NE), lh,
                               xlf[par][:, 0:wti], start=st, stop=sp)),
                waits=[ldx] + (pre_w if dc == 0 else []), inc=True)
        b.bank_w[Abk] = slot_rd[(DC - 1) % 2]
        b.bank_r[Abk] = []
        cpA = b.evac("V", Abk, (lambda e, wti=wti, Abk=Abk: e.tensor_copy(
            asb[0:NE, 0:wti], b.bank(Abk, 0, wti, np_=NE))), waits=[msa])
        for j in range(wti // 128):
            tc = (o + j * 128) // 128
            bk = 6 + (j % 2)
            b.transpose_to(bk, b.bank(bk, 0, 128),
                           asb[:, j * 128: j * 128 + 128], ev.ident[:, :],
                           waits=[cpA])
            _moe_top2(b, ev, bk, tc)
            idx_stage(tc)

    st_cw = b.dma("Q", cwd.rearrange("(c p) n -> p c n", p=128),
                  ev.cw.rearrange("p (c n) -> p c n", n=NE),
                  waits=idx_sc_toks[-1:])
    locs = ev.locs
    NJ = NE * JCN
    lfil = b.op("G", lambda e: e.memset(locs[:, 0:KC], NE * C),
                waits=idx_sc_toks[-NE:])
    zbuf = ev.a1f(ev.R1 + 8192, 1024)
    zz = b.op("V", lambda e: e.memset(zbuf[:, :], 0.0))
    zpad = b.dma("Q", yed[NE * C:NE * C + 128, :],
                 zbuf.bitcast(BF16)[:, 0:D], waits=[zz])
    pf1 = b.dma("Q", t1d.rearrange("(p c) a -> p (c a)", p=128),
                locs[:, 0:KC], waits=[lfil])
    pf2 = b.dma("Q", t2d.rearrange("(p c) a -> p (c a)", p=128),
                locs[:, 0:KC], waits=[lfil])
    lio = b.op("G", lambda e: e.iota(locs[:, 0:NJ], pattern=[[128, NJ]],
                                     base=0, channel_multiplier=1),
               waits=[pf1, pf2] + idx_sc_toks[-NE:])
    i2sb = ev.smalls[:, 0:16].bitcast(mybir.dt.int32)
    tb_buf = {1: i2sb[:, 8:16], 2: i2sb[:, 0:8]}
    tb_prev = {1: [pf1, lio], 2: [pf2, lio]}

    def tb_round(k):
        for cid, tbl, e_ in ((2, t2d, k), (1, t1d, NE - 1 - k)):
            buf = tb_buf[cid]
            toks = []
            for jc in range(JCN):
                lde = b.dma("Q", buf[:, jc:jc + 1],
                            idxd[e_, jc * 128:(jc + 1) * 128].rearrange(
                                "(p a) -> p a", p=128),
                            waits=tb_prev[cid] + idx_sc_toks[-NE:])
                toks.append(b.idma(
                    (lambda e, tbl=tbl, buf=buf, jc=jc, e_=e_:
                     e.indirect_dma_start(
                         tbl[:, :],
                         IndirectOffsetOnAxis(buf[:, jc:jc + 1], 0),
                         locs[:, e_ * JCN + jc: e_ * JCN + jc + 1], None,
                         bounds_check=bregs["T"], oob_is_err=False)),
                    waits=[lde] + tb_prev[cid]))
            tb_prev[cid] = toks
    b.barrier()

    FF8 = max(FF // 8, 128)
    NPH = FF // FF8
    w8els = DC * FF8
    wq8 = [ev.a2(8 * C, w8els), ev.a2(8 * C + w8els, w8els)]
    xgt = [ev.a1(ev.R0, JCN * D), ev.a1(ev.R0 + 8192, JCN * D)]
    xgf = [ev.a1(ev.R1, DC * C), ev.a1(ev.R2, DC * C)]
    y_sb = ev.a1(ev.R1 + DC * C, 4 * 1024)
    cwcp = [ev.smalls[:, 24 + 8 * i: 24 + 8 * i + JCN] for i in range(4)]
    slot_free = [None, None]
    wq_tok = [None]
    y_war = [[] for _ in range(JCN)]
    up_w = idx_sc_toks[-NE:]
    xgt_war = [list(up_w), list(up_w)]
    fm_war = [[], []]
    cw_war = [list(up_w) for _ in range(4)]
    ld_x = {}
    ld_cw = {}
    trd = {}
    idx_sb = ev.colbf[:, 0:16].bitcast(mybir.dt.int32)
    idxp = [idx_sb, ev.smalls[:, 16:24].bitcast(mybir.dt.int32)]
    cwg = ev.smalls[:, 56:56 + NE]
    prev_cc = [None]
    STG = globals().get("DBG_MOE_STG", 99)
    NEXP = NE if DBG_MOE_NEXP is None else DBG_MOE_NEXP

    def load(e_):
        par, p4 = e_ % 2, e_ % 4
        xts, cts = [], []
        for jc in range(JCN):
            ldi = b.dma("Q", idxp[par][:, jc:jc + 1],
                        idxd[e_, jc * 128:(jc + 1) * 128].rearrange(
                            "(p a) -> p a", p=128),
                        waits=(xgt_war[par] + idx_sc_toks[-NE:]) if jc == 0
                        else ())
            gt = b.idma((lambda e, jc=jc, par=par: e.indirect_dma_start(
                xgt[par][:, jc * D: jc * D + D], None, xtm[:, :],
                IndirectOffsetOnAxis(idxp[par][:, jc:jc + 1], 0),
                bounds_check=bregs["T"], oob_is_err=False)),
                waits=[ldi] + (xgt_war[par] if jc == 0 else []))
            cg = b.idma((lambda e, jc=jc, par=par: e.indirect_dma_start(
                cwg, None, cwd[:, :],
                IndirectOffsetOnAxis(idxp[par][:, jc:jc + 1], 0),
                bounds_check=bregs["T"], oob_is_err=False)),
                waits=[gt, prev_cc[0], st_cw])
            cc = b.op("V", (lambda e, jc=jc, e_=e_, p4=p4: e.tensor_copy(
                cwcp[p4][:, jc:jc + 1], cwg[:, e_:e_ + 1])),
                waits=[cg] + (cw_war[p4] if jc == 0 else []))
            prev_cc[0] = cc
            xts.append(gt)
            cts.append(cc)
        ld_x[e_], ld_cw[e_] = xts, cts

    def xpose(e_):
        par = e_ % 2
        jlast = None
        for jc in range(JCN):
            for dc in range(DC):
                bt = 6 + (dc % 2)
                src = xgt[par][:, jc * D + dc * 128: jc * D + dc * 128 + 128]
                b.transpose_to(bt, b.bankb(bt, 0, 128), src, ev.identb[:, :],
                               waits=[ld_x[e_][jc]])
                dst = xgf[par][:, dc * C + jc * 128: dc * C + jc * 128 + 128]
                jlast = b.evac("V", bt, (lambda e, dst=dst, bt=bt:
                                         e.tensor_copy(dst, b.bankb(bt, 0, 128))),
                               waits=fm_war[par])
        trd[e_] = jlast
        xgt_war[par] = [jlast, ld_cw[e_][-1]]

    def compute(e_):
        par, p4 = e_ % 2, e_ % 4
        xg_fm = xgf[par]
        cwc = cwcp[p4]
        for gate in (True, False):
            nm = "w_gate" if gate else "w_up"
            for ph in range(NPH):
                slot = ph % 2
                ld = b.dma("G", wq8[slot].rearrange("p (c n) -> p c n", c=DC),
                           W[nm][e_].rearrange("(c p) n -> p c n", p=128)[
                               :, :, ph * FF8:(ph + 1) * FF8],
                           waits=[slot_free[slot]])
                rot = [0]
                tok = None
                for fcl in range(FF8 // 128):
                    fc = ph * (FF8 // 128) + fcl
                    for (o, w) in tiles(C, 512):
                        bk = rot[0] % 4
                        rot[0] += 1
                        pairs = [(wq8[slot][:, dc * FF8 + fcl * 128:
                                            dc * FF8 + fcl * 128 + 128],
                                  xg_fm[:, dc * C + o: dc * C + o + w])
                                 for dc in range(DC)]
                        b.mm_group(bk, b.bank(bk, 0, w), pairs,
                                   waits=[ld, trd[e_]])
                        dst = h_sl(fc, o, w)
                        if gate:
                            tok = b.evac("S", bk, (lambda e, dst=dst, bk=bk,
                                                   w=w: e.activation(
                                dst, b.bank(bk, 0, w), AF.Silu)))
                        else:
                            tok = b.evac("V", bk, (lambda e, dst=dst, bk=bk,
                                                   w=w: e.tensor_mul(
                                dst, b.bank(bk, 0, w), dst)))
                slot_free[slot] = tok
                wq_tok[0] = tok
        fm_war[par] = [wq_tok[0]]
        if STG < 5:
            return
        ring_tok = [None, None]
        ev_last = None
        for dh in range(2):
            dhw = min(512, D - dh * 512)
            if dhw <= 0:
                break
            last_mm = [None]
            for fj in range(FC // 4):
                slot = fj % 2
                ld = b.dma("G",
                           wdring[:, slot * 2048: slot * 2048 + 4 * dhw].rearrange(
                               "p (c n) -> p c n", c=4),
                           W["w_down"][e_].rearrange("(c p) n -> p c n", p=128)[
                               :, 4 * fj: 4 * fj + 4, dh * 512: dh * 512 + dhw],
                           waits=[ring_tok[slot]])
                for fc4 in range(4):
                    fc = fj * 4 + fc4
                    for jc in range(JCN):
                        lh = h_sl(fc, jc * 128, 128)
                        rh = wdring[:, slot * 2048 + fc4 * dhw:
                                    slot * 2048 + fc4 * dhw + dhw]
                        st_ = (fc == 0)
                        sp_ = (fc == FC - 1)
                        w_ = [ld, wq_tok[0]]
                        if st_:
                            w_ += b.bank_r[jc] + [b.bank_w[jc]]
                        tok = b.op("T", (lambda e, lh=lh, rh=rh, st_=st_,
                                         sp_=sp_, jc=jc, dhw=dhw:
                                         e.matmul(b.bank(jc, 0, dhw), lh, rh,
                                                  start=st_, stop=sp_)),
                                   waits=w_,
                                   inc=(fc4 == 3 and jc == JCN - 1))
                ring_tok[slot] = tok
                last_mm[0] = tok
            for jc in range(JCN):
                b.bank_w[jc] = last_mm[0]
                b.bank_r[jc] = []
                ydst = y_sb[:, jc * 512: jc * 512 + dhw]
                ev_t = b.evac("V", jc, (lambda e, ydst=ydst, jc=jc, dhw=dhw,
                                        cwc=cwc:
                                        e.tensor_scalar(ydst, b.bank(jc, 0, dhw),
                                                        cwc[:, jc:jc + 1], None,
                                                        ALU.mult)),
                              waits=y_war[jc] + [ld_cw[e_][jc]])
                ev_last = ev_t
                if STG < 6:
                    continue
                y_war[jc] = [b.dma(
                    "Q", yed[e_ * C + jc * 128: e_ * C + jc * 128 + 128,
                             dh * 512: dh * 512 + dhw],
                    ydst, waits=[ev_t])]
        cw_war[p4] = [ev_last]
        tb_round(e_)

    for k in range(min(2, NEXP)):
        load(k)
    if NEXP > 0:
        xpose(0)
    for e_ in range(NEXP):
        if e_ + 2 < NEXP:
            load(e_ + 2)
        if e_ + 1 < NEXP:
            xpose(e_ + 1)
        if STG < 4:
            continue
        compute(e_)
    b.barrier()

    def xfull(dc):
        if dc < 4:
            return ev.a1f(ev.R3 + dc * 2 * T, T)
        if dc < 6:
            return ev.a1f(ev.R4 + (dc - 4) * 2 * T, T)
        return ev.a2f((dc - 6) * 2 * T, T)

    ld_toks = []
    for dc in range(DC):
        ld_toks.append(b.dma("Q", xfull(dc)[:, :],
                             x_in[dc * 128:(dc + 1) * 128, :]))
    obuf = ev.a2f(max(8 * C, max(DC - 6, 0) * 2 * T), 2 * D)
    abuf = [ev.a1(ev.R0, D), ev.a1(ev.R0 + D, D)]
    bbuf = [ev.a1(ev.R0 + 2 * D, D), ev.a1(ev.R0 + 3 * D, D)]
    gsum = [ev.a1f(ev.R0 + 4 * D, D), ev.a1f(ev.R0 + 6 * D, D)]
    st_tok = [None, None]
    tc_war = [[], []]
    for tc in range(KC):
        par = tc % 2
        ldt = b.dma("Q", idx_sb[:, 2 * par: 2 * par + 1],
                    t1d[tc * 128:(tc + 1) * 128, :], waits=tc_war[par])
        ldt2 = b.dma("Q", idx_sb[:, 2 * par + 1: 2 * par + 2],
                     t2d[tc * 128:(tc + 1) * 128, :], waits=tc_war[par])
        g1 = b.idma((lambda e, par=par: e.indirect_dma_start(
            abuf[par][:, :], None, yed[:, :],
            IndirectOffsetOnAxis(idx_sb[:, 2 * par: 2 * par + 1], 0),
            bounds_check=bregs["Y"], oob_is_err=False)),
            waits=[ldt, st_tok[par]])
        g2 = b.idma((lambda e, par=par: e.indirect_dma_start(
            bbuf[par][:, :], None, yed[:, :],
            IndirectOffsetOnAxis(idx_sb[:, 2 * par + 1: 2 * par + 2], 0),
            bounds_check=bregs["Y"], oob_is_err=False)),
            waits=[ldt2])
        g12 = b.op("V", (lambda e, par=par: e.tensor_add(
            gsum[par][:, :], abuf[par][:, :], bbuf[par][:, :])),
            waits=[g1, g2])
        tc_war[par] = [g12]
        tts = []
        for dc in range(DC):
            bk = 4 + (dc % 2)
            b.transpose_to(bk, b.bank(bk, 0, 128),
                           xfull(dc)[:, tc * 128: tc * 128 + 128],
                           ev.ident[:, :], waits=[ld_toks[dc]])
            dst = obuf[:, par * D + dc * 128: par * D + dc * 128 + 128]
            a_sl = gsum[par][:, dc * 128: dc * 128 + 128]
            tts.append(b.evac("V", bk, (lambda e, dst=dst, a_sl=a_sl, bk=bk:
                                        e.tensor_add(dst, b.bank(bk, 0, 128), a_sl)),
                              waits=[g12, st_tok[par]]))
        st_tok[par] = b.dma("Q", out_tm[tc * 128:(tc + 1) * 128, :],
                            obuf[:, par * D:(par + 1) * D], waits=tts)
    b.barrier()
    return [t for t in st_tok if t]



BLOCK_WEIGHTS = {
    "l0a": [("Wq", "L0_attn_Wq"), ("Wk", "L0_attn_Wk"), ("Wv", "L0_attn_Wv"),
            ("Wo", "L0_attn_Wo")],
    "l0f": [("W1", "L0_ffn_W1"), ("b1", "L0_ffn_b1"), ("W2", "L0_ffn_W2"),
            ("b2", "L0_ffn_b2")],
    "l1a": [("Wq", "L1_attn_Wq"), ("bq", "L1_attn_bq"), ("Wk", "L1_attn_Wk"),
            ("bk", "L1_attn_bk"), ("Wv", "L1_attn_Wv"), ("bv", "L1_attn_bv"),
            ("Wo", "L1_attn_Wo"), ("bo", "L1_attn_bo"), ("U", "L1_attn_U"),
            ("V", "L1_attn_V")],
    "moe": [("router_w", "L1_ffn_router_w"), ("w_gate", "L1_ffn_w_gate"),
            ("w_up", "L1_ffn_w_up"), ("w_down", "L1_ffn_w_down")],
}

WSHAPES = {
    "L0_attn_Wq": ("D", "D"), "L0_attn_Wk": ("D", "D"),
    "L0_attn_Wv": ("D", "D"), "L0_attn_Wo": ("D", "D"),
    "L0_ffn_W1": ("FF", "D"), "L0_ffn_b1": ("FF",),
    "L0_ffn_W2": ("D", "FF"), "L0_ffn_b2": ("D",),
    "L1_attn_Wq": ("D", "D"), "L1_attn_bq": ("D",),
    "L1_attn_Wk": ("D", "D"), "L1_attn_bk": ("D",),
    "L1_attn_Wv": ("D", "D"), "L1_attn_bv": ("D",),
    "L1_attn_Wo": ("D", "D"), "L1_attn_bo": ("D",),
    "L1_attn_U": ("HD", "R"), "L1_attn_V": ("HD", "R"),
    "L1_ffn_router_w": ("NE", "D"),
    "L1_ffn_w_gate": ("NE", "D", "FF"), "L1_ffn_w_up": ("NE", "D", "FF"),
    "L1_ffn_w_down": ("NE", "FF", "D"),
}

BLOCK_FN = {}


def build_program(cfg, parts):
    nc = bass.Bass("TRN2", target_bir_lowering=False)
    stack = contextlib.ExitStack()
    b = Bld(nc, stack)
    ev = Env(b, stack, cfg)
    b.barrier()
    D, T = cfg["D"], cfg["T"]

    ev.xlnf_dram = None
    ev.xtm_dram = ev.yed_dram = ev.idx_dram = ev.cw_dram = None
    ev.t1_dram = ev.t2_dram = None
    if "moe" in parts:
        ev.xlnf_dram = nc.dram_tensor("xlnf", [D, T], F32, kind="Internal").ap()
        if SPARSE_MOE:
            NE = cfg["NE"]
            _k = "ExternalOutput" if globals().get("DBG_SPARSE") else "Internal"
            CP = moe_cap(T)
            ev.xtm_dram = nc.dram_tensor("xtm", [T, D], BF16, kind=_k).ap()
            ev.yed_dram = nc.dram_tensor("yed", [NE * CP + 128, D], BF16,
                                         kind=_k).ap()
            ev.t1_dram = nc.dram_tensor("t1d", [T, 1], mybir.dt.int32,
                                        kind=_k).ap()
            ev.t2_dram = nc.dram_tensor("t2d", [T, 1], mybir.dt.int32,
                                        kind=_k).ap()
            ev.idx_dram = nc.dram_tensor("idxd", [NE, CP], mybir.dt.int32,
                                         kind=_k).ap()
            ev.cw_dram = nc.dram_tensor("cwd", [T, NE], F32, kind=_k).ap()

    needed = sorted({nm for p in parts for (_, nm) in BLOCK_WEIGHTS[p]})
    wdict = {nm: nc.dram_tensor(nm, [cfg[s] for s in WSHAPES[nm]], F32,
                                kind="ExternalInput").ap() for nm in needed}

    first, last = parts[0], parts[-1]
    if first == "l0a":
        x0 = nc.dram_tensor("x0", [T, D], F32, kind="ExternalInput").ap()
        xa = nc.dram_tensor("xa", [D, T], F32, kind="Internal").ap()
    else:
        xa = nc.dram_tensor("xin", [D, T], F32, kind="ExternalInput").ap()
    xb = nc.dram_tensor("xb", [D, T], F32, kind="Internal").ap()
    if last == "moe":
        out = nc.dram_tensor("out", [T, D], F32, kind="ExternalOutput").ap()
    else:
        out = nc.dram_tensor("xout", [D, T], F32, kind="ExternalOutput").ap()
    ev.w1c_dram = ev.w2c_dram = None
    if "l0f" in parts:
        FF = cfg["FF"]
        ev.w1c_dram = nc.dram_tensor(
            "w1c", [cfg["FF"] // 128, 128, (D // 128) * 128], BF16,
            kind="Internal").ap()
        ev.w2c_dram = nc.dram_tensor("w2c", [FF, D], BF16, kind="Internal").ap()

    if first == "l0a":
        pre_transpose(b, ev, x0, xa)
        b.barrier()

    ev.l0f_W = None
    fns = {"l0a": block_l0attn, "l0f": block_l0ffn, "l1a": block_l1attn,
           "moe": block_moe_sparse if SPARSE_MOE else block_moe}
    cur = xa
    final_toks = []
    for i, p in enumerate(parts):
        Wp = {k: wdict[nm] for (k, nm) in BLOCK_WEIGHTS[p]}
        is_last = i == len(parts) - 1
        dst = out if is_last else (xb if cur is xa else xa)
        tok = fns[p](b, ev, cur, dst, Wp)
        final_toks = tok if isinstance(tok, list) else [tok]
        cur = dst
    b.finish(final_toks)
    return nc, stack


_PART_SETS = {
    "fused": [["l0a", "l0f", "l1a", "moe"]],
    "blocks": [["l0a"], ["l0f"], ["l1a"], ["moe"]],
}


def kernel(**inputs):
    cfg = cfg_full()
    part_sets = _PART_SETS[MODE]
    x_full = np.ascontiguousarray(np.asarray(inputs["L0_attn_x"], np.float32))
    B = x_full.shape[0]
    assert B == N_CORES
    cur_per_core = [np.ascontiguousarray(x_full[i]) for i in range(B)]
    for parts in part_sets:
        nc, stack = build_program(cfg, parts)
        needed = sorted({nm for p in parts for (_, nm) in BLOCK_WEIGHTS[p]})
        wvals = {nm: np.ascontiguousarray(np.asarray(inputs[nm], np.float32))
                 for nm in needed}
        xkey = "x0" if parts[0] == "l0a" else "xin"
        in_maps = [dict(wvals, **{xkey: cur_per_core[c]}) for c in range(B)]
        res = run_bass_kernel_spmd(nc, in_maps, core_ids=list(range(B)))
        stack.close()
        okey = "out" if parts[-1] == "moe" else "xout"
        cur_per_core = [np.ascontiguousarray(np.asarray(res.results[c][okey],
                                                        np.float32))
                        for c in range(B)]
    return np.stack(cur_per_core, axis=0).astype(np.float32)


# version 25
# speedup vs baseline: 1.1664x; 1.0186x over previous
import contextlib
import numpy as np
import ml_dtypes

import concourse.bass as bass
import concourse.mybir as mybir
from concourse.bass_utils import run_bass_kernel_spmd
from concourse.bass_core_rust import IndirectOffsetOnAxis

F32 = mybir.dt.float32
BF16 = mybir.dt.bfloat16
AF = mybir.ActivationFunctionType
ALU = mybir.AluOpType
AX = mybir.AxisListType

N_CORES = 8
LN_EPS = 1e-5
MODE = "fused"
SPARSE_MOE = True
DBG_MOE_NEXP = None


def cfg_full():
    return dict(D=1024, T=2048, H=16, HD=64, R=32, FF=4096, NE=8)


def tiles(total, w):
    o = 0
    while o < total:
        yield o, min(w, total - o)
        o += w


def moe_cap(T):
    return max(128, (3 * T // 8) // 128 * 128)



class Bld:
    NLANE = 8

    def __init__(self, nc, stack):
        self.nc = nc
        self.ops = {e: [] for e in ("T", "V", "S", "G", "Q")}
        self.sem_names = ["sT", "sV", "sS", "sG"]
        for q in ("Q", "G"):
            self.sem_names += [f"q{q}{i}" for i in range(self.NLANE)]
        self.sems = {n: stack.enter_context(nc.semaphore(n)) for n in self.sem_names}
        self.cnt = {n: 0 for n in self.sem_names}
        self.rr = {"Q": 0, "G": 0}
        self.lastw = {}
        self.eng_sem = {"T": "sT", "V": "sV", "S": "sS", "G": "sG"}
        self.psum = stack.enter_context(nc.psum_tensor("psA", [128, 4096], F32))
        self.bank_w = [None] * 8
        self.bank_r = [[] for _ in range(8)]
        self.t_dirty = False

    def bank(self, bk, lo=0, hi=512, np_=128, p0=0):
        return self.psum[p0:p0 + np_, bk * 512 + lo: bk * 512 + hi]

    def bankb(self, bk, lo=0, hi=128, np_=128, p0=0):
        return self.psum.bitcast(BF16)[p0:p0 + np_, bk * 1024 + lo: bk * 1024 + hi]

    def _wait_list(self, ek, waits):
        out = []
        for t in waits:
            if t is None:
                continue
            s, v = t
            if v <= 0:
                continue
            key = (ek, s)
            if self.lastw.get(key, 0) < v:
                self.lastw[key] = v
                out.append((self.sems[s], v))
        return out

    def op(self, ek, fn, waits=(), inc=False):
        waits = list(waits)
        sem_name = self.eng_sem.get(ek)
        if ek in ("V", "S", "G"):
            waits.append((sem_name, self.cnt[sem_name]))
            inc = True
        wl = self._wait_list(ek, waits)
        tok = None
        if inc:
            self.cnt[sem_name] += 1
            tok = (sem_name, self.cnt[sem_name])
            if ek == "T":
                self.t_dirty = False
        elif ek == "T":
            self.t_dirty = True
        sems = self.sems

        def cl(eng, fn=fn, wl=wl, inc=inc, sem_name=sem_name):
            for (h, v) in wl:
                eng.wait_ge(h, v)
            ins = fn(eng)
            if inc:
                ins.then_inc(sems[sem_name], 1)
        self.ops[ek].append(cl)
        return tok

    def dma(self, q, out, in_, waits=()):
        lane = self.rr[q] % self.NLANE
        self.rr[q] += 1
        sem_name = f"q{q}{lane}"
        wl = self._wait_list(q, list(waits) + [(sem_name, self.cnt[sem_name])])
        self.cnt[sem_name] += 16
        tok = (sem_name, self.cnt[sem_name])
        sems = self.sems

        def cl(eng, out=out, in_=in_, wl=wl, sem_name=sem_name):
            for (h, v) in wl:
                eng.wait_ge(h, v)
            eng.dma_start(out=out, in_=in_).then_inc(sems[sem_name], 16)
        self.ops[q].append(cl)
        return tok

    def idma(self, fn, waits=()):
        lane = self.rr["G"] % self.NLANE
        self.rr["G"] += 1
        sem_name = f"qG{lane}"
        wl = self._wait_list("G", list(waits) + [(sem_name, self.cnt[sem_name])])
        self.cnt[sem_name] += 16
        tok = (sem_name, self.cnt[sem_name])
        sems = self.sems

        def cl(eng, fn=fn, wl=wl, sem_name=sem_name):
            for (h, v) in wl:
                eng.wait_ge(h, v)
            fn(eng).then_inc(sems[sem_name], 16)
        self.ops["G"].append(cl)
        return tok

    def barrier(self, engs="TVSGQ"):
        assert not self.t_dirty, "barrier with un-inc'd T ops"
        snap = [(s, self.cnt[s]) for s in self.sem_names]
        for ek in engs:
            own = self.eng_sem.get(ek)
            wl = self._wait_list(ek, [(s, v) for (s, v) in snap if s != own])
            if not wl:
                continue

            def cl(eng, wl=wl):
                for (h, v) in wl:
                    eng.wait_ge(h, v)
            self.ops[ek].append(cl)

    def mm_group(self, bk, out_ap, pairs, waits=()):
        w = list(waits) + self.bank_r[bk] + [self.bank_w[bk]]
        n = len(pairs)
        tok = None
        for i, (l, r) in enumerate(pairs):
            tok = self.op(
                "T",
                (lambda e, l=l, r=r, st=(i == 0), sp=(i == n - 1), o=out_ap:
                 e.matmul(o, l, r, start=st, stop=sp)),
                waits=w if i == 0 else (), inc=(i == n - 1))
        self.bank_w[bk] = tok
        self.bank_r[bk] = []
        return tok

    def transpose_to(self, bk, out_ap, in_ap, ident, waits=()):
        w = list(waits) + self.bank_r[bk] + [self.bank_w[bk]]
        tok = self.op("T", (lambda e, o=out_ap, i=in_ap, id_=ident:
                            e.transpose(o, i, id_)), waits=w, inc=True)
        self.bank_w[bk] = tok
        self.bank_r[bk] = []
        return tok

    def evac(self, ek, bks, fn, waits=()):
        if isinstance(bks, int):
            bks = [bks]
        w = list(waits) + [self.bank_w[bk] for bk in bks]
        tok = self.op(ek, fn, waits=w, inc=True)
        for bk in bks:
            self.bank_r[bk].append(tok)
        return tok

    def finish(self, out_tokens=()):
        snap = [(s, self.cnt[s]) for s in self.sem_names]
        wl = self._wait_list("Q", list(out_tokens) + snap)

        def cl(eng, wl=wl):
            for (h, v) in wl:
                eng.wait_ge(h, v)
        self.ops["Q"].append(cl)
        with self.nc.Block() as block:
            for ek, reg in (("T", block.tensor), ("V", block.vector),
                            ("S", block.scalar), ("G", block.gpsimd),
                            ("Q", block.sync)):
                ops = self.ops[ek]

                def runner(eng, ops=ops):
                    for fn in ops:
                        fn(eng)
                reg(runner)



class Env:
    def __init__(self, b: Bld, stack, cfg):
        nc = b.nc
        self.cfg = cfg
        D, T, H, FF, NE = cfg["D"], cfg["T"], cfg["H"], cfg["FF"], cfg["NE"]
        DC = D // 128
        DCT = DC * T
        TH = T // 2
        KC = T // 128
        VON = KC * H * 65
        WELS = DC * D
        FHW = min(2048, FF)
        FC = FF // 128
        r = max(DCT, VON, 5 * T + 2048, 2 * WELS + 256, min(16, FC) * TH,
                8 * FHW, 4 * TH + 2 * FHW + FC * 128, 8 * 2 * D, 16384)
        self.RSZ = r
        self.R0, self.R1, self.R2, self.R3 = 0, r, 2 * r, 2 * r + DCT
        self.R4 = 3 * r + DCT
        self.R4_els = max(4 * T, 4 * D, KC * 512, 2 * D + 3 * 1024 + 128, 2048)
        self.a1_els = self.R4 + self.R4_els
        self.a2_els = max(DCT, 2 * WELS,
                          8 * TH + 2 * DC * max(FF // 8, 128) + 4096,
                          8 * TH + 4 * D, 8192)
        self.A1 = stack.enter_context(nc.sbuf_tensor("A1", [128, self.a1_els], BF16))
        self.A2 = stack.enter_context(nc.sbuf_tensor("A2", [128, self.a2_els], BF16))
        sb = lambda name, shape, dt: stack.enter_context(nc.sbuf_tensor(name, shape, dt))
        self.ident = sb("c_ident", [128, 128], F32)
        self.identb = sb("c_identb", [128, 128], BF16)
        self.ones_row = sb("c_ones_row", [1, 1024], BF16)
        self.ones_row_f = sb("c_ones_row_f", [1, 128], F32)
        self.ones_col_bf = sb("c_ones_col_bf", [128, 1], BF16)
        self.ones_col_f = sb("c_ones_col_f", [128, 1], F32)
        self.stair = sb("c_stair", [128, 896], BF16)
        self.bias = sb("c_bias", [1, 2048], BF16)
        self.rows = sb("c_rows", [1, 2112], F32)
        self.rows_bf = sb("c_rows_bf", [1, 1024], BF16)
        self.invbc = sb("c_invbc", [64, 512], BF16)
        self.smalls = sb("c_smalls", [128, 64], F32)
        self.colbf = sb("c_colbf", [128, 16], BF16)
        self.rwt = sb("c_rwt", [128, DC * NE], BF16)
        self.ustrict = sb("c_ustrict", [128, 128], F32)
        self.iota = sb("c_iota", [128, T // 128], mybir.dt.int32)
        self.ifill = sb("c_ifill", [128, 8], mybir.dt.int32)
        self.rwtf = sb("c_rwtf", [128, DC * NE], F32)
        self.cw = sb("c_cw", [128, KC * NE], F32)
        self.locs = sb("c_locs", [128, 64], mybir.dt.int32)
        b.op("G", lambda e: e.memset(self.ident[:, :], 0.0))
        b.op("G", lambda e: e.affine_select(
            out=self.ident[:, :], in_=self.ident[:, :],
            compare_op=ALU.not_equal, fill=1.0,
            base=0, pattern=[[-1, 128]], channel_multiplier=1))
        b.op("G", lambda e: e.memset(self.identb[:, :], 0.0))
        b.op("G", lambda e: e.affine_select(
            out=self.identb[:, :], in_=self.identb[:, :],
            compare_op=ALU.not_equal, fill=1.0,
            base=0, pattern=[[-1, 128]], channel_multiplier=1))
        b.op("G", lambda e: e.memset(self.ones_row[:, :], 1.0))
        b.op("G", lambda e: e.memset(self.ones_row_f[:, :], 1.0))
        b.op("G", lambda e: e.memset(self.ones_col_bf[:, :], 1.0))
        b.op("G", lambda e: e.memset(self.ones_col_f[:, :], 1.0))
        b.op("G", lambda e: e.memset(self.stair[:, :], 1.0))
        b.op("G", lambda e: e.affine_select(
            out=self.stair[:, :], in_=self.stair[:, :],
            compare_op=ALU.is_ge, fill=0.0,
            base=-384, pattern=[[1, 896]], channel_multiplier=-1))
        b.op("G", lambda e: e.memset(self.ustrict[:, :], 1.0))
        b.op("G", lambda e: e.affine_select(
            out=self.ustrict[:, :], in_=self.ustrict[:, :],
            compare_op=ALU.is_ge, fill=0.0,
            base=-1, pattern=[[1, 128]], channel_multiplier=-1))
        b.op("G", lambda e: e.memset(self.ifill[:, :], T))
        b.op("G", lambda e: e.iota(self.iota[:, :], pattern=[[128, T // 128]],
                                   base=0, channel_multiplier=1))

    def a1(self, lo, n):
        assert 0 <= lo and lo + n <= self.a1_els, (lo, n, self.a1_els)
        return self.A1[:, lo:lo + n]

    def a1f(self, lo, n_f32):
        return self.a1(lo, 2 * n_f32).bitcast(F32)

    def a2(self, lo, n):
        assert 0 <= lo and lo + n <= self.a2_els, (lo, n, self.a2_els)
        return self.A2[:, lo:lo + n]

    def a2f(self, lo, n_f32):
        return self.a2(lo, 2 * n_f32).bitcast(F32)



def layernorm(b: Bld, ev: Env, x_dram, xlnf_dram=None, xtm_dram=None):
    cfg = ev.cfg
    D, T = cfg["D"], cfg["T"]
    DC = D // 128
    DCT = DC * T
    xbf = ev.a1(ev.R0, DCT)
    xf = ev.a1f(ev.R1, 2 * T)
    sqbf = ev.a1(ev.R1 + 4 * T, T)
    bc = ev.a1(ev.R1 + 5 * T, 1024)
    xln = ev.a1(ev.R2, DCT)
    tile_list = list(tiles(T, 512))
    nt = len(tile_list)
    assert nt <= 4
    inv_d = 1.0 / D

    pre = []
    for bk in range(8):
        pre += b.bank_r[bk] + [b.bank_w[bk]]
    buf_free = [None, None]
    last_mm = None
    for dc in range(DC):
        buf = dc % 2
        ld = b.dma("Q", xf[:, buf * T:(buf + 1) * T],
                   x_dram[dc * 128:(dc + 1) * 128, :], waits=[buf_free[buf]])
        cast = b.op("V", (lambda e, dc=dc, buf=buf: e.tensor_copy(
            xbf[:, dc * T:(dc + 1) * T], xf[:, buf * T:(buf + 1) * T])), waits=[ld])
        sq = b.op("S", (lambda e, dc=dc: e.activation(
            sqbf[:, :], xbf[:, dc * T:(dc + 1) * T], AF.Square)),
            waits=[cast, last_mm])
        buf_free[buf] = cast
        mm_w = [cast, sq] + (pre if dc == 0 else [])
        for ti, (o, w) in enumerate(tile_list):
            pa = b.bank(ti, 0, w, np_=1)
            xa_ = xbf[:, dc * T + o: dc * T + o + w]
            b.op("T", (lambda e, pa=pa, xa_=xa_, st=(dc == 0), sp=(dc == DC - 1):
                       e.matmul(pa, ev.ones_col_bf[:, :], xa_, start=st, stop=sp)),
                 waits=mm_w if ti == 0 else ())
            pb = b.bank(4 + ti, 0, w, np_=1)
            sa = sqbf[:, o:o + w]
            last_mm = b.op("T", (lambda e, pb=pb, sa=sa, st=(dc == 0),
                                 sp=(dc == DC - 1):
                                 e.matmul(pb, ev.ones_col_bf[:, :], sa,
                                          start=st, stop=sp)),
                           inc=(ti == nt - 1))
    for bk in range(8):
        b.bank_w[bk] = last_mm
        b.bank_r[bk] = []

    rows, rows_bf = ev.rows, ev.rows_bf
    prev_asm_st = [[] for _ in range(4)]
    asmb_st = [[], []]
    slot_rd = [list(buf_free), list(buf_free)]
    slot_st = [list(buf_free), list(buf_free)]
    for ti, (o, w) in enumerate(tile_list):
        s0 = rows[0:1, 0:w]
        s1 = rows[0:1, 512:512 + w]
        s2 = rows[0:1, 1024:1024 + w]
        rbf = rows_bf[0:1, 0:w]
        mbf = rows_bf[0:1, 512:512 + w]
        b.evac("V", ti, (lambda e, ti=ti, w=w, s0=s0: e.tensor_scalar_mul(
            s0, b.bank(ti, 0, w, np_=1), inv_d)))
        b.evac("V", 4 + ti, (lambda e, ti=ti, w=w, s1=s1: e.tensor_scalar_mul(
            s1, b.bank(4 + ti, 0, w, np_=1), inv_d)))
        b.op("V", lambda e, s0=s0, s2=s2: e.tensor_mul(s2, s0, s0))
        b.op("V", lambda e, s1=s1, s2=s2: e.tensor_sub(s1, s1, s2))
        tve = b.op("V", lambda e, s1=s1: e.tensor_scalar_add(s1, s1, LN_EPS))
        tsq = b.op("S", (lambda e, s1=s1, s2=s2: e.activation(s2, s1, AF.Sqrt)),
                   waits=[tve])
        trec = b.op("V", (lambda e, s1=s1, s2=s2: e.reciprocal(s1, s2)),
                    waits=[tsq])
        bk_r, bk_m = ti, 4 + ti
        if xlnf_dram is None:
            b.op("V", lambda e, s0=s0, s1=s1, s2=s2: e.tensor_mul(s2, s0, s1))
            b.op("V", lambda e, s1=s1, rbf=rbf: e.tensor_copy(rbf, s1))
            cv = b.op("V", lambda e, s2=s2, mbf=mbf: e.tensor_copy(mbf, s2))
            b.mm_group(bk_r, b.bank(bk_r, 0, w),
                       [(ev.ones_row[0:1, 0:128], rbf)], waits=[cv])
            b.mm_group(bk_m, b.bank(bk_m, 0, w),
                       [(ev.ones_row[0:1, 0:128], mbf)], waits=[cv])
            b.evac("V", bk_r, (lambda e, w=w, bk_r=bk_r: e.tensor_copy(
                bc[:, 0:w], b.bank(bk_r, 0, w))))
            b.evac("V", bk_m, (lambda e, w=w, bk_m=bk_m: e.tensor_copy(
                bc[:, 512:512 + w], b.bank(bk_m, 0, w))))
            for dc in range(DC):
                xo = xln[:, dc * T + o: dc * T + o + w]
                xi = xbf[:, dc * T + o: dc * T + o + w]
                b.op("V", (lambda e, xo=xo, xi=xi, w=w: e.tensor_mul(
                    xo, xi, bc[:, 0:w])))
                b.op("V", (lambda e, xo=xo, w=w: e.tensor_sub(
                    xo, xo, bc[:, 512:512 + w])))
        else:
            b.mm_group(bk_r, b.bank(bk_r, 0, w),
                       [(ev.ones_row_f[0:1, 0:128], s1)], waits=[trec])
            b.mm_group(bk_m, b.bank(bk_m, 0, w),
                       [(ev.ones_row_f[0:1, 0:128], s0)], waits=[trec])
            xft = [ev.a1f(ev.R1, 512), ev.a1f(ev.R1 + 1024, 512)]
            xlt = [ev.a1f(ev.R1 + 2048, 512), ev.a1f(ev.R1 + 3072, 512)]
            asm = ev.a1f(ev.R1 + 4096, 4096)
            bcf = ev.a1f(ev.R1 + 12288, 1024)
            b.evac("V", bk_r, (lambda e, w=w, bk_r=bk_r: e.tensor_copy(
                bcf[:, 0:w], b.bank(bk_r, 0, w))))
            cvb = b.evac("V", bk_m, (lambda e, w=w, bk_m=bk_m: e.tensor_copy(
                bcf[:, 512:512 + w], b.bank(bk_m, 0, w))))
            slot_rd[0] = slot_rd[0] + [cvb]
            slot_rd[1] = slot_rd[1] + [cvb]
            asm_toks = [[] for _ in range(4)]
            for dc in range(DC):
                par = dc % 2
                ldx = b.dma("Q", xft[par][:, 0:w],
                            x_dram[dc * 128:(dc + 1) * 128, o:o + w],
                            waits=slot_rd[par])
                b.op("V", (lambda e, par=par, w=w:
                           e.tensor_sub(xlt[par][:, 0:w], xft[par][:, 0:w],
                                        bcf[:, 512:512 + w])),
                     waits=[ldx] + slot_st[par])
                t2 = b.op("V", (lambda e, par=par, w=w:
                                e.tensor_mul(xlt[par][:, 0:w], xlt[par][:, 0:w],
                                             bcf[:, 0:w])))
                slot_rd[par] = [t2]
                xo = xln[:, dc * T + o: dc * T + o + w]
                b.op("V", (lambda e, xo=xo, par=par, w=w: e.tensor_copy(
                    xo, xlt[par][:, 0:w])))
                slot_st[par] = [b.dma(
                    "Q", xlnf_dram[dc * 128:(dc + 1) * 128, o:o + w],
                    xlt[par][:, 0:w], waits=[t2])]
                if xtm_dram is not None:
                    for j in range(w // 128):
                        bt = bk_r if j % 2 == 0 else bk_m
                        src = xlt[par][:, j * 128: j * 128 + 128]
                        b.transpose_to(bt, b.bank(bt, 0, 128), src,
                                       ev.ident[:, :],
                                       waits=[t2] + prev_asm_st[j])
                        dsta = asm[:, j * 1024 + dc * 128:
                                   j * 1024 + dc * 128 + 128]
                        asm_toks[j].append(b.evac("V", bt, (
                            lambda e, dsta=dsta, bt=bt: e.tensor_copy(
                                dsta, b.bank(bt, 0, 128)))))
                        slot_rd[par] = slot_rd[par] + [asm_toks[j][-1]]
            if xtm_dram is not None:
                asmb = ev.a1(ev.R1 + 14336, 2048)
                for j in range(w // 128):
                    jp = j % 2
                    cvt = b.op("V", (lambda e, j=j, jp=jp: e.tensor_copy(
                        asmb[:, jp * 1024: jp * 1024 + D],
                        asm[:, j * 1024: j * 1024 + D])),
                        waits=asm_toks[j] + asmb_st[jp])
                    prev_asm_st[j] = [cvt]
                    asmb_st[jp] = [b.dma(
                        "Q", xtm_dram[o + j * 128: o + j * 128 + 128, :],
                        asmb[:, jp * 1024: jp * 1024 + D], waits=[cvt])]



def pre_transpose(b: Bld, ev: Env, x_tm_dram, x_fm_dram):
    cfg = ev.cfg
    D, T = cfg["D"], cfg["T"]
    DC = D // 128
    xin = ev.a1f(ev.R0, 2 * D)
    xtr = ev.a1f(ev.R1, 2 * DC * 128)
    buf_free = [None, None]
    out_free = [None, None]
    for tc in range(T // 128):
        buf = tc % 2
        ld = b.dma("Q", xin[:, buf * D:(buf + 1) * D],
                   x_tm_dram[tc * 128:(tc + 1) * 128, :], waits=[buf_free[buf]])
        cp_toks = []
        for dc in range(DC):
            bk = 4 + (dc % 2)
            src = xin[:, buf * D + dc * 128: buf * D + dc * 128 + 128]
            dst = xtr[:, buf * DC * 128 + dc * 128: buf * DC * 128 + dc * 128 + 128]
            b.transpose_to(bk, b.bank(bk, 0, 128), src, ev.ident[:, :], waits=[ld])
            cp = b.evac("V", bk, (lambda e, dst=dst, bk=bk: e.tensor_copy(
                dst, b.bank(bk, 0, 128))), waits=[out_free[buf]])
            cp_toks.append(cp)
        st = b.dma("Q",
                   x_fm_dram[:, tc * 128:(tc + 1) * 128].rearrange(
                       "(c p) n -> p c n", p=128),
                   xtr[:, buf * DC * 128:(buf + 1) * DC * 128].rearrange(
                       "p (c n) -> p c n", n=128),
                   waits=cp_toks)
        buf_free[buf] = cp_toks[-1]
        out_free[buf] = st



def proj_fm(b: Bld, ev: Env, xln, wsl, out, scale=1.0, bias_row=None, wait0=()):
    cfg = ev.cfg
    D, T = cfg["D"], cfg["T"]
    DC = D // 128
    rot = [0]
    last = []
    for ic in range(DC):
        for (o, w) in tiles(T, 512):
            bk = rot[0] % 4
            rot[0] += 1
            pairs = [(wsl(dc, ic), xln[:, dc * T + o: dc * T + o + w])
                     for dc in range(DC)]
            if bias_row is not None:
                pairs.append((bias_row[0:1, ic * 128: ic * 128 + 128],
                              ev.ones_row[0:1, 0:w]))
            b.mm_group(bk, b.bank(bk, 0, w), pairs, waits=wait0)
            dst = out[:, ic * T + o: ic * T + o + w]
            if scale != 1.0:
                last.append(b.evac("V", bk, (lambda e, dst=dst, bk=bk, w=w:
                                             e.tensor_scalar_mul(
                                                 dst, b.bank(bk, 0, w), scale))))
            else:
                last.append(b.evac("S", bk, (lambda e, dst=dst, bk=bk, w=w:
                                             e.activation(dst, b.bank(bk, 0, w),
                                                          AF.Copy))))
    return last


def proj_tm_vones(b: Bld, ev: Env, xln, wsl_r, v_ones, bias_row=None, wait0=()):
    cfg = ev.cfg
    D, T, H, HD = cfg["D"], cfg["T"], cfg["H"], cfg["HD"]
    assert HD == 64
    DC = D // 128
    rot = [0]
    last = []
    for tc in range(T // 128):
        for (io, iw) in tiles(D, 512):
            bk = rot[0] % 4
            rot[0] += 1
            pairs = [(xln[:, dc * T + tc * 128: dc * T + tc * 128 + 128],
                      wsl_r(dc, io, iw)) for dc in range(DC)]
            if bias_row is not None:
                pairs.append((ev.ones_row[0:1, 0:128],
                              bias_row[0:1, io: io + iw]))
            b.mm_group(bk, b.bank(bk, 0, iw), pairs, waits=wait0)
            nh = iw // 64
            base = tc * H * 65 + (io // 64) * 65
            dst = v_ones[:, base: base + nh * 65].rearrange(
                "p (h c) -> p h c", h=nh)[:, :, 0:64]
            last.append(b.evac("V", bk, (lambda e, dst=dst, bk=bk, iw=iw, nh=nh:
                                         e.tensor_copy(dst, b.bank(bk, 0, iw).rearrange(
                                             "p (h d) -> p h d", h=nh)))))
    return last



def attention_core(b: Bld, ev: Env, sc_lh, sc_rh, v_ones, y, causal, ptil2):
    cfg = ev.cfg
    T, H, HD = cfg["T"], cfg["H"], cfg["HD"]
    KC = T // 128
    ptils = [ev.a1(ev.R4, KC * 512), ptil2]
    inv_f = ev.rows[0:1, 1536:2048]
    inv_bf = ev.rows_bf[0:1, 0:512]
    items = [(h, go, gw) for h in range(H) for (go, gw) in tiles(T, 512)]
    exp_q = {}
    av_toks = {}

    def stage_a(i):
        h, go, gw = items[i]
        ptil = ptils[i % 2]
        nkc = (go + gw) // 128 if causal else KC
        war = [av_toks.get(i - 2)]
        toks = []
        for kc in range(nkc):
            bk = kc % 6
            b.mm_group(bk, b.bank(bk, 0, gw),
                       [(sc_lh(h, kc), sc_rh(h, go, gw))])
            pdst = ptil[:, kc * 512: kc * 512 + gw]
            tok = b.evac("S", bk, (lambda e, pdst=pdst, bk=bk, gw=gw:
                                   e.activation(pdst, b.bank(bk, 0, gw),
                                                AF.Exp)),
                         waits=war if kc == 0 else ())
            if causal and kc * 128 + 127 >= go:
                off = 384 + go - kc * 128
                assert 0 <= off and off + gw <= 896, (off, gw)
                msl = ev.stair[:, off:off + gw]
                tok = b.op("V", (lambda e, pdst=pdst, msl=msl:
                                 e.tensor_mul(pdst, pdst, msl)),
                           waits=[tok])
            toks.append(tok)
        exp_q[i] = toks

    def stage_b(i):
        h, go, gw = items[i]
        ptil = ptils[i % 2]
        nkc = (go + gw) // 128 if causal else KC
        pairs = []
        for kc in range(nkc):
            vsl = v_ones[:, (kc * H + h) * 65: (kc * H + h) * 65 + 65]
            pairs.append((vsl, ptil[:, kc * 512: kc * 512 + gw]))
        av_toks[i] = b.mm_group(6, b.bank(6, 0, gw, np_=65), pairs,
                                waits=exp_q.pop(i))
        b.evac("V", 6, (lambda e, gw=gw: e.reciprocal(
            inv_f[0:1, 0:gw], b.bank(6, 0, gw, np_=1, p0=64))))
        cv = b.op("V", lambda e, gw=gw: e.tensor_copy(inv_bf[0:1, 0:gw],
                                                      inv_f[0:1, 0:gw]))
        b.mm_group(7, b.bank(7, 0, gw, np_=64),
                   [(ev.ones_row[0:1, 0:64], inv_bf[0:1, 0:gw])], waits=[cv])
        b.evac("V", 7, (lambda e, gw=gw: e.tensor_copy(
            ev.invbc[0:64, 0:gw], b.bank(7, 0, gw, np_=64))))
        dc = (h * HD) // 128
        p0 = (h * HD) % 128
        ydst = y[p0:p0 + 64, dc * T + go: dc * T + go + gw]
        b.evac("V", 6, (lambda e, ydst=ydst, gw=gw: e.tensor_mul(
            ydst, b.bank(6, 0, gw, np_=64), ev.invbc[0:64, 0:gw])))

    stage_a(0)
    for i in range(1, len(items)):
        stage_a(i)
        stage_b(i - 1)
    stage_b(len(items) - 1)



def out_proj_residual(b: Bld, ev: Env, y, wsl, x_in, x_out, bias_row=None,
                      wait0=()):
    cfg = ev.cfg
    D, T = cfg["D"], cfg["T"]
    DC = D // 128
    xi = ev.a1f(ev.R4, T)
    xo = ev.a1f(ev.R4 + 2 * T, T)
    st_tok = [None]
    tt_last = [None]
    rot = [0]
    for dco in range(DC):
        ld = b.dma("Q", xi[:, :], x_in[dco * 128:(dco + 1) * 128, :],
                   waits=[tt_last[0]])
        tts = []
        for (o, w) in tiles(T, 512):
            bk = rot[0] % 4
            rot[0] += 1
            pairs = [(wsl(dc, dco), y[:, dc * T + o: dc * T + o + w])
                     for dc in range(DC)]
            if bias_row is not None:
                pairs.append((bias_row[0:1, dco * 128: dco * 128 + 128],
                              ev.ones_row[0:1, 0:w]))
            b.mm_group(bk, b.bank(bk, 0, w), pairs, waits=wait0)
            dst = xo[:, o:o + w]
            src1 = xi[:, o:o + w]
            tts.append(b.evac("V", bk, (lambda e, dst=dst, src1=src1, bk=bk, w=w:
                                        e.tensor_add(dst, b.bank(bk, 0, w), src1)),
                              waits=[ld, st_tok[0]]))
        tt_last[0] = tts[-1]
        st_tok[0] = b.dma("Q", x_out[dco * 128:(dco + 1) * 128, :], xo[:, :],
                          waits=tts)
    return st_tok[0]



def block_l0attn(b: Bld, ev: Env, x_in, x_out, W):
    cfg = ev.cfg
    D, T, H, HD = cfg["D"], cfg["T"], cfg["H"], cfg["HD"]
    DC = D // 128
    DCT = DC * T
    KC = T // 128
    WELS = DC * D
    wA = ev.a2(0, WELS)
    wB = ev.a2(WELS, WELS)
    xln = ev.a1(ev.R2, DCT)
    q = ev.a1(ev.R0, DCT)
    k = ev.a1(ev.R3, DCT)
    v_ones = ev.a1(ev.R1, KC * H * 65)
    y = ev.a1(ev.R2, DCT)

    def wload(dst, nm, waits=()):
        return b.dma("G", dst.rearrange("p (c n) -> p c n", c=DC),
                     W[nm].rearrange("(c p) n -> p c n", p=128), waits=waits)

    ldq = wload(wA, "Wq")
    ldk = wload(wB, "Wk")
    layernorm(b, ev, x_in)
    b.barrier()
    wsl = lambda Wt: (lambda dc, ic: Wt[:, dc * D + ic * 128: dc * D + ic * 128 + 128])
    q_toks = proj_fm(b, ev, xln, wsl(wA), q, scale=1.0 / float(np.sqrt(HD)),
                     wait0=[ldq])
    ldv = wload(wA, "Wv", waits=[q_toks[-1]])
    k_toks = proj_fm(b, ev, xln, wsl(wB), k, wait0=[ldk])
    ldo = wload(wB, "Wo", waits=[k_toks[-1]])
    von_ms = b.op("G", lambda e: e.memset(v_ones[:, :], 1.0))
    wslA_r = lambda dc, io, iw: wA[:, dc * D + io: dc * D + io + iw]
    proj_tm_vones(b, ev, xln, wslA_r, v_ones, wait0=[ldv, von_ms])
    b.barrier()

    hpc = 128 // HD
    sc_lh = lambda h, kc: k[(h % hpc) * HD:(h % hpc) * HD + HD,
                            (h // hpc) * T + kc * 128:(h // hpc) * T + kc * 128 + 128]
    sc_rh = lambda h, go, gw: q[(h % hpc) * HD:(h % hpc) * HD + HD,
                                (h // hpc) * T + go:(h // hpc) * T + go + gw]
    if 2 * WELS + KC * 512 <= ev.a2_els:
        ptil2 = ev.a2(2 * WELS, KC * 512)
    else:
        assert KC * 512 <= WELS
        ptil2 = ev.a2(0, KC * 512)
    attention_core(b, ev, sc_lh, sc_rh, v_ones, y, True, ptil2)
    b.barrier()
    wslo = lambda dc, dco: wB[:, dc * D + dco * 128: dc * D + dco * 128 + 128]
    st = out_proj_residual(b, ev, y, wslo, x_in, x_out, wait0=[ldo])
    if getattr(ev, "l0f_W", None) is not None:
        sc_stage = ev.a2f(ev.a2_els - 4096, 2048)
        t1 = tern_scale_pass(b, ev, ev.l0f_W["W1"], cfg["FF"], D, None,
                             thrcol=0, bank=6, eng="V", stage=sc_stage,
                             cw_=1024)
        tern_scale_pass(b, ev, ev.l0f_W["W2"], D, cfg["FF"], None,
                        thrcol=1, bank=7, war=t1, eng="V", stage=sc_stage,
                        cw_=1024)
    b.barrier()
    return st



def tern_scale_pass(b: Bld, ev: Env, Wdram, nrows, ncols, stage_off,
                    thrcol, bank=6, war=None, eng="V", stage=None, cw_=1024):
    cw_ = min(cw_, ncols)
    if stage is None:
        stage = ev.a1f(stage_off, 2 * cw_)
    asum = ev.smalls[:, 60:61]
    tmp = ev.smalls[:, 61:62]
    thr = ev.smalls[:, thrcol:thrcol + 1]
    zz = b.op(eng, lambda e: e.memset(asum, 0.0))
    buf_free = [war, war]
    ad = zz
    i = 0
    for g in range(nrows // 128):
        for (co, cn) in tiles(ncols, cw_):
            buf = i % 2
            i += 1
            ld = b.dma("Q", stage[:, buf * cw_: buf * cw_ + cn],
                       Wdram[g * 128:(g + 1) * 128, co:co + cn],
                       waits=[buf_free[buf]])
            rd = b.op(eng, (lambda e, buf=buf, cn=cn: e.reduce_sum(
                tmp, stage[:, buf * cw_: buf * cw_ + cn], axis=AX.X,
                apply_absolute_value=True)), waits=[ld, zz])
            ad = b.op(eng, lambda e: e.tensor_add(asum, asum, tmp))
            buf_free[buf] = rd
    b.mm_group(bank, b.bank(bank, 0, 1, np_=1),
               [(asum, ev.ones_col_f[:, :])], waits=[ad])
    srow = ev.rows[0:1, 2048:2049]
    b.evac(eng, bank, (lambda e: e.tensor_scalar(
        srow, b.bank(bank, 0, 1, np_=1), 1.0 / float(nrows * ncols), 1e-8,
        ALU.mult, ALU.max)))
    hrow = ev.rows[0:1, 2049:2050]
    cv = b.op(eng, lambda e: e.tensor_scalar_mul(hrow, srow, 0.5))
    b.mm_group(bank, b.bank(bank, 0, 1),
               [(ev.ones_row_f[0:1, 0:128], hrow)], waits=[cv])
    return b.evac(eng, bank, (lambda e: e.tensor_copy(thr, b.bank(bank, 0, 1))))


def ternarize(b: Bld, ev: Env, src_f32, thrcol, sgn, ab, tern_f, waits=()):
    t1 = b.op("S", (lambda e: e.activation(sgn, src_f32, AF.Sign)), waits=waits)
    t2 = b.op("S", (lambda e: e.activation(ab, src_f32, AF.Abs)))
    thr = ev.smalls[:, thrcol:thrcol + 1]
    b.op("V", (lambda e: e.tensor_scalar(ab, ab, thr, None, ALU.is_gt)),
         waits=[t2])
    return b.op("V", (lambda e: e.tensor_mul(tern_f, ab, sgn)), waits=[t1])



def block_l0ffn(b: Bld, ev: Env, x_in, x_out, W):
    cfg = ev.cfg
    D, T, FF = cfg["D"], cfg["T"], cfg["FF"]
    DC = D // 128
    DCT = DC * T
    TH = T // 2
    FC = FF // 128
    FHW = min(1024, FF)
    GR = DC * 128
    r = ev.RSZ
    xln = ev.a1(ev.R2, DCT)
    w1c = ev.w1c_dram
    w2c = ev.w2c_dram

    def h_sl(fc, o, w):
        if fc < 16:
            return ev.a1(ev.R1 + fc * TH + o, w)
        if fc < 24:
            return ev.a1(ev.R4 + (fc - 16) * TH + o, w)
        return ev.a2((fc - 24) * TH + o, w)

    if getattr(ev, "l0f_W", None) is None:
        t1 = tern_scale_pass(b, ev, W["W1"], FF, D, ev.R3, thrcol=0)
        tern_scale_pass(b, ev, W["W2"], D, FF, ev.R3, thrcol=1, bank=7,
                        war=t1)
    layernorm(b, ev, x_in)
    b.barrier()

    b1row = W["b1"].rearrange("(a d) -> a d", a=1)
    b2row = W["b2"].rearrange("(a d) -> a d", a=1)

    stage = ev.a1f(ev.R3, 2 * D)
    sgn = ev.a1f(ev.R3 + 4 * D, D)
    ab = ev.a1f(ev.R3 + 6 * D, D)
    ternf = ev.a1f(ev.R3 + 8 * D, D)
    gbuf = [ev.a1(ev.R0, GR), ev.a1(ev.R0 + GR, GR)]
    buf_free = [None, None]
    gb_free = [None, None]
    w1_cache = []
    prev_tk = [None]
    for fc in range(FC):
        buf = fc % 2
        ld = b.dma("Q", stage[:, buf * D:(buf + 1) * D],
                   W["W1"][fc * 128:(fc + 1) * 128, :], waits=[buf_free[buf]])
        tk = ternarize(b, ev, stage[:, buf * D:(buf + 1) * D], 0,
                       sgn, ab, ternf, waits=[ld, prev_tk[0]])
        prev_tk[0] = tk
        buf_free[buf] = tk
        cps = []
        for dc in range(DC):
            bk = 4 + (dc % 2)
            b.transpose_to(bk, b.bank(bk, 0, 128),
                           ternf[:, dc * 128: dc * 128 + 128],
                           ev.ident[:, :], waits=[tk])
            dst = gbuf[buf][:, dc * 128: dc * 128 + 128]
            cps.append(b.evac("V", bk, (lambda e, dst=dst, bk=bk:
                                        e.tensor_copy(dst, b.bank(bk, 0, 128))),
                              waits=[gb_free[buf]]))
        w1_cache.append(b.dma("Q", w1c[fc], gbuf[buf], waits=cps))
        gb_free[buf] = w1_cache[-1]

    ring = [ev.a1(ev.R3 + 10 * D + i * GR, GR) for i in range(3)]

    def h_steps(th, war):
        rot = [0]
        b1_half = [0]
        b1_tok = [b.dma("G", ev.bias[0:1, 0:min(FF, 2048)],
                        b1row[0:1, 0:min(FF, 2048)], waits=war)]
        slot_mm = [list(war), list(war), list(war)]
        h_toks = [None]
        for fc in range(FC):
            sl = fc % 3
            ld = b.dma("Q", ring[sl], w1c[fc],
                       waits=slot_mm[sl] + [w1_cache[fc]])
            if fc * 128 >= (b1_half[0] + 1) * 2048:
                b1_half[0] += 1
                b1_tok[0] = b.dma(
                    "G", ev.bias[0:1, 0:2048],
                    b1row[0:1, b1_half[0] * 2048:(b1_half[0] + 1) * 2048],
                    waits=[h_toks[-1]])
            gts = []
            for (o, w) in tiles(TH, 512):
                bk = rot[0] % 4
                rot[0] += 1
                pairs = [(ring[sl][:, dc * 128: dc * 128 + 128],
                          xln[:, dc * T + th * TH + o: dc * T + th * TH + o + w])
                         for dc in range(DC)]
                boff = fc * 128 - b1_half[0] * 2048
                pairs.append((ev.bias[0:1, boff:boff + 128],
                              ev.ones_row[0:1, 0:w]))
                gts.append(b.mm_group(bk, b.bank(bk, 0, w), pairs,
                                      waits=[ld, b1_tok[0]]))
                dst = h_sl(fc, o, w)
                h_toks.append(b.evac("S", bk, (lambda e, dst=dst, bk=bk, w=w:
                                               e.activation(dst, b.bank(bk, 0, w),
                                                            AF.Gelu))))
            slot_mm[sl] = gts
            yield h_toks[-1]

    w2stage = ev.a1f(ev.R0, 2 * FHW)
    ternf2 = ev.a1f(ev.R0 + 4 * FHW, FHW)
    sgn2 = ev.a1f(ev.R0 + 6 * FHW, FHW)
    ab2 = ev.a1f(ev.R3, FHW)
    strip = ev.a2(8 * TH, FC * 128)

    def w2_steps(war):
        buf2 = [list(war), list(war)]
        prev2 = [prev_tk[0]] + war
        strip_free = [None]
        cache = []
        for dcg in range(DC):
            w2_done = []
            for fh, fhw in tiles(FF, FHW):
                pbuf = (fh // FHW) % 2
                ld = b.dma("Q", w2stage[:, pbuf * FHW: pbuf * FHW + fhw],
                           W["W2"][dcg * 128:(dcg + 1) * 128, fh:fh + fhw],
                           waits=buf2[pbuf] + prev2)
                tk = ternarize(b, ev, w2stage[:, pbuf * FHW:pbuf * FHW + fhw],
                               1, sgn2[:, 0:fhw], ab2[:, 0:fhw],
                               ternf2[:, 0:fhw], waits=[ld] + prev2)
                prev2 = [tk]
                buf2[pbuf] = [tk]
                for j in range(fhw // 128):
                    fc = (fh + j * 128) // 128
                    bk = 4 + (j % 2)
                    b.transpose_to(bk, b.bank(bk, 0, 128),
                                   ternf2[:, j * 128: j * 128 + 128],
                                   ev.ident[:, :], waits=[tk])
                    dst = strip[:, fc * 128: fc * 128 + 128]
                    w2_done.append(b.evac("V", bk, (lambda e, dst=dst, bk=bk:
                                                    e.tensor_copy(dst, b.bank(bk, 0, 128))),
                                          waits=[strip_free[0]]))
            strip_free[0] = b.dma(
                "Q", w2c[:, dcg * 128:(dcg + 1) * 128].rearrange(
                    "(c p) n -> p c n", p=128),
                strip.rearrange("p (c n) -> p c n", n=128), waits=w2_done)
            cache.append(strip_free[0])
            yield cache

    g_w2 = w2_steps(w1_cache[-2:])
    g_h0 = h_steps(0, [])
    w2_cache = []
    h_last = None
    hs_per = max(1, FC // DC)
    try:
        for dcg in range(DC):
            w2_cache = next(g_w2)
            for _ in range(hs_per):
                h_last = next(g_h0, h_last)
    except StopIteration:
        pass
    for t in g_h0:
        h_last = t

    st_tok = [None]
    dh_list = list(tiles(D, 512))
    for th in range(2):
        if th == 1:
            gen = h_steps(1, [bar_tok[0]] if False else [])
            for t in gen:
                h_last = t
        ldb2 = b.dma("G", ev.bias[0:1, 0:D], b2row, waits=[h_last])
        w2th = ev.a1(ev.R0, FC * 512)
        xi = ev.a1f(ev.R3, TH)
        xo = ev.a1f(ev.R3 + 2 * TH, TH)
        last_dh_mm = [h_last]
        tt_last = [h_last]
        for (dho, dhw) in dh_list:
            ldh = []
            for fg in range(0, FC, 8):
                fgn = min(8, FC - fg)
                ldh.append(b.dma(
                    "Q",
                    w2th[:, fg * dhw: (fg + fgn) * dhw].rearrange(
                        "p (c n) -> p c n", c=fgn),
                    w2c[fg * 128:(fg + fgn) * 128, dho:dho + dhw].rearrange(
                        "(c p) n -> p c n", p=128),
                    waits=[last_dh_mm[0]] + w2_cache))
            for dcg in range(dho // 128, (dho + dhw) // 128):
                jj = dcg - dho // 128
                ld_x = b.dma("Q", xi[:, 0:TH],
                             x_in[dcg * 128:(dcg + 1) * 128,
                                  th * TH:(th + 1) * TH],
                             waits=[tt_last[0]])
                tts = []
                for (o, w) in tiles(TH, 512):
                    bk = (dcg * 2 + o // 512) % 4
                    pairs = [(w2th[:, fc * dhw + jj * 128:
                                   fc * dhw + jj * 128 + 128], h_sl(fc, o, w))
                             for fc in range(FC)]
                    pairs.append((ev.bias[0:1, dcg * 128: dcg * 128 + 128],
                                  ev.ones_row[0:1, 0:w]))
                    gt = b.mm_group(bk, b.bank(bk, 0, w), pairs,
                                    waits=ldh + [ldb2])
                    last_dh_mm[0] = gt
                    dst = xo[:, o:o + w]
                    src1 = xi[:, o:o + w]
                    tts.append(b.evac("V", bk, (lambda e, dst=dst, src1=src1,
                                                bk=bk, w=w: e.tensor_add(
                        dst, b.bank(bk, 0, w), src1)), waits=[ld_x, st_tok[0]]))
                tt_last[0] = tts[-1]
                st_tok[0] = b.dma("Q", x_out[dcg * 128:(dcg + 1) * 128,
                                             th * TH:(th + 1) * TH],
                                  xo[:, 0:TH], waits=tts)
        b.barrier()
    return st_tok[0]



def block_l1attn(b: Bld, ev: Env, x_in, x_out, W):
    cfg = ev.cfg
    D, T, H, HD, R = cfg["D"], cfg["T"], cfg["H"], cfg["HD"], cfg["R"]
    assert HD == 64
    DC = D // 128
    DCT = DC * T
    KC = T // 128
    WELS = DC * D
    HH = H // 2
    xln = ev.a1(ev.R2, DCT)
    qk = ev.a1(ev.R0, DCT)
    v_ones = ev.a1(ev.R0, KC * H * 65)
    zq = ev.a1(ev.R3, HH * T)
    zk = ev.a2(0, HH * T)
    y = ev.a1(ev.R2, DCT)
    wA = ev.a1(ev.R1, WELS)
    wB = ev.a1(ev.R1 + WELS, WELS)
    tiny = ev.R1 + 2 * WELS
    vt_bf = ev.a1(tiny, 64)
    ut_bf = ev.a1(tiny + 64, 64)
    vut2 = ev.a1(tiny + 128, 64)
    v2 = ev.a1(tiny + 192, 32)
    v2q = ev.a1(tiny + 224, 32)
    stage = ev.a1f(ev.R4, 2 * D)
    gran_bf = ev.a1(ev.R4, 2 * D)

    layernorm(b, ev, x_in)
    b.barrier()

    pads = ev.a1f(ev.R4, 128)
    prev_ld, prev_tr = None, None
    for (nm, dst) in (("U", ut_bf), ("V", vt_bf)):
        ms = b.op("G", (lambda e: e.memset(pads[:, :], 0.0)),
                  waits=[prev_ld, prev_tr])
        ld = b.dma("Q", pads[0:HD, 0:R], W[nm], waits=[ms])
        tr = b.transpose_to(6, b.bank(6, 0, 128), pads[:, :], ev.ident[:, :],
                            waits=[ld, ms])
        cp = b.evac("V", 6, (lambda e, dst=dst: e.tensor_copy(
            dst[0:R, 0:HD], b.bank(6, 0, HD, np_=R))))
        prev_ld, prev_tr = ld, tr
    b.mm_group(7, b.bank(7, 0, HD, np_=HD),
               [(vt_bf[0:R, 0:HD], ut_bf[0:R, 0:HD])], waits=[cp])
    b.evac("V", 7, (lambda e: e.tensor_copy(vut2[0:64, :],
                                            b.bank(7, 0, HD, np_=HD))))
    b.evac("V", 7, (lambda e: e.tensor_copy(vut2[64:128, :],
                                            b.bank(7, 0, HD, np_=HD))))
    ldv2a = b.dma("G", v2[0:64, 0:R], W["V"])
    ldv2b = b.dma("G", v2[64:128, 0:R], W["V"])
    b.op("V", (lambda e: e.tensor_scalar_mul(v2q[:, :], v2[:, :],
                                             float(R) ** -0.5)),
         waits=[ldv2a, ldv2b])
    b.barrier()

    def wt_prep(nm, slot, war=None):
        toks = []
        buf_free = [war, war]
        for g in range(DC):
            buf = g % 2
            ld = b.dma("Q", stage[:, buf * D:(buf + 1) * D],
                       W[nm][g * 128:(g + 1) * 128, :], waits=[buf_free[buf]])
            for dc in range(DC):
                bk = 4 + (dc % 2)
                b.transpose_to(bk, b.bank(bk, 0, 128),
                               stage[:, buf * D + dc * 128: buf * D + dc * 128 + 128],
                               ev.ident[:, :], waits=[ld])
                dst = slot[:, dc * D + g * 128: dc * D + g * 128 + 128]
                toks.append(b.evac("V", bk, (lambda e, dst=dst, bk=bk:
                                             e.tensor_copy(dst, b.bank(bk, 0, 128)))))
            buf_free[buf] = toks[-1]
        return toks

    def z_proj(zdst, vsrc, src_toks):
        toks = []
        rot = [0]
        for h in range(H):
            for (o, w) in tiles(T, 512):
                bk = rot[0] % 4
                rot[0] += 1
                p0 = (h % 2) * 64
                lh = vsrc[p0:p0 + 64, 0:R]
                rh = qk[p0:p0 + 64, (h // 2) * T + o:(h // 2) * T + o + w]
                b.mm_group(bk, b.bank(bk, 0, w, np_=R), [(lh, rh)],
                           waits=[src_toks[-1]])
                dst = zdst[(h % 2) * 32:(h % 2) * 32 + 32,
                           (h // 2) * T + o:(h // 2) * T + o + w]
                toks.append(b.evac("V", bk, (lambda e, dst=dst, bk=bk, w=w:
                                             e.tensor_copy(dst, b.bank(bk, 0, w, np_=R)))))
        return toks

    wsl = lambda Wt: (lambda dc, ic: Wt[:, dc * D + ic * 128: dc * D + ic * 128 + 128])
    wtq = wt_prep("Wq", wA)
    ldbq = b.dma("G", ev.bias[0:1, 0:D], W["bq"].rearrange("(a d) -> a d", a=1))
    q_toks = proj_fm(b, ev, xln, wsl(wA), qk, bias_row=ev.bias,
                     wait0=[wtq[-1], ldbq])
    zq_toks = z_proj(zq, v2q, q_toks)
    wtk = wt_prep("Wk", wB, war=wtq[-1])
    ldbk = b.dma("G", ev.bias[0:1, 0:D], W["bk"].rearrange("(a d) -> a d", a=1),
                 waits=[q_toks[-1]])
    k_toks = proj_fm(b, ev, xln, wsl(wB), qk, bias_row=ev.bias,
                     wait0=[wtk[-1], ldbk, zq_toks[-1]])
    zk_toks = z_proj(zk, v2, k_toks)

    wveff_toks = [None]
    bufv = [None, None]
    for g in range(DC):
        buf = g % 2
        ld = b.dma("G", gran_bf[:, buf * D:(buf + 1) * D],
                   W["Wv"][g * 128:(g + 1) * 128, :],
                   waits=[bufv[buf], q_toks[-1], wtk[-1], zk_toks[-1]])
        for half in range(2):
            h_ = 2 * g + half
            for dc in range(DC):
                bk = dc % 4
                lh = gran_bf[half * 64:(half + 1) * 64,
                             buf * D + dc * 128: buf * D + dc * 128 + 128]
                rh = vut2[half * 64:(half + 1) * 64, :]
                b.mm_group(bk, b.bank(bk, 0, 64), [(lh, rh)], waits=[ld])
                dst = wA[:, dc * D + h_ * 64: dc * D + h_ * 64 + 64]
                wveff_toks.append(b.evac("V", bk, (lambda e, dst=dst, bk=bk:
                                                   e.tensor_copy(dst, b.bank(bk, 0, 64)))))
        bufv[buf] = wveff_toks[-1]

    ldbv = b.dma("G", ev.bias[0:1, 0:D], W["bv"].rearrange("(a d) -> a d", a=1),
                 waits=[k_toks[-1]])
    cb_toks = []
    for dcb in range(DC):
        b.mm_group(6, b.bank(6, 0, 1),
                   [(ev.bias[0:1, dcb * 128: dcb * 128 + 128],
                     ev.ones_row[0:1, 0:1])], waits=[ldbv])
        dst = ev.colbf[:, dcb:dcb + 1]
        cb_toks.append(b.evac("V", 6, (lambda e, dst=dst: e.tensor_copy(
            dst, b.bank(6, 0, 1)))))
    bve_toks = []
    for h_ in range(H):
        p0 = (h_ % 2) * 64
        lh = ev.colbf[p0:p0 + 64, h_ // 2: h_ // 2 + 1]
        rh = vut2[p0:p0 + 64, :]
        b.mm_group(7, b.bank(7, 0, 64, np_=1), [(lh, rh)], waits=[cb_toks[-1]])
        dst = ev.bias[0:1, h_ * 64: h_ * 64 + 64]
        bve_toks.append(b.evac("V", 7, (lambda e, dst=dst: e.tensor_copy(
            dst, b.bank(7, 0, 64, np_=1)))))

    von_ms = b.op("G", lambda e: e.memset(v_ones[:, :], 1.0),
                  waits=[zk_toks[-1]])
    wslA_r = lambda dc, io, iw: wA[:, dc * D + io: dc * D + io + iw]
    v_toks = proj_tm_vones(b, ev, xln, wslA_r, v_ones, bias_row=ev.bias,
                           wait0=[wveff_toks[-1], bve_toks[-1], von_ms])
    b.barrier()

    wto = wt_prep("Wo", wB)
    ldbo = b.dma("G", ev.bias[0:1, 0:D], W["bo"].rearrange("(a d) -> a d", a=1),
                 waits=[v_toks[-1]])
    b.barrier()

    sc_lh = lambda h, kc: zk[(h % 2) * 32:(h % 2) * 32 + 32,
                             (h // 2) * T + kc * 128:(h // 2) * T + kc * 128 + 128]
    sc_rh = lambda h, go, gw: zq[(h % 2) * 32:(h % 2) * 32 + 32,
                                 (h // 2) * T + go:(h // 2) * T + go + gw]
    if KC * 512 <= WELS:
        ptil2 = ev.a1(ev.R1, KC * 512)
    else:
        ptil2 = ev.a1(ev.R1 + 2 * WELS + 256, KC * 512)
    attention_core(b, ev, sc_lh, sc_rh, v_ones, y, False, ptil2)
    b.barrier()
    wslo = lambda dc, dco: wB[:, dc * D + dco * 128: dc * D + dco * 128 + 128]
    st = out_proj_residual(b, ev, y, wslo, x_in, x_out, bias_row=ev.bias,
                           wait0=[wto[-1], ldbo])
    b.barrier()
    return st



def block_moe(b: Bld, ev: Env, x_in, out_tm, W):
    cfg = ev.cfg
    D, T, FF, NE = cfg["D"], cfg["T"], cfg["FF"], cfg["NE"]
    DC = D // 128
    DCT = DC * T
    TH = T // 2
    FC = FF // 128
    FQ = FF // 4
    KC = T // 128
    TCH = TH // 128
    xln = ev.a1(ev.R2, DCT)
    wq_els = DC * FQ

    def acc_sl(tc):
        if tc < 8:
            return ev.a1f(ev.R0 + tc * 2 * D, D)
        return ev.a1f(ev.R1 + (tc - 8) * 2 * D, D)

    def h_sl(fc, o, w):
        if fc < 16:
            return ev.a1(ev.R3 + fc * TH + o, w)
        if fc < 24:
            return ev.a1(ev.R4 + (fc - 16) * TH + o, w)
        return ev.a2((fc - 24) * TH + o, w)

    wdring = ev.a2(8 * TH + 2 * DC * max(FF // 8, 128), 4096)
    xlnf = ev.xlnf_dram

    layernorm(b, ev, x_in, xlnf_dram=xlnf)
    b.barrier()

    pads = ev.a1f(ev.R4, D)
    asb = ev.a1f(ev.R4 + 2 * D, 512)
    xlf = [ev.a1f(ev.R4 + 2 * D + 1024, 512),
           ev.a1f(ev.R4 + 2 * D + 2048, 512)]
    ms = b.op("G", lambda e: e.memset(pads[:, :], 0.0))
    msa = b.op("G", lambda e: e.memset(asb[:, :], 0.0))
    ldr = b.dma("Q", pads[0:NE, 0:D], W["router_w"], waits=[ms])
    rw_toks = []
    for dc in range(DC):
        bk = 4 + (dc % 2)
        b.transpose_to(bk, b.bank(bk, 0, 128),
                       pads[:, dc * 128: dc * 128 + 128], ev.ident[:, :],
                       waits=[ldr, ms])
        dstf = ev.rwtf[:, dc * NE: dc * NE + NE]
        rw_toks.append(b.evac("V", bk, (lambda e, dstf=dstf, bk=bk:
                                        e.tensor_copy(dstf, b.bank(bk, 0, NE)))))
    sm = ev.smalls
    slot_rd = [None, None]
    for ti, (o, wti) in enumerate(tiles(T, 512)):
        Abk = 2 + (ti % 2)
        pre_w = b.bank_r[Abk] + [b.bank_w[Abk], rw_toks[-1]]
        for dc in range(DC):
            par = dc % 2
            ldx = b.dma("Q", xlf[par][:, 0:wti],
                        xlnf[dc * 128:(dc + 1) * 128, o:o + wti],
                        waits=[slot_rd[par]])
            lh = ev.rwtf[:, dc * NE: dc * NE + NE]
            slot_rd[par] = b.op(
                "T", (lambda e, lh=lh, par=par, wti=wti, Abk=Abk,
                      st=(dc == 0), sp=(dc == DC - 1):
                      e.matmul(b.bank(Abk, 0, wti, np_=NE), lh,
                               xlf[par][:, 0:wti], start=st, stop=sp)),
                waits=[ldx] + (pre_w if dc == 0 else []), inc=True)
        b.bank_w[Abk] = slot_rd[(DC - 1) % 2]
        b.bank_r[Abk] = []
        cpA = b.evac("V", Abk, (lambda e, wti=wti, Abk=Abk: e.tensor_copy(
            asb[0:NE, 0:wti], b.bank(Abk, 0, wti, np_=NE))), waits=[msa])
        for j in range(wti // 128):
            tc = (o + j * 128) // 128
            bk = 6 + (j % 2)
            b.transpose_to(bk, b.bank(bk, 0, 128),
                           asb[:, j * 128: j * 128 + 128], ev.ident[:, :],
                           waits=[cpA])
            _moe_top2(b, ev, bk, tc)
    b.barrier()

    z0 = b.op("G", lambda e: e.memset(ev.a1f(ev.R0, 8 * D)[:, :], 0.0))
    z1 = b.op("G", lambda e: e.memset(ev.a1f(ev.R1, 8 * D)[:, :], 0.0))
    b.barrier()

    FF8 = max(FF // 8, 128)
    NPH = FF // FF8
    w8els = DC * FF8
    wq8 = [ev.a2(8 * TH, w8els), ev.a2(8 * TH + w8els, w8els)]
    slot_free = [None, None]
    wq_tok = [None]
    for e_ in range(NE):
        for th in range(2):
            for gate in (True, False):
                nm = "w_gate" if gate else "w_up"
                for ph in range(NPH):
                    slot = ph % 2
                    ld = b.dma("G", wq8[slot].rearrange("p (c n) -> p c n", c=DC),
                               W[nm][e_].rearrange("(c p) n -> p c n", p=128)[
                                   :, :, ph * FF8:(ph + 1) * FF8],
                               waits=[slot_free[slot]])
                    rot = [0]
                    tok = None
                    for fcl in range(FF8 // 128):
                        fc = ph * (FF8 // 128) + fcl
                        for (o, w) in tiles(TH, 512):
                            bk = rot[0] % 2
                            rot[0] += 1
                            pairs = [(wq8[slot][:, dc * FF8 + fcl * 128:
                                                dc * FF8 + fcl * 128 + 128],
                                      xln[:, dc * T + th * TH + o:
                                          dc * T + th * TH + o + w])
                                     for dc in range(DC)]
                            b.mm_group(bk, b.bank(bk, 0, w), pairs, waits=[ld])
                            dst = h_sl(fc, o, w)
                            if gate:
                                tok = b.evac("S", bk, (lambda e, dst=dst, bk=bk,
                                                       w=w: e.activation(
                                    dst, b.bank(bk, 0, w), AF.Silu)))
                            else:
                                tok = b.evac("V", bk, (lambda e, dst=dst, bk=bk,
                                                       w=w: e.tensor_mul(
                                    dst, b.bank(bk, 0, w), dst)))
                    slot_free[slot] = tok
                    wq_tok[0] = tok
            ring_tok = [None, None]
            for dh in range(2):
                dhw = min(512, D - dh * 512)
                if dhw <= 0:
                    break
                first_done = [False] * TCH
                last_mm = [None]
                for fj in range(FC // 4):
                    slot = fj % 2
                    ld = b.dma("G",
                               wdring[:, slot * 2048: slot * 2048 + 4 * dhw].rearrange(
                                   "p (c n) -> p c n", c=4),
                               W["w_down"][e_].rearrange("(c p) n -> p c n", p=128)[
                                   :, 4 * fj: 4 * fj + 4, dh * 512: dh * 512 + dhw],
                               waits=[ring_tok[slot]])
                    for fc4 in range(4):
                        fc = fj * 4 + fc4
                        for tc in range(TCH):
                            lh = h_sl(fc, tc * 128, 128)
                            rh = wdring[:, slot * 2048 + fc4 * dhw:
                                        slot * 2048 + fc4 * dhw + dhw]
                            st_ = (fc == 0)
                            sp_ = (fc == FC - 1)
                            w_ = [ld, wq_tok[0]]
                            if st_:
                                w_ += b.bank_r[tc] + [b.bank_w[tc]]
                            tok = b.op("T", (lambda e, lh=lh, rh=rh, st_=st_,
                                             sp_=sp_, tc=tc, dhw=dhw:
                                             e.matmul(b.bank(tc, 0, dhw), lh, rh,
                                                      start=st_, stop=sp_)),
                                       waits=w_,
                                       inc=(fc4 == 3 and tc == TCH - 1))
                    ring_tok[slot] = tok
                    last_mm[0] = tok
                for tc in range(TCH):
                    b.bank_w[tc] = last_mm[0]
                    b.bank_r[tc] = []
                    gtc = th * TCH + tc
                    a_sl = acc_sl(gtc)[:, dh * 512: dh * 512 + dhw]
                    c_sl = ev.cw[:, gtc * NE + e_: gtc * NE + e_ + 1]
                    b.evac("V", tc, (lambda e, a_sl=a_sl, c_sl=c_sl, tc=tc,
                                     dhw=dhw: e.scalar_tensor_tensor(
                        a_sl, b.bank(tc, 0, dhw), c_sl, a_sl,
                        ALU.mult, ALU.add)), waits=[z0, z1])
    b.barrier()

    def xfull(dc):
        if dc < 4:
            return ev.a1f(ev.R3 + dc * 2 * T, T)
        if dc < 6:
            return ev.a1f(ev.R4 + (dc - 4) * 2 * T, T)
        return ev.a2f((dc - 6) * 2 * T, T)

    ld_toks = []
    for dc in range(DC):
        ld_toks.append(b.dma("Q", xfull(dc)[:, :],
                             x_in[dc * 128:(dc + 1) * 128, :]))
    obuf = ev.a2f(8 * TH, 2 * D)
    st_tok = [None, None]
    for tc in range(KC):
        par = tc % 2
        tts = []
        for dc in range(DC):
            bk = 4 + (dc % 2)
            b.transpose_to(bk, b.bank(bk, 0, 128),
                           xfull(dc)[:, tc * 128: tc * 128 + 128],
                           ev.ident[:, :], waits=[ld_toks[dc]])
            dst = obuf[:, par * D + dc * 128: par * D + dc * 128 + 128]
            a_sl = acc_sl(tc)[:, dc * 128: dc * 128 + 128]
            tts.append(b.evac("V", bk, (lambda e, dst=dst, a_sl=a_sl, bk=bk:
                                        e.tensor_add(dst, b.bank(bk, 0, 128), a_sl)),
                              waits=[st_tok[par]]))
        st_tok[par] = b.dma("Q", out_tm[tc * 128:(tc + 1) * 128, :],
                            obuf[:, par * D:(par + 1) * D], waits=tts)
    b.barrier()
    return [t for t in st_tok if t]


def _moe_top2(b, ev, bk, tc):
    NE = ev.cfg["NE"]
    sm = ev.smalls
    if True:
        b.evac("V", bk, (lambda e, bk=bk: e.reduce_max(
            sm[:, 0:1], b.bank(bk, 0, NE), axis=AX.X)))
        tv = b.evac("V", bk, (lambda e, bk=bk: e.tensor_scalar(
            sm[:, 8:8 + NE], b.bank(bk, 0, NE), sm[:, 0:1], None, ALU.subtract)))
        ts = b.op("S", (lambda e: e.activation(sm[:, 16:16 + NE],
                                               sm[:, 8:8 + NE], AF.Exp)),
                  waits=[tv])
        b.op("V", (lambda e: e.reduce_sum(sm[:, 1:2], sm[:, 16:16 + NE],
                                          axis=AX.X)), waits=[ts])
        b.op("V", lambda e: e.reciprocal(sm[:, 2:3], sm[:, 1:2]))
        b.op("V", lambda e: e.tensor_scalar(
            sm[:, 24:24 + NE], sm[:, 16:16 + NE], sm[:, 2:3], None, ALU.mult))
        b.op("V", lambda e: e.reduce_max(sm[:, 3:4], sm[:, 24:24 + NE], axis=AX.X))
        b.op("V", lambda e: e.tensor_scalar(
            sm[:, 32:32 + NE], sm[:, 24:24 + NE], sm[:, 3:4], 1e9,
            ALU.is_equal, ALU.mult))
        b.op("V", lambda e: e.tensor_tensor(
            sm[:, 40:40 + NE], sm[:, 24:24 + NE], sm[:, 32:32 + NE], ALU.subtract))
        b.op("V", lambda e: e.reduce_max(sm[:, 4:5], sm[:, 40:40 + NE], axis=AX.X))
        b.op("V", lambda e: e.tensor_scalar(
            sm[:, 48:48 + NE], sm[:, 24:24 + NE], sm[:, 4:5], None, ALU.is_ge))
        b.op("V", lambda e: e.tensor_tensor(
            sm[:, 48:48 + NE], sm[:, 48:48 + NE], sm[:, 24:24 + NE], ALU.mult))
        b.op("V", lambda e: e.tensor_add(sm[:, 5:6], sm[:, 3:4], sm[:, 4:5]))
        b.op("V", lambda e: e.reciprocal(sm[:, 6:7], sm[:, 5:6]))
        dst = ev.cw[:, tc * NE:(tc + 1) * NE]
        b.op("V", (lambda e, dst=dst: e.tensor_scalar(
            dst, sm[:, 48:48 + NE], sm[:, 6:7], None, ALU.mult)))



def block_moe_sparse(b: Bld, ev: Env, x_in, out_tm, W):
    cfg = ev.cfg
    D, T, FF, NE = cfg["D"], cfg["T"], cfg["FF"], cfg["NE"]
    DC = D // 128
    DCT = DC * T
    C = moe_cap(T)
    JCN = C // 128
    FC = FF // 128
    KC = T // 128
    xln = ev.a1(ev.R2, DCT)
    xlnf = ev.xlnf_dram
    xtm = ev.xtm_dram
    yed = ev.yed_dram
    t1d = ev.t1_dram
    t2d = ev.t2_dram
    idxd = ev.idx_dram
    cwd = ev.cw_dram

    def h_sl(fc, o, w):
        if fc < 16:
            return ev.a1(ev.R3 + fc * C + o, w)
        if fc < 24:
            return ev.a1(ev.R4 + (fc - 16) * C + o, w)
        return ev.a2((fc - 24) * C + o, w)

    wdring = ev.a2(8 * C + 2 * DC * max(FF // 8, 128), 4096)

    bregs = {}

    def _mk_regs(e):
        bregs["C"] = e.to_reg(C - 1)
        bregs["T"] = e.to_reg(T - 1)
        bregs["Y"] = e.to_reg(NE * C + 127)
        return e.engine_nop()
    b.op("G", _mk_regs)

    layernorm(b, ev, x_in, xlnf_dram=xlnf, xtm_dram=xtm)
    b.barrier()

    pads = ev.a1f(ev.R4, D)
    asb = ev.a1f(ev.R4 + 2 * D, 512)
    xlf = [ev.a1f(ev.R4 + 2 * D + 1024, 512),
           ev.a1f(ev.R4 + 2 * D + 2048, 512)]
    ms = b.op("G", lambda e: e.memset(pads[:, :], 0.0))
    msa = b.op("G", lambda e: e.memset(asb[:, :], 0.0))
    ldr = b.dma("Q", pads[0:NE, 0:D], W["router_w"], waits=[ms])
    rw_toks = []
    for dc in range(DC):
        bk = 4 + (dc % 2)
        b.transpose_to(bk, b.bank(bk, 0, 128),
                       pads[:, dc * 128: dc * 128 + 128], ev.ident[:, :],
                       waits=[ldr, ms])
        dstf = ev.rwtf[:, dc * NE: dc * NE + NE]
        rw_toks.append(b.evac("V", bk, (lambda e, dstf=dstf, bk=bk:
                                        e.tensor_copy(dstf, b.bank(bk, 0, NE)))))
    sm = ev.smalls
    prefill_toks = []
    for e_ in range(NE):
        prefill_toks.append(b.dma(
            "Q", idxd[e_].rearrange("(p a) -> p a", p=128),
            ev.ifill[:, 0:C // 128]))
    carry = ev.rows[0:1, 2050:2050 + NE]
    bx = ev.R4 + 2 * D + 3072
    mk = ev.a1f(bx, NE)
    rank = ev.a1f(bx + 16, NE)
    rib = [ev.locs[:, 32 + 8 * i: 32 + 8 * i + NE] for i in range(4)]
    cz = b.op("V", lambda e: e.memset(carry, 0.0))
    st = {"carry_t": cz, "idx": [],
          "ri_rd": [[] for _ in range(4)]}
    idx_sc_toks = st["idx"]

    def idx_stage(tc):
        xpar = tc % 4
        cwt = ev.cw[:, tc * NE:(tc + 1) * NE]
        tmk = b.op("V", (lambda e, cwt=cwt: e.tensor_single_scalar(
            mk, cwt, 0.0, ALU.is_gt)))
        bkr = tc % 2
        b.mm_group(bkr, b.bank(bkr, 0, NE),
                   [(ev.ustrict[:, :], mk),
                    (ev.ones_row_f[0:1, 0:128], carry)],
                   waits=[tmk, st["carry_t"]])
        b.evac("V", bkr, (lambda e, bkr=bkr: e.tensor_copy(
            rank, b.bank(bkr, 0, NE))))
        bkc = 4
        b.mm_group(bkc, b.bank(bkc, 0, NE, np_=1),
                   [(ev.ones_col_f[:, :], mk)], waits=[tmk])
        st["carry_t"] = b.evac("V", bkc, (lambda e: e.tensor_tensor(
            carry, carry, b.bank(bkc, 0, NE, np_=1), ALU.add)))
        b.op("V", lambda e: e.tensor_single_scalar(rank, rank, 65536.0,
                                                   ALU.subtract))
        b.op("V", lambda e: e.tensor_tensor(rank, rank, mk, ALU.mult))
        b.op("V", lambda e: e.tensor_single_scalar(rank, rank, 65536.0, ALU.add))
        ri = rib[xpar]
        cvi = b.op("V", (lambda e, ri=ri: e.tensor_copy(ri, rank)),
                   waits=st["ri_rd"][xpar])
        st["ri_rd"][xpar] = []
        for e_ in range(NE):
            tk = b.idma(
                (lambda e, e_=e_, tc=tc, ri=ri: e.indirect_dma_start(
                    idxd.rearrange("e (c a) -> (e c) a", a=1),
                    IndirectOffsetOnAxis(ri[:, e_:e_ + 1], 0),
                    ev.iota[:, tc:tc + 1], None,
                    element_offset=e_ * C,
                    bounds_check=bregs["C"], oob_is_err=False)),
                waits=[cvi] + prefill_toks)
            st["idx"].append(tk)
            st["ri_rd"][xpar].append(tk)

    slot_rd = [None, None]
    for ti, (o, wti) in enumerate(tiles(T, 512)):
        Abk = 2 + (ti % 2)
        pre_w = b.bank_r[Abk] + [b.bank_w[Abk], rw_toks[-1]]
        for dc in range(DC):
            par = dc % 2
            ldx = b.dma("Q", xlf[par][:, 0:wti],
                        xlnf[dc * 128:(dc + 1) * 128, o:o + wti],
                        waits=[slot_rd[par]])
            lh = ev.rwtf[:, dc * NE: dc * NE + NE]
            slot_rd[par] = b.op(
                "T", (lambda e, lh=lh, par=par, wti=wti, Abk=Abk,
                      st=(dc == 0), sp=(dc == DC - 1):
                      e.matmul(b.bank(Abk, 0, wti, np_=NE), lh,
                               xlf[par][:, 0:wti], start=st, stop=sp)),
                waits=[ldx] + (pre_w if dc == 0 else []), inc=True)
        b.bank_w[Abk] = slot_rd[(DC - 1) % 2]
        b.bank_r[Abk] = []
        cpA = b.evac("V", Abk, (lambda e, wti=wti, Abk=Abk: e.tensor_copy(
            asb[0:NE, 0:wti], b.bank(Abk, 0, wti, np_=NE))), waits=[msa])
        for j in range(wti // 128):
            tc = (o + j * 128) // 128
            bk = 6 + (j % 2)
            b.transpose_to(bk, b.bank(bk, 0, 128),
                           asb[:, j * 128: j * 128 + 128], ev.ident[:, :],
                           waits=[cpA])
            _moe_top2(b, ev, bk, tc)
            idx_stage(tc)

    st_cw = b.dma("Q", cwd.rearrange("(c p) n -> p c n", p=128),
                  ev.cw.rearrange("p (c n) -> p c n", n=NE),
                  waits=idx_sc_toks[-1:])
    locs = ev.locs
    NJ = NE * JCN
    lfil = b.op("G", lambda e: e.memset(locs[:, 0:KC], NE * C),
                waits=idx_sc_toks[-NE:])
    zbuf = ev.a1f(ev.R1 + 8192, 1024)
    zz = b.op("V", lambda e: e.memset(zbuf[:, :], 0.0))
    zpad = b.dma("Q", yed[NE * C:NE * C + 128, :],
                 zbuf.bitcast(BF16)[:, 0:D], waits=[zz])
    pf1 = b.dma("Q", t1d.rearrange("(p c) a -> p (c a)", p=128),
                locs[:, 0:KC], waits=[lfil])
    pf2 = b.dma("Q", t2d.rearrange("(p c) a -> p (c a)", p=128),
                locs[:, 0:KC], waits=[lfil])
    lio = b.op("G", lambda e: e.iota(locs[:, 0:NJ], pattern=[[128, NJ]],
                                     base=0, channel_multiplier=1),
               waits=[pf1, pf2] + idx_sc_toks[-NE:])
    i2sb = ev.smalls[:, 0:16].bitcast(mybir.dt.int32)
    tb_buf = {1: i2sb[:, 8:16], 2: i2sb[:, 0:8]}
    tb_prev = {1: [pf1, lio], 2: [pf2, lio]}

    def tb_round(k):
        for cid, tbl, e_ in ((2, t2d, k), (1, t1d, NE - 1 - k)):
            buf = tb_buf[cid]
            toks = []
            for jc in range(JCN):
                lde = b.dma("Q", buf[:, jc:jc + 1],
                            idxd[e_, jc * 128:(jc + 1) * 128].rearrange(
                                "(p a) -> p a", p=128),
                            waits=tb_prev[cid] + idx_sc_toks[-NE:])
                toks.append(b.idma(
                    (lambda e, tbl=tbl, buf=buf, jc=jc, e_=e_:
                     e.indirect_dma_start(
                         tbl[:, :],
                         IndirectOffsetOnAxis(buf[:, jc:jc + 1], 0),
                         locs[:, e_ * JCN + jc: e_ * JCN + jc + 1], None,
                         bounds_check=bregs["T"], oob_is_err=False)),
                    waits=[lde] + tb_prev[cid]))
            tb_prev[cid] = toks
    b.barrier()

    FF8 = max(FF // 8, 128)
    NPH = FF // FF8
    w8els = DC * FF8
    wq8 = [ev.a2(8 * C, w8els), ev.a2(8 * C + w8els, w8els)]
    xgt = [ev.a1(ev.R0, JCN * D), ev.a1(ev.R0 + 8192, JCN * D)]
    xgf = [ev.a1(ev.R1, DC * C), ev.a1(ev.R2, DC * C)]
    y_sb = ev.a1(ev.R1 + DC * C, 4 * 1024)
    cwcp = [ev.smalls[:, 24 + 8 * i: 24 + 8 * i + JCN] for i in range(4)]
    slot_free = [None, None]
    wq_tok = [None]
    y_war = [[] for _ in range(JCN)]
    up_w = idx_sc_toks[-NE:]
    xgt_war = [list(up_w), list(up_w)]
    fm_war = [[], []]
    cw_war = [list(up_w) for _ in range(4)]
    ld_x = {}
    ld_cw = {}
    trd = {}
    idx_sb = ev.colbf[:, 0:16].bitcast(mybir.dt.int32)
    idxp = [idx_sb, ev.smalls[:, 16:24].bitcast(mybir.dt.int32)]
    cwg = ev.smalls[:, 56:56 + NE]
    prev_cc = [None]
    STG = globals().get("DBG_MOE_STG", 99)
    NEXP = NE if DBG_MOE_NEXP is None else DBG_MOE_NEXP

    def load(e_):
        par, p4 = e_ % 2, e_ % 4
        xts, cts, ldis = [], [], []
        for jc in range(JCN):
            ldis.append(b.dma("Q", idxp[par][:, jc:jc + 1],
                        idxd[e_, jc * 128:(jc + 1) * 128].rearrange(
                            "(p a) -> p a", p=128),
                        waits=(xgt_war[par] + idx_sc_toks[-NE:]) if jc == 0
                        else ()))
        for jc in range(JCN):
            cg = b.idma((lambda e, jc=jc, par=par: e.indirect_dma_start(
                cwg, None, cwd[:, :],
                IndirectOffsetOnAxis(idxp[par][:, jc:jc + 1], 0),
                bounds_check=bregs["T"], oob_is_err=False)),
                waits=[ldis[jc], prev_cc[0], st_cw]
                + (xgt_war[par] if jc == 0 else []))
            cc = b.op("V", (lambda e, jc=jc, e_=e_, p4=p4: e.tensor_copy(
                cwcp[p4][:, jc:jc + 1], cwg[:, e_:e_ + 1])),
                waits=[cg] + (cw_war[p4] if jc == 0 else []))
            prev_cc[0] = cc
            cts.append(cc)
        for jc in range(JCN):
            xts.append(b.idma((lambda e, jc=jc, par=par: e.indirect_dma_start(
                xgt[par][:, jc * D: jc * D + D], None, xtm[:, :],
                IndirectOffsetOnAxis(idxp[par][:, jc:jc + 1], 0),
                bounds_check=bregs["T"], oob_is_err=False)),
                waits=[ldis[jc]] + (xgt_war[par] if jc == 0 else [])))
        ld_x[e_], ld_cw[e_] = xts, cts

    def xpose(e_):
        par = e_ % 2
        jlast = None
        for jc in range(JCN):
            for dc in range(DC):
                bt = 6 + (dc % 2)
                src = xgt[par][:, jc * D + dc * 128: jc * D + dc * 128 + 128]
                b.transpose_to(bt, b.bankb(bt, 0, 128), src, ev.identb[:, :],
                               waits=[ld_x[e_][jc]])
                dst = xgf[par][:, dc * C + jc * 128: dc * C + jc * 128 + 128]
                jlast = b.evac("V", bt, (lambda e, dst=dst, bt=bt:
                                         e.tensor_copy(dst, b.bankb(bt, 0, 128))),
                               waits=fm_war[par])
        trd[e_] = jlast
        xgt_war[par] = [jlast, ld_cw[e_][-1]]

    def compute(e_):
        par, p4 = e_ % 2, e_ % 4
        xg_fm = xgf[par]
        cwc = cwcp[p4]
        for gate in (True, False):
            nm = "w_gate" if gate else "w_up"
            for ph in range(NPH):
                slot = ph % 2
                ld = b.dma("G", wq8[slot].rearrange("p (c n) -> p c n", c=DC),
                           W[nm][e_].rearrange("(c p) n -> p c n", p=128)[
                               :, :, ph * FF8:(ph + 1) * FF8],
                           waits=[slot_free[slot]])
                rot = [0]
                tok = None
                for fcl in range(FF8 // 128):
                    fc = ph * (FF8 // 128) + fcl
                    for (o, w) in tiles(C, 512):
                        bk = rot[0] % 4
                        rot[0] += 1
                        pairs = [(wq8[slot][:, dc * FF8 + fcl * 128:
                                            dc * FF8 + fcl * 128 + 128],
                                  xg_fm[:, dc * C + o: dc * C + o + w])
                                 for dc in range(DC)]
                        b.mm_group(bk, b.bank(bk, 0, w), pairs,
                                   waits=[ld, trd[e_]])
                        dst = h_sl(fc, o, w)
                        if gate:
                            tok = b.evac("S", bk, (lambda e, dst=dst, bk=bk,
                                                   w=w: e.activation(
                                dst, b.bank(bk, 0, w), AF.Silu)))
                        else:
                            tok = b.evac("V", bk, (lambda e, dst=dst, bk=bk,
                                                   w=w: e.tensor_mul(
                                dst, b.bank(bk, 0, w), dst)))
                slot_free[slot] = tok
                wq_tok[0] = tok
        fm_war[par] = [wq_tok[0]]
        if STG < 5:
            return
        ring_tok = [None, None]
        ev_last = None
        for dh in range(2):
            dhw = min(512, D - dh * 512)
            if dhw <= 0:
                break
            last_mm = [None]
            for fj in range(FC // 4):
                slot = fj % 2
                ld = b.dma("G",
                           wdring[:, slot * 2048: slot * 2048 + 4 * dhw].rearrange(
                               "p (c n) -> p c n", c=4),
                           W["w_down"][e_].rearrange("(c p) n -> p c n", p=128)[
                               :, 4 * fj: 4 * fj + 4, dh * 512: dh * 512 + dhw],
                           waits=[ring_tok[slot]])
                for fc4 in range(4):
                    fc = fj * 4 + fc4
                    for jc in range(JCN):
                        lh = h_sl(fc, jc * 128, 128)
                        rh = wdring[:, slot * 2048 + fc4 * dhw:
                                    slot * 2048 + fc4 * dhw + dhw]
                        st_ = (fc == 0)
                        sp_ = (fc == FC - 1)
                        w_ = [ld, wq_tok[0]]
                        if st_:
                            w_ += b.bank_r[jc] + [b.bank_w[jc]]
                        tok = b.op("T", (lambda e, lh=lh, rh=rh, st_=st_,
                                         sp_=sp_, jc=jc, dhw=dhw:
                                         e.matmul(b.bank(jc, 0, dhw), lh, rh,
                                                  start=st_, stop=sp_)),
                                   waits=w_,
                                   inc=(fc4 == 3 and jc == JCN - 1))
                ring_tok[slot] = tok
                last_mm[0] = tok
            for jc in range(JCN):
                b.bank_w[jc] = last_mm[0]
                b.bank_r[jc] = []
                ydst = y_sb[:, jc * 512: jc * 512 + dhw]
                ev_t = b.evac("V", jc, (lambda e, ydst=ydst, jc=jc, dhw=dhw,
                                        cwc=cwc:
                                        e.tensor_scalar(ydst, b.bank(jc, 0, dhw),
                                                        cwc[:, jc:jc + 1], None,
                                                        ALU.mult)),
                              waits=y_war[jc] + [ld_cw[e_][jc]])
                ev_last = ev_t
                if STG < 6:
                    continue
                y_war[jc] = [b.dma(
                    "Q", yed[e_ * C + jc * 128: e_ * C + jc * 128 + 128,
                             dh * 512: dh * 512 + dhw],
                    ydst, waits=[ev_t])]
        cw_war[p4] = [ev_last]
        tb_round(e_)

    for k in range(min(2, NEXP)):
        load(k)
    if NEXP > 0:
        xpose(0)
    if NEXP > 2:
        load(2)
    for e_ in range(NEXP):
        if e_ + 1 < NEXP:
            xpose(e_ + 1)
        if e_ + 3 < NEXP:
            load(e_ + 3)
        if STG < 4:
            continue
        compute(e_)
    b.barrier()

    def xfull(dc):
        if dc < 4:
            return ev.a1f(ev.R3 + dc * 2 * T, T)
        if dc < 6:
            return ev.a1f(ev.R4 + (dc - 4) * 2 * T, T)
        return ev.a2f((dc - 6) * 2 * T, T)

    ld_toks = []
    for dc in range(DC):
        ld_toks.append(b.dma("Q", xfull(dc)[:, :],
                             x_in[dc * 128:(dc + 1) * 128, :]))
    obuf = ev.a2f(max(8 * C, max(DC - 6, 0) * 2 * T), 2 * D)
    abuf = [ev.a1(ev.R0, D), ev.a1(ev.R0 + D, D)]
    bbuf = [ev.a1(ev.R0 + 2 * D, D), ev.a1(ev.R0 + 3 * D, D)]
    gsum = [ev.a1f(ev.R0 + 4 * D, D), ev.a1f(ev.R0 + 6 * D, D)]
    st_tok = [None, None]
    tc_war = [[], []]
    for tc in range(KC):
        par = tc % 2
        ldt = b.dma("Q", idx_sb[:, 2 * par: 2 * par + 1],
                    t1d[tc * 128:(tc + 1) * 128, :], waits=tc_war[par])
        ldt2 = b.dma("Q", idx_sb[:, 2 * par + 1: 2 * par + 2],
                     t2d[tc * 128:(tc + 1) * 128, :], waits=tc_war[par])
        g1 = b.idma((lambda e, par=par: e.indirect_dma_start(
            abuf[par][:, :], None, yed[:, :],
            IndirectOffsetOnAxis(idx_sb[:, 2 * par: 2 * par + 1], 0),
            bounds_check=bregs["Y"], oob_is_err=False)),
            waits=[ldt, st_tok[par]])
        g2 = b.idma((lambda e, par=par: e.indirect_dma_start(
            bbuf[par][:, :], None, yed[:, :],
            IndirectOffsetOnAxis(idx_sb[:, 2 * par + 1: 2 * par + 2], 0),
            bounds_check=bregs["Y"], oob_is_err=False)),
            waits=[ldt2])
        g12 = b.op("V", (lambda e, par=par: e.tensor_add(
            gsum[par][:, :], abuf[par][:, :], bbuf[par][:, :])),
            waits=[g1, g2])
        tc_war[par] = [g12]
        tts = []
        for dc in range(DC):
            bk = 4 + (dc % 2)
            b.transpose_to(bk, b.bank(bk, 0, 128),
                           xfull(dc)[:, tc * 128: tc * 128 + 128],
                           ev.ident[:, :], waits=[ld_toks[dc]])
            dst = obuf[:, par * D + dc * 128: par * D + dc * 128 + 128]
            a_sl = gsum[par][:, dc * 128: dc * 128 + 128]
            tts.append(b.evac("V", bk, (lambda e, dst=dst, a_sl=a_sl, bk=bk:
                                        e.tensor_add(dst, b.bank(bk, 0, 128), a_sl)),
                              waits=[g12, st_tok[par]]))
        st_tok[par] = b.dma("Q", out_tm[tc * 128:(tc + 1) * 128, :],
                            obuf[:, par * D:(par + 1) * D], waits=tts)
    b.barrier()
    return [t for t in st_tok if t]



BLOCK_WEIGHTS = {
    "l0a": [("Wq", "L0_attn_Wq"), ("Wk", "L0_attn_Wk"), ("Wv", "L0_attn_Wv"),
            ("Wo", "L0_attn_Wo")],
    "l0f": [("W1", "L0_ffn_W1"), ("b1", "L0_ffn_b1"), ("W2", "L0_ffn_W2"),
            ("b2", "L0_ffn_b2")],
    "l1a": [("Wq", "L1_attn_Wq"), ("bq", "L1_attn_bq"), ("Wk", "L1_attn_Wk"),
            ("bk", "L1_attn_bk"), ("Wv", "L1_attn_Wv"), ("bv", "L1_attn_bv"),
            ("Wo", "L1_attn_Wo"), ("bo", "L1_attn_bo"), ("U", "L1_attn_U"),
            ("V", "L1_attn_V")],
    "moe": [("router_w", "L1_ffn_router_w"), ("w_gate", "L1_ffn_w_gate"),
            ("w_up", "L1_ffn_w_up"), ("w_down", "L1_ffn_w_down")],
}

WSHAPES = {
    "L0_attn_Wq": ("D", "D"), "L0_attn_Wk": ("D", "D"),
    "L0_attn_Wv": ("D", "D"), "L0_attn_Wo": ("D", "D"),
    "L0_ffn_W1": ("FF", "D"), "L0_ffn_b1": ("FF",),
    "L0_ffn_W2": ("D", "FF"), "L0_ffn_b2": ("D",),
    "L1_attn_Wq": ("D", "D"), "L1_attn_bq": ("D",),
    "L1_attn_Wk": ("D", "D"), "L1_attn_bk": ("D",),
    "L1_attn_Wv": ("D", "D"), "L1_attn_bv": ("D",),
    "L1_attn_Wo": ("D", "D"), "L1_attn_bo": ("D",),
    "L1_attn_U": ("HD", "R"), "L1_attn_V": ("HD", "R"),
    "L1_ffn_router_w": ("NE", "D"),
    "L1_ffn_w_gate": ("NE", "D", "FF"), "L1_ffn_w_up": ("NE", "D", "FF"),
    "L1_ffn_w_down": ("NE", "FF", "D"),
}

BLOCK_FN = {}


def build_program(cfg, parts):
    nc = bass.Bass("TRN2", target_bir_lowering=False)
    stack = contextlib.ExitStack()
    b = Bld(nc, stack)
    ev = Env(b, stack, cfg)
    b.barrier()
    D, T = cfg["D"], cfg["T"]

    ev.xlnf_dram = None
    ev.xtm_dram = ev.yed_dram = ev.idx_dram = ev.cw_dram = None
    ev.t1_dram = ev.t2_dram = None
    if "moe" in parts:
        ev.xlnf_dram = nc.dram_tensor("xlnf", [D, T], F32, kind="Internal").ap()
        if SPARSE_MOE:
            NE = cfg["NE"]
            _k = "ExternalOutput" if globals().get("DBG_SPARSE") else "Internal"
            CP = moe_cap(T)
            ev.xtm_dram = nc.dram_tensor("xtm", [T, D], BF16, kind=_k).ap()
            ev.yed_dram = nc.dram_tensor("yed", [NE * CP + 128, D], BF16,
                                         kind=_k).ap()
            ev.t1_dram = nc.dram_tensor("t1d", [T, 1], mybir.dt.int32,
                                        kind=_k).ap()
            ev.t2_dram = nc.dram_tensor("t2d", [T, 1], mybir.dt.int32,
                                        kind=_k).ap()
            ev.idx_dram = nc.dram_tensor("idxd", [NE, CP], mybir.dt.int32,
                                         kind=_k).ap()
            ev.cw_dram = nc.dram_tensor("cwd", [T, NE], F32, kind=_k).ap()

    needed = sorted({nm for p in parts for (_, nm) in BLOCK_WEIGHTS[p]})
    wdict = {nm: nc.dram_tensor(nm, [cfg[s] for s in WSHAPES[nm]], F32,
                                kind="ExternalInput").ap() for nm in needed}

    first, last = parts[0], parts[-1]
    if first == "l0a":
        x0 = nc.dram_tensor("x0", [T, D], F32, kind="ExternalInput").ap()
        xa = nc.dram_tensor("xa", [D, T], F32, kind="Internal").ap()
    else:
        xa = nc.dram_tensor("xin", [D, T], F32, kind="ExternalInput").ap()
    xb = nc.dram_tensor("xb", [D, T], F32, kind="Internal").ap()
    if last == "moe":
        out = nc.dram_tensor("out", [T, D], F32, kind="ExternalOutput").ap()
    else:
        out = nc.dram_tensor("xout", [D, T], F32, kind="ExternalOutput").ap()
    ev.w1c_dram = ev.w2c_dram = None
    if "l0f" in parts:
        FF = cfg["FF"]
        ev.w1c_dram = nc.dram_tensor(
            "w1c", [cfg["FF"] // 128, 128, (D // 128) * 128], BF16,
            kind="Internal").ap()
        ev.w2c_dram = nc.dram_tensor("w2c", [FF, D], BF16, kind="Internal").ap()

    if first == "l0a":
        pre_transpose(b, ev, x0, xa)
        b.barrier()

    ev.l0f_W = None
    fns = {"l0a": block_l0attn, "l0f": block_l0ffn, "l1a": block_l1attn,
           "moe": block_moe_sparse if SPARSE_MOE else block_moe}
    cur = xa
    final_toks = []
    for i, p in enumerate(parts):
        Wp = {k: wdict[nm] for (k, nm) in BLOCK_WEIGHTS[p]}
        is_last = i == len(parts) - 1
        dst = out if is_last else (xb if cur is xa else xa)
        tok = fns[p](b, ev, cur, dst, Wp)
        final_toks = tok if isinstance(tok, list) else [tok]
        cur = dst
    b.finish(final_toks)
    return nc, stack


_PART_SETS = {
    "fused": [["l0a", "l0f", "l1a", "moe"]],
    "blocks": [["l0a"], ["l0f"], ["l1a"], ["moe"]],
}


def kernel(**inputs):
    cfg = cfg_full()
    part_sets = _PART_SETS[MODE]
    x_full = np.ascontiguousarray(np.asarray(inputs["L0_attn_x"], np.float32))
    B = x_full.shape[0]
    assert B == N_CORES
    cur_per_core = [np.ascontiguousarray(x_full[i]) for i in range(B)]
    for parts in part_sets:
        nc, stack = build_program(cfg, parts)
        needed = sorted({nm for p in parts for (_, nm) in BLOCK_WEIGHTS[p]})
        wvals = {nm: np.ascontiguousarray(np.asarray(inputs[nm], np.float32))
                 for nm in needed}
        xkey = "x0" if parts[0] == "l0a" else "xin"
        in_maps = [dict(wvals, **{xkey: cur_per_core[c]}) for c in range(B)]
        res = run_bass_kernel_spmd(nc, in_maps, core_ids=list(range(B)))
        stack.close()
        okey = "out" if parts[-1] == "moe" else "xout"
        cur_per_core = [np.ascontiguousarray(np.asarray(res.results[c][okey],
                                                        np.float32))
                        for c in range(B)]
    return np.stack(cur_per_core, axis=0).astype(np.float32)
